# version 05
# baseline (speedup 1.0000x reference)
import contextlib
import numpy as np
import ml_dtypes
import concourse.bass as bass
import concourse.mybir as mybir
from concourse.bass_utils import run_bass_kernel_spmd

F32 = mybir.dt.float32
BF16 = mybir.dt.bfloat16
AF = mybir.ActivationFunctionType
ALU = mybir.AluOpType
AX = mybir.AxisListType.X
BF = ml_dtypes.bfloat16
ET = mybir.EngineType

NCORE = 8
T_FULL = 16384
S = 2048
D = 1024
DI = 2048
NST = 16
DTR = 64
TC = 512
NCH = S // TC
NM = DI // 128
EPS = 1e-5
DEBUG = False
DEBUG2 = False


class P:

    ENGS = ("t", "v", "s", "g", "y")

    def __init__(self, nc):
        self.nc = nc
        self.th = {e: [] for e in P.ENGS}
        self.sem = {}
        self.ct = {}
        self.guards = []
        for n in ("dg", "ds", "pe", "v", "s", "c"):
            g = nc.semaphore(n)
            self.sem[n] = g.__enter__()
            self.guards.append(g)
            self.ct[n] = 0

    def on(self, e, fn, mark=None, k=1):
        if mark is not None:
            self.ct[mark] += k
        sem = self.sem.get(mark) if mark is not None else None

        def thunk(eng, fn=fn, sem=sem, k=k):
            ins = fn(eng)
            if sem is not None:
                ins.then_inc(sem, k)
        self.th[e].append(thunk)
        return self.ct[mark] if mark is not None else None

    def dma(self, e, fn):
        return self.on(e, fn, mark="ds" if e == "y" else "dg", k=16)

    def wt(self, e, name, val=None):
        v = self.ct[name] if val is None else val
        sem = self.sem[name]
        self.th[e].append(lambda eng, sem=sem, v=v: eng.wait_ge(sem, v))

    def replay(self, block):
        for e, deco in (("y", block.sync), ("g", block.gpsimd),
                        ("t", block.tensor), ("v", block.vector),
                        ("s", block.scalar)):
            def runner(eng, e=e):
                for thunk in self.th[e]:
                    thunk(eng)
            deco(runner)


def ln_stats(p, xt_ap, scratch, cols, width, eps=EPS):
    p.wt("s", "v")
    p.on("v", lambda v: v.reduce_sum(out=cols["sum"], in_=xt_ap, axis=AX),
         mark="v")
    p.on("s", lambda s: s.activation(out=scratch, in_=xt_ap, func=AF.Square,
                                     accum_out=cols["ssq"]), mark="s")
    p.wt("v", "s")
    p.wt("v", "v")
    p.on("v", lambda v: v.tensor_scalar_mul(out=cols["mean"], in0=cols["sum"],
                                            scalar1=1.0 / width))
    p.on("v", lambda v: v.tensor_scalar_mul(out=cols["nmean"],
                                            in0=cols["sum"],
                                            scalar1=-1.0 / width))
    p.on("v", lambda v: v.tensor_scalar(out=cols["ex2"], in0=cols["ssq"],
                                        scalar1=1.0 / width, scalar2=eps,
                                        op0=ALU.mult, op1=ALU.add), mark="v")
    p.wt("v", "v")
    p.on("v", lambda v: v.scalar_tensor_tensor(
        out=cols["nvar"], in0=cols["mean"], scalar=cols["mean"],
        in1=cols["ex2"], op0=ALU.mult, op1=ALU.subtract), mark="v")
    p.wt("s", "v")
    p.on("s", lambda s: s.activation(out=cols["lnv"], in_=cols["nvar"],
                                     func=AF.Ln, scale=-1.0))
    p.on("s", lambda s: s.activation(out=cols["rstd"], in_=cols["lnv"],
                                     func=AF.Exp, scale=-0.5), mark="s")
    p.wt("v", "s")
    p.on("v", lambda v: v.tensor_scalar(
        out=cols["nmr"], in0=cols["nmean"], scalar1=cols["rstd"],
        scalar2=None, op0=ALU.mult), mark="v")
    p.wt("v", "v")



def _gp_prev(nc, ctx):
    rc = ctx.get("regcache")
    if rc is None:
        rc = {}
        ctx["regcache"] = rc
    if "prev" not in rc:
        cid = nc.partition_id([ET.Pool])
        rc["cid"] = cid
        rc["prev"] = (cid + 7) % 8
    return rc["prev"]


def _gp_cid(nc, ctx):
    _gp_prev(nc, ctx)
    return ctx["regcache"]["cid"]


def build_stage1(ctx=None):
    tag = "" if ctx is None else ctx["tag"]
    if ctx is None:
        nc = bass.Bass("TRN2", target_bir_lowering=False, debug=False,
                       num_devices=NCORE)
        din = lambda nm, sh, dt=F32: nc.dram_tensor(nm, sh, dt,
                                                    kind="ExternalInput").ap()
    else:
        nc = ctx["nc"]
        din = lambda nm, sh, dt=F32: ctx["io"][nm]
    xs = din("xs", [S, D])
    xh = din("xh", [128, D])
    hsc = din("hsc", [128, 1])
    lnw = din("lnw", [1, D]); lnb = din("lnb", [1, D])
    Win = din("Win", [128, 8 * 2 * DI])
    convw = din("convw", [128, NM * 4]); convb = din("convb", [128, NM])
    Wxp = din("Wxp", [128, NM * 96])
    Wdt = din("Wdt", [DTR, DI]); bdt = din("bdt", [128, NM])
    Alog = din("Alog", [128, NM * NST]); Dvec = din("Dvec", [128, NM])
    Wout = din("Wout", [128, NM * D])
    if ctx is None:
        ident_in = nc.dram_tensor("ident", [128, 128], BF16,
                                  kind="ExternalInput").ap()
        ones_in = din("ones1", [1, 128])
        x1 = nc.dram_tensor("x1", [S, D], F32, kind="ExternalOutput").ap()
        x1b = nc.dram_tensor("x1b", [S, D], F32,
                             kind="ExternalOutput").ap()
        xr_d = nc.dram_tensor("xr_d", [S, D], F32,
                              kind="ExternalOutput").ap()
        yf_d = nc.dram_tensor("yf_d", [NCH * 128, NM * TC], BF16,
                              kind="ExternalOutput").ap()
    else:
        ident_in = ctx["io"]["ident"]
        ones_in = ctx["io"]["ones1"]
        x1 = ctx["x_out"]
        x1b = nc.dram_tensor(tag + "x1b", [S, D], F32)
        xr_d = nc.dram_tensor(tag + "xr_d", [S, D], F32)
        yf_d = nc.dram_tensor(tag + "yf_d", [NCH * 128, NM * TC], BF16)
    dbg = {}
    if DEBUG:
        for nm, sh, dt in [("dbg_normT", [128, 8 * TC], BF16),
                           ("dbg_u", [128, NM * TC], BF16),
                           ("dbg_dT", [128, NM * TC], BF16),
                           ("dbg_dt", [DTR, TC], BF16),
                           ("dbg_BC", [32, TC], F32),
                           ("dbg_hfin", [128, 256], F32),
                           ("dbg_decay", [128, 256], F32),
                           ("dbg_hstate", [128, 256], F32),
                           ("dbg_ball", [128, NST * TC], BF16),
                           ("dbg_Bbc", [128, NST * TC], BF16),
                           ("dbg_aall", [128, NST * TC], BF16),
                           ("dbg_hall", [128, NST * TC], BF16),
                           ("dbg_yfin", [128, NM * TC], BF16),
                           ("dbg_hs4", [NCH, 128, NM * NST], F32),
                           ("dbg_yf4", [NCH, 128, NM * TC], BF16),
                           ("dbg_xr", [128, 4 * D], F32)]:
            dbg[nm] = nc.dram_tensor(nm, sh, dt, kind="ExternalOutput").ap()

    uT_sp = nc.dram_tensor(tag + "uT_sp", [128, NM, S], BF16)
    szT_sp = nc.dram_tensor(tag + "szT_sp", [128, NM, S], BF16)
    dT_sp = nc.dram_tensor(tag + "dT_sp", [128, NM, S], BF16)
    b_sp = nc.dram_tensor(tag + "b_sp", [NCH, NM, 128, NST * TC], BF16)
    BC_d = nc.dram_tensor(tag + "BC_d", [32, S], F32)
    cc_in = nc.dram_tensor(tag + "cc_in", [256, NM * NST], F32)
    cc_out = nc.dram_tensor(tag + "cc_out", [256 * NCORE, NM * NST], F32,
                            addr_space="Shared")
    g_bounce = nc.dram_tensor(tag + "g_bounce", [128, NCORE * 256], F32)

    _stage_guards = []

    def sb(nm, sh, dt=F32):
        g = nc.sbuf_tensor(tag + nm, sh, dt)
        h = g.__enter__()
        _stage_guards.append(g)
        return h
    ident = sb("identb", [128, 128], BF16)
    ones1 = sb("ones1b", [1, 128])
    lnw_b = sb("lnw_b", [128, D]); lnb_b = sb("lnb_b", [128, D])
    cw_sb = sb("cw_sb", [128, NM * 4]); cb_sb = sb("cb_sb", [128, NM])
    bdt_sb = sb("bdt_sb", [128, NM]); D_sb = sb("D_sb", [128, NM])
    hsc_sb = sb("hsc_sb", [128, 1])
    negA = sb("negA", [128, NM * NST]); A_sb = sb("A_sb", [128, NM * NST])
    AdT = sb("AdT", [128, NM * NST])
    dcum_start = sb("dcum_start", [128, NCH * NM])
    dc_carry = sb("dc_carry", [128, NM])
    hfin = sb("hfin", [128, NM * NST]); decay = sb("decay", [128, NM * NST])
    hacc = sb("hacc", [128, NM * NST]); h_state = sb("h_state", [128, NM * NST])
    lnsc = sb("lnsc", [128, D]); lncol = sb("lncol", [128, 12])
    padv = sb("padv", [128, 384]); pads = sb("pads", [128, 96])
    ones_tc = sb("ones_tc", [128, TC])
    dcum_f = sb("dcum_f", [128, 2 * TC])
    bcrow = sb("bcrow", [1, 2 * TC])

    if ctx is None:
        ps0 = nc.alloc_psum_tensor("ps0", [128, 512], F32)
        ps1 = nc.alloc_psum_tensor("ps1", [128, 512], F32)
        psT = nc.alloc_psum_tensor("psT", [128, 1024], BF16)
        ps2 = nc.alloc_psum_tensor("ps2", [128, 512], F32)
        ps3 = nc.alloc_psum_tensor("ps3", [128, 512], F32)
    else:
        ps0, ps1, ps2, ps3, psT = (ctx["psA"], ctx["psB"], ctx["psC"],
                                   ctx["psO"], ctx["psT"])
    psmm = [ps0, ps1]

    with (nc.Block() if ctx is None else contextlib.nullcontext()) as block:
        p = P(nc) if ctx is None else ctx["p"]

        def vpad():
            p.on("v", lambda v: v.memset(padv[:], 0.0), mark="v")

        def spad():
            p.on("s", lambda s_: s_.activation(out=pads[:],
                                               in_=padv[:, 0:pads.shape[1]],
                                               func=AF.Square), mark="s")
        cols = dict(sum=lncol[:, 0:1], ssq=lncol[:, 1:2], mean=lncol[:, 2:3],
                    nmean=lncol[:, 3:4], ex2=lncol[:, 4:5], nvar=lncol[:, 5:6],
                    lnv=lncol[:, 6:7], rstd=lncol[:, 7:8], nmr=lncol[:, 8:9])
        last_rd = ({id(b): ("v", 0) for b in (ps0, ps1, psT, ps2, ps3)}
                   if ctx is None else ctx["last_rd"])

        def psum_guard(bank):
            p.wt("t", last_rd[id(bank)][0], last_rd[id(bank)][1])

        def set_rd(bank, eng):
            last_rd[id(bank)] = (eng, p.ct[eng])

        def ln_apply_tile(xt_ap, out_bf):
            p.wt("v", "pe")
            p.on("v", lambda v: v.tensor_scalar(
                out=tnorm[:], in0=xt_ap, scalar1=cols["rstd"],
                scalar2=cols["nmr"], op0=ALU.mult, op1=ALU.add), mark="v")
            p.on("v", lambda v: v.tensor_tensor(out=tnorm2[:], in0=tnorm[:],
                                                in1=lnw_b[:], op=ALU.mult))
            p.on("v", lambda v, o=out_bf: v.tensor_tensor(
                out=o, in0=tnorm2[:], in1=lnb_b[:], op=ALU.add), mark="v")

        def transposes(src_bf, nj=8):
            p.wt("t", "v")
            psum_guard(psT)
            for j in range(nj):
                p.on("t", lambda t, j=j: t.transpose(
                    out=psT[:, j * 128:(j + 1) * 128],
                    in_=src_bf[:, j * 128:(j + 1) * 128], identity=ident[:]),
                    mark="pe" if j == nj - 1 else None)
            p.wt("s", "pe")

        with contextlib.ExitStack() as a1st:
            a1sb = lambda nm, sh, dt=F32: a1st.enter_context(
                nc.sbuf_tensor(tag + nm, sh, dt))
            Win_bf = a1sb("Win_bf", [128, 8 * 2 * DI], BF16)
            normT = a1sb("normT", [128, 8 * TC], BF16)
            arena = a1sb("arena", [128, NM * 515], BF16)
            uT_bf = a1sb("uT_bf", [128, NM * TC], BF16)
            dT_bf = a1sb("dT_bf", [128, NM * TC], BF16)
            Wxp_bf = a1sb("Wxp_bf", [128, NM * 96], BF16)
            Wdt_bf = a1sb("Wdt_bf", [DTR, DI], BF16)
            dt_bf = a1sb("dt_bf", [DTR, S], BF16)
            BC_f = a1st.enter_context(nc.sbuf_tensor(tag + "BC_f", [32, S], F32))
            xt = a1st.enter_context(nc.sbuf_tensor(tag + "xt", [128, 2 * D], F32))
            tnorm = a1st.enter_context(nc.sbuf_tensor(tag + "tnorm", [128, D], F32))
            tnorm2 = a1st.enter_context(nc.sbuf_tensor(tag + "tnorm2", [128, D], F32))
            norm_bf = a1sb("norm_bf", [128, D], BF16)
            hnT = a1sb("hnT", [128, 8 * 3], BF16)
            halo_sav = a1st.enter_context(
                nc.sbuf_tensor(tag + "halo_sav", [128, NM * 3], F32))
            cbuf0 = a1st.enter_context(nc.sbuf_tensor(tag + "cbuf0", [128, TC], F32))
            cbuf1 = a1st.enter_context(nc.sbuf_tensor(tag + "cbuf1", [128, TC], F32))
            spl0 = a1st.enter_context(nc.sbuf_tensor(tag + "spl0", [128, TC], F32))
            spl1 = a1st.enter_context(nc.sbuf_tensor(tag + "spl1", [128, TC], F32))
            spl2 = a1st.enter_context(nc.sbuf_tensor(tag + "spl2", [128, TC], F32))
            sz_small = a1sb("sz_small", [128, 2 * TC], BF16)

            p.dma("g", lambda g: g.dma_start(out=Win_bf[:], in_=Win[:, :]))
            p.dma("g", lambda g: g.dma_start(out=Wxp_bf[:], in_=Wxp[:, :]))
            p.dma("g", lambda g: g.dma_start(out=Wdt_bf[:], in_=Wdt[:, :]))
            p.dma("g", lambda g: g.dma_start(out=ident[:], in_=ident_in[:, :]))
            for dst, src in [(cw_sb, convw), (cb_sb, convb), (bdt_sb, bdt),
                             (D_sb, Dvec), (hsc_sb, hsc), (ones1, ones_in),
                             (AdT, Alog)]:
                p.dma("y", lambda y, d=dst, s_=src: y.dma_start(out=d[:],
                                                                in_=s_[:, :]))
            p.dma("y", lambda y: y.dma_start(out=lnw_b[0:1, :], in_=lnw[:, :]))
            p.dma("y", lambda y: y.dma_start(out=lnb_b[0:1, :], in_=lnb[:, :]))
            p.wt("v", "ds")
            p.on("v", lambda v: v.memset(ones_tc[:], 1.0))
            p.on("v", lambda v: v.memset(h_state[:], 0.0))
            p.on("v", lambda v: v.memset(hfin[:], 0.0))
            p.on("v", lambda v: v.memset(dc_carry[:], 0.0), mark="v")
            p.wt("s", "ds")
            p.on("s", lambda s: s.activation(out=negA[:], in_=AdT[:],
                                             func=AF.Exp), mark="s")
            p.wt("v", "s")
            p.on("v", lambda v: v.tensor_scalar_mul(out=A_sb[:], in0=negA[:],
                                                    scalar1=-1.0), mark="v")
            p.wt("t", "ds")
            for dstb in (lnw_b, lnb_b):
                for half in range(2):
                    sl = slice(half * 512, (half + 1) * 512)
                    psum_guard(ps2)
                    p.on("t", lambda t, d=dstb, sl=sl: t.matmul(
                        out=ps2[:], lhsT=ones1[:, :], rhs=d[0:1, sl],
                        start=True, stop=True), mark="pe")
                    p.wt("v", "pe")
                    p.on("v", lambda v, d=dstb, sl=sl: v.tensor_copy(
                        out=d[:, sl], in_=ps2[:]), mark="v")
                    set_rd(ps2, "v")

            p.dma("y", lambda y: y.dma_start(out=xt[:, 0:D], in_=xh[:, :]))
            p.wt("v", "ds"); p.wt("s", "ds")
            ln_stats(p, xt[:, 0:D], lnsc[:], cols, D)
            ln_apply_tile(xt[:, 0:D], norm_bf[:])
            transposes(norm_bf)
            for j in range(8):
                p.on("s", lambda s, j=j: s.copy(
                    out=hnT[:, j * 3:(j + 1) * 3],
                    in_=psT[:, j * 128 + 125:j * 128 + 128]), mark="s")
            set_rd(psT, "s")
            p.wt("t", "dg"); p.wt("t", "s")
            for m in range(NM):
                psum_guard(ps2)
                for k in range(8):
                    p.on("t", lambda t, m=m, k=k: t.matmul(
                        out=ps2[:, 0:3],
                        lhsT=Win_bf[:, k * 2 * DI + m * 128:
                                    k * 2 * DI + (m + 1) * 128],
                        rhs=hnT[:, k * 3:(k + 1) * 3],
                        start=(k == 0), stop=(k == 7)),
                        mark="pe" if k == 7 else None)
                p.wt("v", "pe")
                p.on("v", lambda v, m=m: v.tensor_scalar(
                    out=halo_sav[:, m * 3:(m + 1) * 3], in0=ps2[:, 0:3],
                    scalar1=hsc_sb[:, 0:1], scalar2=None, op0=ALU.mult),
                    mark="v")
                set_rd(ps2, "v")

            for ck in range(NCH):
                p.wt("v", "ds"); p.wt("s", "ds")
                p.wt("v", "pe"); p.wt("s", "pe")
                for m in range(NM):
                    p.on("v", lambda v, m=m: v.tensor_copy(
                        out=arena[:, m * 515:m * 515 + 3],
                        in_=halo_sav[:, m * 3:(m + 1) * 3]),
                        mark="v" if m == NM - 1 else None)
                for kt in range(4):
                    tile = 4 * ck + kt
                    sl_x = slice((kt % 2) * D, (kt % 2) * D + D)
                    p.wt("y", "v"); p.wt("y", "s")
                    p.dma("y", lambda y, t_=tile, sl=sl_x: y.dma_start(
                        out=xt[:, sl], in_=xs[t_ * 128:(t_ + 1) * 128, :]))
                    p.wt("v", "ds"); p.wt("s", "ds")
                    ln_stats(p, xt[:, sl_x], lnsc[:], cols, D)
                    ln_apply_tile(xt[:, sl_x], norm_bf[:])
                    transposes(norm_bf)
                    for j in range(8):
                        p.on("s", lambda s, j=j, kt=kt: s.copy(
                            out=normT[:, j * TC + kt * 128:
                                      j * TC + kt * 128 + 128],
                            in_=psT[:, j * 128:(j + 1) * 128]), mark="s")
                    set_rd(psT, "s")
                for m in range(32):
                    bank = psmm[m % 2]
                    psum_guard(bank)
                    p.wt("t", "s")
                    for k in range(8):
                        p.on("t", lambda t, m=m, k=k, b=bank: t.matmul(
                            out=b[:],
                            lhsT=Win_bf[:, k * 2 * DI + m * 128:
                                        k * 2 * DI + (m + 1) * 128],
                            rhs=normT[:, k * TC:(k + 1) * TC],
                            start=(k == 0), stop=(k == 7)),
                            mark="pe" if k == 7 else None)
                    if m < NM:
                        p.wt("v", "pe")
                        p.on("v", lambda v, m=m, b=bank: v.tensor_copy(
                            out=arena[:, m * 515 + 3:m * 515 + 515],
                            in_=b[:]), mark="v")
                        set_rd(bank, "v")
                        p.on("v", lambda v, m=m: v.tensor_scalar(
                            out=cbuf0[:], in0=arena[:, m * 515:m * 515 + TC],
                            scalar1=cw_sb[:, m * 4:m * 4 + 1], scalar2=None,
                            op0=ALU.mult))
                        for j in range(1, 4):
                            si = cbuf0 if j % 2 == 1 else cbuf1
                            so = cbuf1 if j % 2 == 1 else cbuf0
                            p.on("v", lambda v, m=m, j=j, si=si, so=so:
                                 v.scalar_tensor_tensor(
                                     out=so[:],
                                     in0=arena[:, m * 515 + j:
                                               m * 515 + j + TC],
                                     scalar=cw_sb[:, m * 4 + j:m * 4 + j + 1],
                                     in1=si[:], op0=ALU.mult, op1=ALU.add),
                                 mark="v" if j == 3 else None)
                        p.on("v", lambda v, m=m: v.tensor_copy(
                            out=halo_sav[:, m * 3:(m + 1) * 3],
                            in_=arena[:, m * 515 + 512:m * 515 + 515]),
                            mark="v")
                        p.wt("s", "v")
                        p.on("s", lambda s, m=m: s.activation(
                            out=uT_bf[:, m * TC:(m + 1) * TC], in_=cbuf1[:],
                            func=AF.Silu, bias=cb_sb[:, m:m + 1]), mark="s")
                        spad(); p.wt("y", "s")
                        p.dma("y", lambda y, m=m, ck=ck: y.dma_start(
                            out=uT_sp[:, m, ck * TC:(ck + 1) * TC],
                            in_=uT_bf[:, m * TC:(m + 1) * TC]))
                    else:
                        mz = m - NM
                        slz = slice((mz % 2) * TC, (mz % 2) * TC + TC)
                        p.wt("s", "pe")
                        p.wt("s", "ds")
                        p.on("s", lambda s, b=bank, sl=slz: s.activation(
                            out=sz_small[:, sl], in_=b[:], func=AF.Silu),
                            mark="s")
                        set_rd(bank, "s")
                        spad(); p.wt("y", "s")
                        p.dma("y", lambda y, mz=mz, ck=ck, sl=slz:
                              y.dma_start(
                                  out=szT_sp[:, mz, ck * TC:(ck + 1) * TC],
                                  in_=sz_small[:, sl]))
                psum_guard(ps2)
                p.wt("t", "s")
                for m in range(NM):
                    p.on("t", lambda t, m=m: t.matmul(
                        out=ps2[0:96, :],
                        lhsT=Wxp_bf[:, m * 96:(m + 1) * 96],
                        rhs=uT_bf[:, m * TC:(m + 1) * TC],
                        start=(m == 0), stop=(m == NM - 1)),
                        mark="pe" if m == NM - 1 else None)
                p.wt("v", "pe")
                p.on("v", lambda v, ck=ck: v.tensor_copy(
                    out=dt_bf[:, ck * TC:(ck + 1) * TC], in_=ps2[0:64, :]))
                p.on("v", lambda v, ck=ck: v.tensor_copy(
                    out=BC_f[:, ck * TC:(ck + 1) * TC], in_=ps2[64:96, :]),
                    mark="v")
                set_rd(ps2, "v")
                vpad(); p.wt("y", "v")
                p.dma("y", lambda y, ck=ck: y.dma_start(
                    out=BC_d[:, ck * TC:(ck + 1) * TC],
                    in_=BC_f[:, ck * TC:(ck + 1) * TC]))
                for m in range(NM):
                    psum_guard(ps3)
                    p.wt("t", "v")
                    p.on("t", lambda t, m=m, ck=ck: t.matmul(
                        out=ps3[:], lhsT=Wdt_bf[:, m * 128:(m + 1) * 128],
                        rhs=dt_bf[:, ck * TC:(ck + 1) * TC],
                        start=True, stop=True), mark="pe")
                    p.wt("v", "pe"); p.wt("v", "s")
                    p.on("v", lambda v, m=m: v.tensor_scalar(
                        out=spl0[:], in0=ps3[:], scalar1=bdt_sb[:, m:m + 1],
                        scalar2=None, op0=ALU.add), mark="v")
                    set_rd(ps3, "v")
                    p.wt("s", "v")
                    p.on("s", lambda s: s.activation(out=spl1[:], in_=spl0[:],
                                                     func=AF.Abs))
                    p.on("s", lambda s: s.activation(out=spl2[:], in_=spl1[:],
                                                     func=AF.Exp, scale=-1.0))
                    p.on("s", lambda s: s.activation(out=spl1[:], in_=spl2[:],
                                                     func=AF.Ln, bias=1.0),
                         mark="s")
                    p.wt("v", "s")
                    p.on("v", lambda v: v.tensor_scalar_max(
                        out=cbuf0[:], in0=spl0[:], scalar1=0.0))
                    p.on("v", lambda v, m=m: v.tensor_tensor(
                        out=dT_bf[:, m * TC:(m + 1) * TC], in0=cbuf0[:],
                        in1=spl1[:], op=ALU.add), mark="v")
                    p.on("v", lambda v, m=m, ck=ck: v.tensor_copy(
                        out=dcum_start[:, ck * NM + m:ck * NM + m + 1],
                        in_=dc_carry[:, m:m + 1]))
                    p.on("v", lambda v, m=m: v.reduce_sum(
                        out=dcum_f[:, 0:1], in_=dT_bf[:, m * TC:(m + 1) * TC],
                        axis=AX), mark="v")
                    p.wt("v", "v")
                    p.on("v", lambda v, m=m: v.tensor_tensor(
                        out=dcum_f[:, 1:2], in0=dc_carry[:, m:m + 1],
                        in1=dcum_f[:, 0:1], op=ALU.add), mark="v")
                    p.wt("v", "v")
                    p.on("v", lambda v, m=m: v.tensor_copy(
                        out=dc_carry[:, m:m + 1], in_=dcum_f[:, 1:2]),
                        mark="v")
                    p.wt("v", "v")
                    vpad(); p.wt("y", "v")
                    p.dma("y", lambda y, m=m, ck=ck: y.dma_start(
                        out=dT_sp[:, m, ck * TC:(ck + 1) * TC],
                        in_=dT_bf[:, m * TC:(m + 1) * TC]))
                if DEBUG and ck == 0:
                    p.wt("y", "v"); p.wt("y", "s")
                    p.dma("y", lambda y: y.dma_start(out=dbg["dbg_normT"][:, :],
                                                     in_=normT[:]))
                    p.dma("y", lambda y: y.dma_start(out=dbg["dbg_u"][:, :],
                                                     in_=uT_bf[:]))
                    p.dma("y", lambda y: y.dma_start(out=dbg["dbg_dT"][:, :],
                                                     in_=dT_bf[:]))
                    p.dma("y", lambda y: y.dma_start(out=dbg["dbg_dt"][:, :],
                                                     in_=dt_bf[:, 0:TC]))
                    p.dma("y", lambda y: y.dma_start(out=dbg["dbg_BC"][:, :],
                                                     in_=BC_f[:, 0:TC]))

        p.wt("v", "v")
        for m in range(NM):
            p.on("v", lambda v, m=m: v.tensor_scalar(
                out=AdT[:, m * NST:(m + 1) * NST],
                in0=A_sb[:, m * NST:(m + 1) * NST],
                scalar1=dc_carry[:, m:m + 1], scalar2=None, op0=ALU.mult),
                mark="v")
        p.wt("s", "v")
        for m in range(NM):
            p.on("s", lambda s, m=m: s.activation(
                out=decay[:, m * NST:(m + 1) * NST],
                in_=A_sb[:, m * NST:(m + 1) * NST], func=AF.Exp,
                scale=dc_carry[:, m:m + 1]), mark="s")

        with contextlib.ExitStack() as a2st:
            a2sb = lambda nm, sh, dt=BF16: a2st.enter_context(
                nc.sbuf_tensor(tag + nm, sh, dt))
            Bbc = a2sb("Bbc", [128, NST * TC])
            b_all = a2sb("b_all", [128, NST * TC])
            du_bf = a2sb("du_bf", [128, 2 * TC])
            w_bf = a2sb("w_bf", [128, 2 * TC])
            sttout = a2sb("sttout", [128, 2 * TC])
            uT_rd = a2sb("uT_rd", [128, 2 * TC])
            dT_rd = a2sb("dT_rd", [128, 2 * TC])
            for ck in range(NCH):
                for n in range(NST):
                    slb = slice((n % 2) * TC, (n % 2) * TC + TC)
                    p.wt("y", "pe")
                    p.dma("y", lambda y, n=n, ck=ck, sl=slb: y.dma_start(
                        out=bcrow[0:1, sl],
                        in_=BC_d[n:n + 1, ck * TC:(ck + 1) * TC]))
                    bank = ps2 if n % 2 == 0 else ps3
                    psum_guard(bank)
                    p.wt("t", "ds")
                    p.on("t", lambda t, b=bank, sl=slb: t.matmul(
                        out=b[:], lhsT=ones1[:, :], rhs=bcrow[0:1, sl],
                        start=True, stop=True), mark="pe")
                    p.wt("v", "pe")
                    p.on("v", lambda v, n=n, b=bank: v.tensor_copy(
                        out=Bbc[:, n * TC:(n + 1) * TC], in_=b[:]), mark="v")
                    set_rd(bank, "v")
                for m in range(NM):
                    slm = slice((m % 2) * TC, (m % 2) * TC + TC)
                    p.wt("y", "v")
                    p.dma("y", lambda y, m=m, ck=ck, sl=slm: y.dma_start(
                        out=uT_rd[:, sl],
                        in_=uT_sp[:, m, ck * TC:(ck + 1) * TC]))
                    p.dma("y", lambda y, m=m, ck=ck, sl=slm: y.dma_start(
                        out=dT_rd[:, sl],
                        in_=dT_sp[:, m, ck * TC:(ck + 1) * TC]))
                    p.wt("v", "ds"); p.wt("v", "s")
                    p.on("v", lambda v, m=m, ck=ck, sl=slm:
                         v.tensor_tensor_scan(
                             out=dcum_f[:, sl], data0=ones_tc[:],
                             data1=dT_rd[:, sl],
                             initial=dcum_start[:, ck * NM + m:
                                                ck * NM + m + 1],
                             op0=ALU.mult, op1=ALU.add))
                    p.on("v", lambda v, sl=slm: v.tensor_tensor(
                        out=du_bf[:, sl], in0=dT_rd[:, sl],
                        in1=uT_rd[:, sl], op=ALU.mult), mark="v")
                    for n in range(NST):
                        sln = slice((n % 2) * TC, (n % 2) * TC + TC)
                        mn = m * NST + n
                        p.wt("s", "v")
                        p.on("s", lambda s, mn=mn, sl=slm, sn=sln:
                             s.activation(
                                 out=w_bf[:, sn], in_=dcum_f[:, sl],
                                 func=AF.Exp, scale=negA[:, mn:mn + 1],
                                 bias=AdT[:, mn:mn + 1]), mark="s")
                        p.wt("v", "s")
                        p.on("v", lambda v, n=n, sl=slm: v.tensor_tensor(
                            out=b_all[:, n * TC:(n + 1) * TC],
                            in0=du_bf[:, sl],
                            in1=Bbc[:, n * TC:(n + 1) * TC], op=ALU.mult))
                        p.on("v", lambda v, mn=mn, n=n, sn=sln:
                             v.scalar_tensor_tensor(
                                 out=sttout[:, sn], in0=w_bf[:, sn],
                                 scalar=1.0,
                                 in1=b_all[:, n * TC:(n + 1) * TC],
                                 op0=ALU.mult, op1=ALU.mult,
                                 accum_out=hacc[:, mn:mn + 1]), mark="v")
                    vpad(); p.wt("y", "v")
                    p.dma("y", lambda y, m=m, ck=ck: y.dma_start(
                        out=b_sp[ck, m, :, :], in_=b_all[:]))
                    if DEBUG and ck == 0 and m == 0:
                        p.dma("y", lambda y: y.dma_start(
                            out=dbg["dbg_ball"][:, :], in_=b_all[:]))
                        p.dma("y", lambda y: y.dma_start(
                            out=dbg["dbg_Bbc"][:, :], in_=Bbc[:]))
                p.wt("v", "v")
                p.on("v", lambda v: v.tensor_tensor(
                    out=hfin[:], in0=hfin[:], in1=hacc[:], op=ALU.add),
                    mark="v")

        p.on("v", lambda v: v.tensor_copy(out=hacc[:], in_=hfin[:]), mark="v")
        p.wt("v", "v")
        p.on("v", lambda v: v.tensor_copy(out=AdT[:], in_=hacc[:]), mark="v")
        p.wt("y", "v"); p.wt("y", "s")
        p.dma("y", lambda y: y.dma_start(out=cc_in[0:128, :], in_=decay[:]))
        p.dma("y", lambda y: y.dma_start(out=cc_in[128:256, :], in_=hfin[:]))
        p.wt("g", "ds")
        p.on("g", lambda g: g.collective_compute(
            "AllGather", ALU.bypass, replica_groups=[list(range(NCORE))],
            ins=[cc_in[:, :]], outs=[cc_out[:, :]]), mark="c")
        p.wt("g", "c")
        with contextlib.ExitStack() as cbst:
            dechf = cbst.enter_context(
                nc.sbuf_tensor(tag + "dechf", [128, NCORE * 512], F32))
            gall = cbst.enter_context(
                nc.sbuf_tensor(tag + "gall", [128, NCORE * 256], F32))
            gtmp = cbst.enter_context(
                nc.sbuf_tensor(tag + "gtmp", [128, 256], F32))
            for r in range(NCORE):
                p.dma("g", lambda g, r=r: g.dma_start(
                    out=dechf[:, r * 512:r * 512 + 256],
                    in_=cc_out[256 * r:256 * r + 128, :]))
                p.dma("g", lambda g, r=r: g.dma_start(
                    out=dechf[:, r * 512 + 256:r * 512 + 512],
                    in_=cc_out[256 * r + 128:256 * r + 256, :]))
            p.wt("v", "dg")
            p.on("v", lambda v: v.memset(gall[:, 0:256], 0.0), mark="v")
            for r in range(1, NCORE):
                p.on("v", lambda v, r=r: v.tensor_tensor(
                    out=gtmp[:],
                    in0=dechf[:, (r - 1) * 512:(r - 1) * 512 + 256],
                    in1=gall[:, (r - 1) * 256:r * 256], op=ALU.mult),
                    mark="v")
                p.wt("v", "v")
                p.on("v", lambda v, r=r: v.tensor_tensor(
                    out=gall[:, r * 256:(r + 1) * 256], in0=gtmp[:],
                    in1=dechf[:, (r - 1) * 512 + 256:(r - 1) * 512 + 512],
                    op=ALU.add), mark="v")
                p.wt("v", "v")
            vpad(); p.wt("y", "v")
            p.dma("y", lambda y: y.dma_start(out=g_bounce[:, :], in_=gall[:]))
            p.wt("g", "ds")

            def pick_hstate(g):
                cid = (_gp_cid(nc, ctx) if ctx is not None
                       else nc.partition_id([ET.Pool]))
                return g.dma_start(out=h_state[:, :],
                                   in_=g_bounce[:, bass.ts(cid, 256)])
            p.dma("g", pick_hstate)

        if DEBUG:
            p.wt("y", "dg")
            p.dma("y", lambda y: y.dma_start(out=dbg["dbg_decay"][:, :],
                                             in_=decay[:]))
            p.dma("y", lambda y: y.dma_start(out=dbg["dbg_hfin"][:, :],
                                             in_=hfin[:]))
            p.dma("y", lambda y: y.dma_start(out=dbg["dbg_hstate"][:, :],
                                             in_=h_state[:]))
        with contextlib.ExitStack() as bst:
            bsb = lambda nm, sh, dt=BF16: bst.enter_context(
                nc.sbuf_tensor(tag + nm, sh, dt))
            BbcB = bsb("BbcB", [128, NST * TC])
            CbcB = bsb("CbcB", [128, NST * TC])
            b_allB = bsb("b_allB", [128, NST * TC])
            a_allB = bst.enter_context(
                nc.sbuf_tensor(tag + "a_allB", [128, NST * TC], F32))
            h_allB = bst.enter_context(
                nc.sbuf_tensor(tag + "h_allB", [128, NST * TC], F32))
            tmp_allB = b_allB
            uT_rdB = bsb("uT_rdB", [128, 2 * TC])
            dT_rdB = bsb("dT_rdB", [128, 2 * TC])
            sz_rdB = bsb("sz_rdB", [128, 2 * TC])
            y_ch = bst.enter_context(nc.sbuf_tensor(tag + "y_chB", [128, TC], F32))
            yfin = bsb("yfinB", [128, NM * TC])
            t1f = bst.enter_context(nc.sbuf_tensor(tag + "t1fB", [128, TC], F32))
            Wo_bf = bsb("Wo_bfB", [128, 2 * 512])
            xt4 = bst.enter_context(nc.sbuf_tensor(tag + "xt4B", [128, 4 * D], F32))
            xr4 = bst.enter_context(nc.sbuf_tensor(tag + "xr4B", [128, 4 * D], F32))
            p.wt("v", "dg")
            for ck in range(NCH):
                for grp, dst in ((0, BbcB), (1, CbcB)):
                    for n in range(NST):
                        slb = slice((n % 2) * TC, (n % 2) * TC + TC)
                        row = grp * NST + n
                        p.wt("y", "pe")
                        p.dma("y", lambda y, row=row, ck=ck, sl=slb:
                              y.dma_start(
                                  out=bcrow[0:1, sl],
                                  in_=BC_d[row:row + 1,
                                           ck * TC:(ck + 1) * TC]))
                        bank = ps2 if n % 2 == 0 else ps3
                        psum_guard(bank)
                        p.wt("t", "ds")
                        p.on("t", lambda t, b=bank, sl=slb: t.matmul(
                            out=b[:], lhsT=ones1[:, :], rhs=bcrow[0:1, sl],
                            start=True, stop=True), mark="pe")
                        p.wt("v", "pe")
                        p.on("v", lambda v, n=n, b=bank, d=dst:
                             v.tensor_copy(
                                 out=d[:, n * TC:(n + 1) * TC], in_=b[:]),
                             mark="v")
                        set_rd(bank, "v")
                p.wt("v", "pe")
                for m in range(NM):
                    slm = slice((m % 2) * TC, (m % 2) * TC + TC)
                    p.wt("y", "v"); p.wt("y", "s")
                    p.dma("y", lambda y, m=m, ck=ck, sl=slm: y.dma_start(
                        out=uT_rdB[:, sl],
                        in_=uT_sp[:, m, ck * TC:(ck + 1) * TC]))
                    p.dma("y", lambda y, m=m, ck=ck, sl=slm: y.dma_start(
                        out=dT_rdB[:, sl],
                        in_=dT_sp[:, m, ck * TC:(ck + 1) * TC]))
                    p.dma("y", lambda y, m=m, ck=ck, sl=slm: y.dma_start(
                        out=sz_rdB[:, sl],
                        in_=szT_sp[:, m, ck * TC:(ck + 1) * TC]))
                    p.dma("y", lambda y, m=m, ck=ck: y.dma_start(
                        out=b_allB[:], in_=b_sp[ck, m, :, :]))
                    p.wt("s", "ds"); p.wt("s", "v")
                    for n in range(NST):
                        mn = m * NST + n
                        p.on("s", lambda s, n=n, mn=mn, sl=slm: s.activation(
                            out=a_allB[:, n * TC:(n + 1) * TC],
                            in_=dT_rdB[:, sl], func=AF.Exp,
                            scale=A_sb[:, mn:mn + 1]),
                            mark="s" if n == NST - 1 else None)
                    p.wt("v", "s"); p.wt("v", "ds")
                    for n in range(NST):
                        mn = m * NST + n
                        p.on("v", lambda v, n=n, mn=mn: v.tensor_tensor_scan(
                            out=h_allB[:, n * TC:(n + 1) * TC],
                            data0=a_allB[:, n * TC:(n + 1) * TC],
                            data1=b_allB[:, n * TC:(n + 1) * TC],
                            initial=h_state[:, mn:mn + 1],
                            op0=ALU.mult, op1=ALU.add),
                            mark="v" if n == NST - 1 else None)
                    p.wt("v", "v")
                    p.on("v", lambda v: v.tensor_tensor(
                        out=tmp_allB[:], in0=h_allB[:], in1=CbcB[:],
                        op=ALU.mult))
                    tv = tmp_allB[:, :].rearrange("p (n t) -> p t n", n=NST)
                    p.on("v", lambda v, m=m, tv=tv: v.reduce_sum(
                        out=y_ch[:, :], in_=tv, axis=AX), mark="v")
                    p.wt("v", "v")
                    hv = h_allB[:, :].rearrange("p (n t) -> p n t", n=NST)
                    p.on("v", lambda v, m=m, hv=hv: v.tensor_copy(
                        out=h_state[:, m * NST:(m + 1) * NST],
                        in_=hv[:, :, TC - 1]))
                    p.on("v", lambda v, m=m, sl=slm: v.scalar_tensor_tensor(
                        out=t1f[:], in0=uT_rdB[:, sl],
                        scalar=D_sb[:, m:m + 1],
                        in1=y_ch[:, :],
                        op0=ALU.mult, op1=ALU.add))
                    p.on("v", lambda v, m=m, sl=slm: v.tensor_tensor(
                        out=yfin[:, m * TC:(m + 1) * TC], in0=t1f[:],
                        in1=sz_rdB[:, sl], op=ALU.mult), mark="v")
                    if DEBUG and ck == 0 and m == 0:
                        p.wt("y", "v")
                        p.dma("g", lambda g: g.dma_start(
                            out=dbg["dbg_aall"][:, :], in_=a_allB[:]))
                        p.dma("g", lambda g: g.dma_start(
                            out=dbg["dbg_hall"][:, :], in_=h_allB[:]))
                if DEBUG and ck == 0:
                    p.wt("y", "v")
                    p.dma("y", lambda y: y.dma_start(
                        out=dbg["dbg_yfin"][:, :], in_=yfin[:]))
                if DEBUG:
                    p.wt("y", "v")
                    p.dma("y", lambda y, ck=ck: y.dma_start(
                        out=dbg["dbg_hs4"][ck, :, :], in_=h_state[:]))
                    p.dma("y", lambda y, ck=ck: y.dma_start(
                        out=dbg["dbg_yf4"][ck, :, :], in_=yfin[:]))
                vpad(); vpad()
                p.wt("y", "v")
                p.dma("y", lambda y, ck=ck: y.dma_start(
                    out=yf_d[ck * 128:(ck + 1) * 128, :], in_=yfin[:]))
                vpad(); vpad(); vpad(); vpad()
                p.wt("y", "v")
                for tb in range(4):
                    tile = ck * 4 + tb
                    p.dma("y", lambda y, t_=tile, tb=tb: y.dma_start(
                        out=xt4[:, tb * D:(tb + 1) * D],
                        in_=xs[t_ * 128:(t_ + 1) * 128, :]))
                pes_after_m = {}
                pe_chunk_start = p.ct["pe"]
                for half in range(2):
                    for m in range(NM):
                        slw = slice((m % 2) * 512, (m % 2) * 512 + 512)
                        if m >= 2:
                            p.wt("g", "pe", pes_after_m[(half, m - 2)])
                        elif half == 1:
                            p.wt("g", "pe", pes_after_m[(0, NM - 2 + m)])
                        else:
                            p.wt("g", "pe", pe_chunk_start)
                        p.dma("g", lambda g, m=m, half=half, sl=slw:
                              g.dma_start(
                                  out=Wo_bf[:, sl],
                                  in_=Wout[:, m * D + half * 512:
                                           m * D + half * 512 + 512]))
                        p.wt("t", "dg")
                        if m == 0:
                            for tb in range(4):
                                bank = [ps0, ps1, ps2, ps3][tb]
                                psum_guard(bank)
                            p.wt("t", "v")
                        for tb in range(4):
                            bank = [ps0, ps1, ps2, ps3][tb]
                            p.on("t", lambda t, m=m, tb=tb, b=bank, sl=slw:
                                 t.matmul(
                                     out=b[:],
                                     lhsT=yfin[:, m * TC + tb * 128:
                                               m * TC + (tb + 1) * 128],
                                     rhs=Wo_bf[:, sl],
                                     start=(m == 0), stop=(m == NM - 1)),
                                 mark="pe" if tb == 3 else None)
                        pes_after_m[(half, m)] = p.ct["pe"]
                    p.wt("v", "pe"); p.wt("v", "ds"); p.wt("v", "s")
                    for tb in range(4):
                        bank = [ps0, ps1, ps2, ps3][tb]
                        p.on("v", lambda v, tb=tb, b=bank, half=half:
                             v.scalar_tensor_tensor(
                                 out=xr4[:, tb * D + half * 512:
                                         tb * D + half * 512 + 512],
                                 in0=xt4[:, tb * D + half * 512:
                                         tb * D + half * 512 + 512],
                                 scalar=2.0, in1=b[:],
                                 op0=ALU.mult, op1=ALU.add), mark="v")
                        set_rd(bank, "v")
                if DEBUG and ck == 0:
                    p.wt("y", "v")
                    p.dma("y", lambda y: y.dma_start(
                        out=dbg["dbg_xr"][:, :], in_=xr4[:]))
                for tb in range(4):
                    xap = xr4[:, tb * D:(tb + 1) * D]
                    ln_stats(p, xap, lnsc[:], cols, D)
                    p.wt("v", "ds")
                    p.on("v", lambda v, xap=xap, tb=tb: v.tensor_scalar(
                        out=xt4[:, tb * D:(tb + 1) * D], in0=xap,
                        scalar1=cols["rstd"], scalar2=cols["nmr"],
                        op0=ALU.mult, op1=ALU.add), mark="v")
                vpad(); vpad()
                p.wt("y", "v")
                p.dma("y", lambda y, ck=ck: y.dma_start(
                    out=xr_d[ck * 512:(ck + 1) * 512, :].rearrange(
                        "(t p) c -> p t c", p=128),
                    in_=xr4[:, :].rearrange("p (t c) -> p t c", t=4)))
                p.dma("y", lambda y, ck=ck: y.dma_start(
                    out=x1[ck * 512:(ck + 1) * 512, :].rearrange(
                        "(t p) c -> p t c", p=128),
                    in_=xt4[:, :].rearrange("p (t c) -> p t c", t=4)))
                p.dma("y", lambda y, ck=ck: y.dma_start(
                    out=x1b[ck * 512:(ck + 1) * 512, :].rearrange(
                        "(t p) c -> p t c", p=128),
                    in_=xt4[:, :].rearrange("p (t c) -> p t c", t=4)))

        p.on("v", lambda v: v.memset(lncol[:, 9:10], 0.0), mark="v")
        p.on("v", lambda v: v.memset(lncol[:, 9:10], 0.0), mark="v")
        p.on("s", lambda s_: s_.activation(out=lncol[:, 10:11],
                                           in_=lncol[:, 9:10],
                                           func=AF.Square), mark="s")
        p.on("s", lambda s_: s_.activation(out=lncol[:, 11:12],
                                           in_=lncol[:, 9:10],
                                           func=AF.Square), mark="s")
        if ctx is None:
            p.wt("y", "v"); p.wt("y", "s")
            p.wt("y", "ds")
            p.wt("y", "dg")
            p.replay(block)
            for g in p.guards:
                g.__exit__(None, None, None)
    for g in reversed(_stage_guards):
        g.__exit__(None, None, None)
    return nc


def _stage1_inputs(x_full, w):
    ident = np.eye(128).astype(BF)
    ones1 = np.ones((1, 128), np.float32)
    conv_w = np.ascontiguousarray(
        w["conv_w"].reshape(DI, 4).reshape(NM, 128, 4).transpose(1, 0, 2)
        .reshape(128, NM * 4))
    r128 = lambda a: np.ascontiguousarray(np.asarray(a).reshape(NM, 128).T)
    Wxp_r = np.ascontiguousarray(
        np.asarray(w["W_xproj"]).reshape(NM, 128, 96).transpose(1, 0, 2)
        .reshape(128, NM * 96))
    Alog_r = np.ascontiguousarray(
        np.asarray(w["A_log"]).reshape(NM, 128, NST).transpose(1, 0, 2)
        .reshape(128, NM * NST))
    Wout_r = np.ascontiguousarray(
        np.asarray(w["W_out"]).reshape(NM, 128, D).transpose(1, 0, 2)
        .reshape(128, NM * D))
    common = dict(
        lnw=np.asarray(w["ln_w"]).reshape(1, D),
        lnb=np.asarray(w["ln_b"]).reshape(1, D),
        Win=np.ascontiguousarray(
            np.asarray(w["W_in"]).reshape(8, 128, 2 * DI).transpose(1, 0, 2)
            .reshape(128, 8 * 2 * DI)), convw=conv_w,
        convb=r128(w["conv_b"]), Wxp=Wxp_r, Wdt=np.asarray(w["W_dt"]),
        bdt=r128(w["b_dt"]), Alog=Alog_r, Dvec=r128(w["D"]), Wout=Wout_r,
        ident=ident, ones1=ones1)
    maps = []
    for c in range(NCORE):
        xsl = x_full[c * S:(c + 1) * S]
        if c == 0:
            xhalo = np.zeros((128, D), np.float32)
            hs = np.zeros((128, 1), np.float32)
        else:
            xhalo = x_full[c * S - 128:c * S]
            hs = np.ones((128, 1), np.float32)
        m = dict(common)
        m.update(xs=xsl, xh=xhalo, hsc=hs)
        maps.append({k: np.ascontiguousarray(np.asarray(v, np.float32))
                     if k != "ident" else v for k, v in m.items()})
    return maps


def run_stage1(x_full, w, full=False):
    nc = build_stage1()
    maps = _stage1_inputs(x_full, w)
    res = run_bass_kernel_spmd(nc, maps, core_ids=list(range(NCORE)))
    if full:
        return res.results
    return np.concatenate([np.asarray(r["x1"]) for r in res.results], axis=0)


USE_FUSED = True


def kernel(**inputs):
    if USE_FUSED:
        return kernel_fused(**inputs)
    x = np.ascontiguousarray(np.asarray(inputs["L0_attn_x"],
                                        np.float32)[0])
    w1m = {k[len("L0_attn_"):]: inputs[k] for k in inputs
           if k.startswith("L0_attn_") and k != "L0_attn_x"}
    l0f = {k[len("L0_ffn_"):]: np.asarray(inputs[k]) for k in inputs
           if k.startswith("L0_ffn_")}
    w4 = {k[len("L1_attn_"):]: np.asarray(inputs[k]) for k in inputs
          if k.startswith("L1_attn_")}
    l1f = {k[len("L1_ffn_"):]: np.asarray(inputs[k]) for k in inputs
           if k.startswith("L1_ffn_")}
    lsel = lambda lf, i: {k: v[i] for k, v in lf.items()}

    x1 = run_stage1(x, w1m)
    t1 = run_local_layer(x1, lsel(l0f, 0), post_ln=False)
    x2 = run_local_layer(t1, lsel(l0f, 1), post_ln=True, xres_full=x1)
    x3 = run_stage4(x2, w4)
    t2 = run_local_layer(x3, lsel(l1f, 0), post_ln=False)
    x4 = run_local_layer(t2, lsel(l1f, 1), post_ln=True, xres_full=x3)
    return x4.reshape(1, T_FULL, D)



EXT = 26
EXTT = EXT * 128
QT0 = 9
KT0 = 2
NQT = EXT - QT0
NKT = EXT - KT0
HEADS = 8
DH = 64
INNER = 512
MC = 22
HID2 = MC * 128


def build_local_layer(post_ln, ctx=None):
    tag = "" if ctx is None else ctx["tag"]
    if ctx is None:
        nc = bass.Bass("TRN2", target_bir_lowering=False, debug=False,
                       num_devices=NCORE)
        din = lambda nm, sh, dt=F32: nc.dram_tensor(nm, sh, dt,
                                                    kind="ExternalInput").ap()
        xw = din("xw", [EXTT, D])
        xres = din("xres", [S, D]) if post_ln else None
    else:
        nc = ctx["nc"]
        din = lambda nm, sh, dt=F32: ctx["io"][nm]
        xw = ctx["x_in"]
        hal = ctx["hal"]
        xres = ctx.get("xres")
    hsc2 = din("hsc2", [1, 1])
    wqkv_r = din("wqkv_r", [128, 8 * 3 * INNER])
    qs_row = din("qs_row", [1, INNER]); ks_row = din("ks_row", [1, INNER])
    wo_r = din("wo_r", [128, 4 * D])
    lng = din("lng", [1, D]); lnb2 = din("lnb2", [1, D])
    w1a_r = din("w1a_r", [MC, 128, 8 * 128])
    w1g_r = din("w1g_r", [MC, 128, 8 * 128])
    b1a_c = din("b1a_c", [128, MC]); b1g_c = din("b1g_c", [128, MC])
    w2_r = din("w2_r", [128, MC * D])
    b2row = din("b2row", [1, D])
    tq_cs = din("tq_cs", [EXTT, INNER]); tq_ss = din("tq_ss", [EXTT, INNER])
    tk_cs = din("tk_cs", [EXTT, INNER]); tk_ss = din("tk_ss", [EXTT, INNER])
    msk = din("msk", [5, 128, 640])
    if ctx is None:
        ident_in = nc.dram_tensor("ident", [128, 128], BF16,
                                  kind="ExternalInput").ap()
        x2 = nc.dram_tensor("x2", [S, D], F32, kind="ExternalOutput").ap()
    else:
        ident_in = ctx["io"]["ident"]
        x2 = ctx["x_out"]
    ones_in = din("ones1", [1, 128])
    xmid_sp = nc.dram_tensor(tag + "xmid_sp", [NQT * 128, D], F32)
    d2 = {}
    if DEBUG2:
        for nm, sh, dt in [("d2_qT", [128, 4 * NQT * 128], BF16),
                           ("d2_kT", [128, 4 * NKT * 128], BF16),
                           ("d2_vbuf", [128, NKT * INNER], BF16),
                           ("d2_P", [128, 640], F32),
                           ("d2_Pm", [128, 640], BF16),
                           ("d2_otok", [128, 512], F32),
                           ("d2_xmid", [NQT * 128, D], F32),
                           ("d2_lnT", [128, 8 * S], BF16),
                           ("d2_xT", [128, 8 * EXTT], BF16),
                           ("d2_wq", [128, 8 * 3 * INNER], BF16),
                           ("d2_qbuf", [128, 512], F32),
                           ("d2_v2", [128, 512], BF16),
                           ("d2_xT2", [128, 8 * EXTT], BF16),
                           ("d2_hh", [128, S], BF16)]:
            d2[nm] = nc.dram_tensor(nm, sh, dt, kind="ExternalOutput").ap()

    _stage_guards = []

    def sb(nm, sh, dt=F32):
        g = nc.sbuf_tensor(tag + nm, sh, dt)
        h = g.__enter__()
        _stage_guards.append(g)
        return h
    ident = sb("identb", [128, 128], BF16)
    ones1 = sb("ones1b", [1, 128])
    ones_bf = sb("ones_bfb", [1, 128], BF16)
    b2_bf = sb("b2_bfb", [1, D], BF16)
    lng_b = sb("lng_b", [128, D]); lnb2_b = sb("lnb2_b", [128, D])
    b1a_sb = sb("b1a_sb", [128, MC]); b1g_sb = sb("b1g_sb", [128, MC])
    hsc2_sb = sb("hsc2_sb", [1, 1])
    lnsc = sb("lnsc2", [128, D]); lncol = sb("lncol2", [128, 12])
    padv = sb("padv", [128, 96]); pads = sb("pads", [128, 32])

    if ctx is None:
        psA = nc.alloc_psum_tensor("psA", [128, 512], F32)
        psB = nc.alloc_psum_tensor("psB", [128, 512], F32)
        psC = nc.alloc_psum_tensor("psC", [128, 512], F32)
        psT = nc.alloc_psum_tensor("psTl", [128, 1024], BF16)
        ps_sc = nc.alloc_psum_tensor("ps_sc", [128, 640], F32)
        ps_o = nc.alloc_psum_tensor("ps_o", [128, 128], F32)
    else:
        psA, psB, psC, psT = (ctx["psA"], ctx["psB"], ctx["psC"], ctx["psT"])
        ps_sc, ps_o = ctx["ps_sc"], ctx["ps_o"]

    with (nc.Block() if ctx is None else contextlib.nullcontext()) as block:
        p = P(nc) if ctx is None else ctx["p"]

        def vpad():
            p.on("v", lambda v: v.memset(padv[:], 0.0), mark="v")

        def spad():
            p.on("s", lambda s_: s_.activation(out=pads[:],
                                               in_=padv[:, 0:pads.shape[1]],
                                               func=AF.Square), mark="s")
        cols = dict(sum=lncol[:, 0:1], ssq=lncol[:, 1:2], mean=lncol[:, 2:3],
                    nmean=lncol[:, 3:4], ex2=lncol[:, 4:5], nvar=lncol[:, 5:6],
                    lnv=lncol[:, 6:7], rstd=lncol[:, 7:8], nmr=lncol[:, 8:9])
        last_rd = ({id(b): ("v", 0) for b in (psA, psB, psC, psT, ps_sc,
                                              ps_o)}
                   if ctx is None else ctx["last_rd"])

        def psum_guard(bank):
            p.wt("t", last_rd[id(bank)][0], last_rd[id(bank)][1])

        def set_rd(bank, eng):
            last_rd[id(bank)] = (eng, p.ct[eng])

        def bcast_row(dst, width):
            for off in range(0, width, 512):
                wdt = min(512, width - off)
                psum_guard(psA)
                p.on("t", lambda t, dst=dst, off=off, wdt=wdt: t.matmul(
                    out=psA[:, 0:wdt], lhsT=ones1[:, :],
                    rhs=dst[0:1, off:off + wdt], start=True, stop=True),
                    mark="pe")
                p.wt("v", "pe")
                p.on("v", lambda v, dst=dst, off=off, wdt=wdt: v.tensor_copy(
                    out=dst[:, off:off + wdt], in_=psA[:, 0:wdt]), mark="v")
                set_rd(psA, "v")

        def hv(buf, half):
            return buf[:, :].rearrange("p (h s d) -> p h s d",
                                       h=HEADS, s=2)[:, :, half]

        p.dma("g", lambda g: g.dma_start(out=ident[:], in_=ident_in[:, :]))
        p.dma("g", lambda g: g.dma_start(out=ones_bf[:], in_=ones_in[:, :]))
        p.dma("g", lambda g: g.dma_start(out=b2_bf[:], in_=b2row[:, :]))
        for dst, src, row in [(ones1, ones_in, True), (lng_b, lng, True),
                              (lnb2_b, lnb2, True), (b1a_sb, b1a_c, False),
                              (b1g_sb, b1g_c, False), (hsc2_sb, hsc2, True)]:
            p.dma("y", lambda y, d=dst, s_=src, row=row: y.dma_start(
                out=d[0:1, :] if row else d[:], in_=s_[:, :]))
        p.wt("t", "ds"); p.wt("t", "dg")
        for dst, wdt in [(lng_b, D), (lnb2_b, D)]:
            bcast_row(dst, wdt)

        with contextlib.ExitStack() as stQ:
            sQ = lambda nm, sh, dt=BF16: stQ.enter_context(
                nc.sbuf_tensor(tag + nm, sh, dt))
            qs_b = stQ.enter_context(
                nc.sbuf_tensor(tag + "qs_b", [128, INNER], F32))
            ks_b = stQ.enter_context(
                nc.sbuf_tensor(tag + "ks_b", [128, INNER], F32))
            msk_sb = stQ.enter_context(
                nc.sbuf_tensor(tag + "msk_sb", [128, 5 * 640], F32))
            qT = sQ("qT", [128, 4 * NQT * 128])
            p.dma("y", lambda y: y.dma_start(out=qs_b[0:1, :],
                                             in_=qs_row[:, :]))
            p.dma("y", lambda y: y.dma_start(out=ks_b[0:1, :],
                                             in_=ks_row[:, :]))
            for i in range(5):
                p.dma("y", lambda y, i=i: y.dma_start(
                    out=msk_sb[:, i * 640:(i + 1) * 640], in_=msk[i, :, :]))
            p.wt("t", "ds")
            for dst, wdt in [(qs_b, INNER), (ks_b, INNER)]:
                bcast_row(dst, wdt)
            kT = sQ("kT", [128, 4 * NKT * 128])
            vbuf = sQ("vbuf", [128, NKT * INNER])

            with contextlib.ExitStack() as st1:
                s1 = lambda nm, sh, dt=BF16: st1.enter_context(
                    nc.sbuf_tensor(tag + nm, sh, dt))
                xT = s1("xT", [128, 8 * EXTT])
                wqkv_bf = s1("wqkv_bf", [128, 8 * 3 * INNER])
                xw_t = st1.enter_context(
                    nc.sbuf_tensor(tag + "xw_t", [128, 2 * D], F32))
                xw_tb = st1.enter_context(
                    nc.sbuf_tensor(tag + "xw_tb", [128, D], BF16))
                if ctx is not None:
                    hscH = st1.enter_context(
                        nc.sbuf_tensor(tag + "hscH", [128, 1], F32))
                    p.dma("y", lambda y: y.dma_start(
                        out=hscH[:], in_=ctx["io"]["hscH"]))
                xbf = s1("xbf", [128, D])
                qbuf = st1.enter_context(
                    nc.sbuf_tensor(tag + "qbuf", [128, 512], F32))
                sqb = st1.enter_context(
                    nc.sbuf_tensor(tag + "sqb", [128, 512], F32))
                nq = st1.enter_context(nc.sbuf_tensor(tag + "nq", [128, 32], F32))
                t1b = st1.enter_context(
                    nc.sbuf_tensor(tag + "t1b", [128, 512], F32))
                t2b = st1.enter_context(
                    nc.sbuf_tensor(tag + "t2b", [128, 512], F32))
                qnb = st1.enter_context(
                    nc.sbuf_tensor(tag + "qnb", [128, 512], F32))
                qrbf = s1("qrbf", [128, 512])
                tbl = st1.enter_context(
                    nc.sbuf_tensor(tag + "tbl", [128, 2 * 512], F32))
                p.dma("g", lambda g: g.dma_start(out=wqkv_bf[:],
                                                 in_=wqkv_r[:, :]))

                for T in range(EXT):
                    slx = slice((T % 2) * D, (T % 2) * D + D)
                    p.wt("y", "v")
                    if ctx is not None and T < 10:
                        def ldhal(g, T=T):
                            prev = _gp_prev(nc, ctx)
                            blk = hal[bass.ts(prev, 1280), :]
                            return g.dma_start(
                                out=xw_tb[:],
                                in_=blk[T * 128:(T + 1) * 128, :])
                        p.wt("g", "v")
                        p.dma("g", ldhal)
                        p.wt("v", "dg"); p.wt("v", "pe")
                        p.on("v", lambda v: v.tensor_scalar(
                            out=xbf[:], in0=xw_tb[:], scalar1=hscH[:, 0:1],
                            scalar2=None, op0=ALU.mult), mark="v")
                    else:
                        src = (xw[T * 128:(T + 1) * 128, :] if ctx is None
                               else xw[(T - 10) * 128:(T - 9) * 128, :])
                        p.dma("y", lambda y, src=src, sl=slx: y.dma_start(
                            out=xw_t[:, sl], in_=src))
                        p.wt("v", "ds"); p.wt("v", "pe")
                        p.on("v", lambda v, sl=slx: v.tensor_copy(
                            out=xbf[:], in_=xw_t[:, sl]), mark="v")
                    p.wt("t", "v")
                    psum_guard(psT)
                    for j in range(8):
                        p.on("t", lambda t, j=j: t.transpose(
                            out=psT[:, j * 128:(j + 1) * 128],
                            in_=xbf[:, j * 128:(j + 1) * 128],
                            identity=ident[:]),
                            mark="pe" if j == 7 else None)
                    p.wt("s", "pe")
                    for j in range(8):
                        p.on("s", lambda s, j=j, T=T: s.copy(
                            out=xT[:, j * EXTT + T * 128:
                                   j * EXTT + (T + 1) * 128],
                            in_=psT[:, j * 128:(j + 1) * 128]), mark="s")
                    set_rd(psT, "s")

                def qk_post(T, src_psum, which):
                    qsb = qs_b if which == "q" else ks_b
                    tcs = tq_cs if which == "q" else tk_cs
                    tss = tq_ss if which == "q" else tk_ss
                    dstT = qT if which == "q" else kT
                    t0 = QT0 if which == "q" else KT0
                    stride = NQT * 128 if which == "q" else NKT * 128
                    p.wt("v", "pe"); p.wt("v", "s")
                    p.on("v", lambda v: v.tensor_copy(out=qbuf[:],
                                                      in_=src_psum[:]),
                         mark="v")
                    set_rd(src_psum, "v")
                    p.wt("s", "v")
                    p.on("s", lambda s: s.activation(out=sqb[:], in_=qbuf[:],
                                                     func=AF.Square),
                         mark="s")
                    p.wt("v", "s")
                    sqv = sqb[:, :].rearrange("p (h d) -> p h d", h=HEADS)
                    p.on("v", lambda v, sqv=sqv: v.reduce_sum(
                        out=nq[:, 0:8], in_=sqv, axis=AX), mark="v")
                    p.wt("v", "v")
                    p.on("v", lambda v: v.tensor_scalar_max(
                        out=nq[:, 8:16], in0=nq[:, 0:8], scalar1=1e-24),
                        mark="v")
                    p.wt("s", "v")
                    p.on("s", lambda s: s.activation(
                        out=nq[:, 16:24], in_=nq[:, 8:16], func=AF.Ln))
                    p.on("s", lambda s: s.activation(
                        out=nq[:, 24:32], in_=nq[:, 16:24], func=AF.Exp,
                        scale=-0.5), mark="s")
                    p.wt("v", "s")
                    for h in range(HEADS):
                        p.on("v", lambda v, h=h, qsb=qsb:
                             v.scalar_tensor_tensor(
                                 out=qnb[:, h * DH:(h + 1) * DH],
                                 in0=qbuf[:, h * DH:(h + 1) * DH],
                                 scalar=nq[:, 24 + h:25 + h],
                                 in1=qsb[:, h * DH:(h + 1) * DH],
                                 op0=ALU.mult, op1=ALU.mult),
                             mark="v" if h == HEADS - 1 else None)
                    p.wt("y", "v")
                    p.dma("y", lambda y, T=T, tcs=tcs: y.dma_start(
                        out=tbl[:, 0:512], in_=tcs[T * 128:(T + 1) * 128, :]))
                    p.dma("y", lambda y, T=T, tss=tss: y.dma_start(
                        out=tbl[:, 512:1024],
                        in_=tss[T * 128:(T + 1) * 128, :]))
                    p.wt("v", "ds")
                    p.on("v", lambda v: v.tensor_tensor(
                        out=t1b[:], in0=qnb[:], in1=tbl[:, 0:512],
                        op=ALU.mult))
                    ssv = lambda half: tbl[:, 512:1024].rearrange(
                        "p (h s d) -> p h s d", h=HEADS, s=2)[:, :, half]
                    p.on("v", lambda v: v.tensor_tensor(
                        out=hv(t2b, 0), in0=hv(qnb, 1), in1=ssv(0),
                        op=ALU.mult))
                    p.on("v", lambda v: v.tensor_tensor(
                        out=hv(t2b, 1), in0=hv(qnb, 0), in1=ssv(1),
                        op=ALU.mult), mark="v")
                    p.on("v", lambda v: v.tensor_tensor(
                        out=hv(qrbf, 0), in0=hv(t1b, 0), in1=hv(t2b, 0),
                        op=ALU.subtract))
                    p.on("v", lambda v: v.tensor_tensor(
                        out=hv(qrbf, 1), in0=hv(t1b, 1), in1=hv(t2b, 1),
                        op=ALU.add), mark="v")
                    p.wt("t", "v")
                    psum_guard(psT)
                    for j in range(4):
                        p.on("t", lambda t, j=j: t.transpose(
                            out=psT[:, j * 128:(j + 1) * 128],
                            in_=qrbf[:, j * 128:(j + 1) * 128],
                            identity=ident[:]),
                            mark="pe" if j == 3 else None)
                    p.wt("s", "pe")
                    for j in range(4):
                        p.on("s", lambda s, j=j, T=T, dstT=dstT, t0=t0,
                             stride=stride: s.copy(
                                 out=dstT[:, j * stride + (T - t0) * 128:
                                          j * stride + (T - t0 + 1) * 128],
                                 in_=psT[:, j * 128:(j + 1) * 128]),
                             mark="s")
                    set_rd(psT, "s")

                p.wt("t", "dg")
                for T in range(KT0, EXT):
                    for nb, bank in ((0, psA), (1, psB), (2, psC)):
                        psum_guard(bank)
                        p.wt("t", "s")
                        for k in range(8):
                            coff = T * 128 - (1 if k >= 4 else 0)
                            p.on("t", lambda t, k=k, nb=nb, bank=bank,
                                 coff=coff: t.matmul(
                                     out=bank[:],
                                     lhsT=xT[:, k * EXTT + coff:
                                             k * EXTT + coff + 128],
                                     rhs=wqkv_bf[:,
                                                 k * 3 * INNER + nb * 512:
                                                 k * 3 * INNER + nb * 512
                                                 + 512],
                                     start=(k == 0), stop=(k == 7)),
                                 mark="pe" if k == 7 else None)
                    p.wt("v", "pe")
                    p.on("v", lambda v, T=T: v.tensor_copy(
                        out=vbuf[:, (T - KT0) * INNER:
                                 (T - KT0 + 1) * INNER],
                        in_=psC[:]), mark="v")
                    set_rd(psC, "v")
                    if DEBUG2 and T == KT0:
                        p.wt("y", "v"); p.wt("y", "s")
                        p.dma("y", lambda y: y.dma_start(
                            out=d2["d2_v2"][:, :], in_=vbuf[:, 0:512]))
                        p.dma("y", lambda y: y.dma_start(
                            out=d2["d2_xT2"][:, :], in_=xT[:]))
                    if DEBUG2 and T == 10:
                        p.wt("y", "v"); p.wt("y", "s")
                        p.dma("y", lambda y: y.dma_start(
                            out=d2["d2_wq"][:, :], in_=wqkv_bf[:]))
                        p.on("v", lambda v: v.tensor_copy(
                            out=qbuf[:], in_=psA[:]), mark="v")
                        p.wt("y", "v")
                        p.dma("y", lambda y: y.dma_start(
                            out=d2["d2_qbuf"][:, :], in_=qbuf[:]))
                    qk_post(T, psB, "k")
                    if T >= QT0:
                        qk_post(T, psA, "q")
                    else:
                        p.wt("v", "pe"); p.wt("v", "s")
                        p.on("v", lambda v: v.tensor_copy(out=qbuf[:],
                                                          in_=psA[:]),
                             mark="v")
                        set_rd(psA, "v")

            if DEBUG2:
                p.wt("y", "s"); p.wt("y", "v")
                p.dma("y", lambda y: y.dma_start(out=d2["d2_xT"][:, :],
                                                 in_=xT[:]))
                p.dma("y", lambda y: y.dma_start(out=d2["d2_qT"][:, :],
                                                 in_=qT[:]))
                p.dma("y", lambda y: y.dma_start(out=d2["d2_kT"][:, :],
                                                 in_=kT[:]))
                p.dma("y", lambda y: y.dma_start(out=d2["d2_vbuf"][:, :],
                                                 in_=vbuf[:]))
            with contextlib.ExitStack() as st2:
                s2 = lambda nm, sh, dt=BF16: st2.enter_context(
                    nc.sbuf_tensor(tag + nm, sh, dt))
                P_f32 = st2.enter_context(
                    nc.sbuf_tensor(tag + "P_f32", [128, 640], F32))
                Pm_bf = s2("Pm_bf", [128, 640])
                PT_sb = s2("PT_sb", [128, 640])
                o_tok = st2.enter_context(
                    nc.sbuf_tensor(tag + "o_tok", [128, 512], F32))
                obf = s2("obf", [128, 512])
                oT = s2("oT", [128, 4 * NQT * 128])
                wo_bf = s2("wo_bf", [128, 4 * D])
                acol = st2.enter_context(
                    nc.sbuf_tensor(tag + "acol2", [128, 4], F32))
                xm_t = st2.enter_context(
                    nc.sbuf_tensor(tag + "xm_t", [128, 2 * D], F32))
                xw_t2 = st2.enter_context(
                    nc.sbuf_tensor(tag + "xw_t2", [128, 2 * D], F32))
                p.wt("g", "pe"); p.wt("g", "v"); p.wt("g", "s")
                p.dma("g", lambda g: g.dma_start(out=wo_bf[:], in_=wo_r[:, :]))
                for T in range(QT0, EXT):
                    for h in range(HEADS):
                        pt = h // 2
                        pb = (h % 2) * 64
                        msel = 0
                        tl = T - 10
                        if 0 <= tl <= 3:
                            msel = 1 + tl
                        psum_guard(ps_sc)
                        p.wt("t", "dg")
                        for so, sw in ((0, 512), (512, 128)):
                            p.on("t", lambda t, T=T, pt=pt, pb=pb, so=so,
                                 sw=sw: t.matmul(
                                     out=ps_sc[:, so:so + sw],
                                     lhsT=qT[pb:pb + 64,
                                             pt * NQT * 128 + (T - QT0) * 128:
                                             pt * NQT * 128 +
                                             (T - QT0 + 1) * 128],
                                     rhs=kT[pb:pb + 64,
                                            pt * NKT * 128 + (T - 6) * 128
                                            + so:
                                            pt * NKT * 128 + (T - 6) * 128
                                            + so + sw],
                                     start=True, stop=True),
                                 mark="pe" if so else None)
                        p.wt("v", "pe"); p.wt("v", "s")
                        p.on("v", lambda v: v.reduce_max(
                            out=acol[:, 0:1], in_=ps_sc[:, :], axis=AX,
                            negate=True), mark="v")
                        p.wt("s", "v")
                        p.on("s", lambda s: s.activation(
                            out=P_f32[:], in_=ps_sc[:, :], func=AF.Exp,
                            bias=acol[:, 0:1]), mark="s")
                        set_rd(ps_sc, "s")
                        p.wt("v", "s")
                        p.on("v", lambda v, msel=msel: v.scalar_tensor_tensor(
                            out=Pm_bf[:], in0=P_f32[:], scalar=1.0,
                            in1=msk_sb[:, msel * 640:(msel + 1) * 640],
                            op0=ALU.mult, op1=ALU.mult,
                            accum_out=acol[:, 1:2]), mark="v")
                        p.wt("v", "v")
                        p.on("v", lambda v: v.reciprocal(
                            out=acol[:, 2:3], in_=acol[:, 1:2]), mark="v")
                        if DEBUG2 and T == 10 and h == 0:
                            p.wt("y", "v"); p.wt("y", "s")
                            p.dma("y", lambda y: y.dma_start(
                                out=d2["d2_P"][:, :], in_=P_f32[:]))
                            p.dma("y", lambda y: y.dma_start(
                                out=d2["d2_Pm"][:, :], in_=Pm_bf[:]))
                        p.wt("t", "v")
                        psum_guard(psT)
                        for j in range(5):
                            p.on("t", lambda t, j=j: t.transpose(
                                out=psT[:, j * 128:(j + 1) * 128],
                                in_=Pm_bf[:, j * 128:(j + 1) * 128],
                                identity=ident[:]),
                                mark="pe" if j == 4 else None)
                        p.wt("s", "pe")
                        p.on("s", lambda s: s.copy(out=PT_sb[:],
                                                   in_=psT[:, 0:640]),
                             mark="s")
                        set_rd(psT, "s")
                        psum_guard(ps_o)
                        p.wt("t", "s")
                        for j in range(5):
                            p.on("t", lambda t, j=j, T=T, h=h: t.matmul(
                                out=ps_o[:, 0:64],
                                lhsT=PT_sb[:, j * 128:(j + 1) * 128],
                                rhs=vbuf[:, (T - 6 + j) * INNER + h * DH:
                                         (T - 6 + j) * INNER + h * DH + 64],
                                start=(j == 0), stop=(j == 4)),
                                mark="pe" if j == 4 else None)
                        p.wt("v", "pe")
                        p.on("v", lambda v, h=h: v.tensor_scalar(
                            out=o_tok[:, h * DH:(h + 1) * DH],
                            in0=ps_o[:, 0:64], scalar1=acol[:, 2:3],
                            scalar2=None, op0=ALU.mult), mark="v")
                        set_rd(ps_o, "v")
                    if DEBUG2 and T == 10:
                        p.wt("y", "v")
                        p.dma("y", lambda y: y.dma_start(
                            out=d2["d2_otok"][:, :], in_=o_tok[:]))
                    p.on("v", lambda v: v.tensor_copy(out=obf[:],
                                                      in_=o_tok[:]),
                         mark="v")
                    p.wt("t", "v")
                    psum_guard(psT)
                    for j in range(4):
                        p.on("t", lambda t, j=j: t.transpose(
                            out=psT[:, j * 128:(j + 1) * 128],
                            in_=obf[:, j * 128:(j + 1) * 128],
                            identity=ident[:]), mark="pe" if j == 3 else None)
                    p.wt("s", "pe")
                    for j in range(4):
                        p.on("s", lambda s, j=j, T=T: s.copy(
                            out=oT[:, j * NQT * 128 + (T - QT0) * 128:
                                   j * NQT * 128 + (T - QT0 + 1) * 128],
                            in_=psT[:, j * 128:(j + 1) * 128]), mark="s")
                    set_rd(psT, "s")
                for T in range(QT0, EXT):
                    slx = slice((T % 2) * D, (T % 2) * D + D)
                    p.wt("y", "v")
                    if ctx is not None and T == QT0:
                        def ldhal9(g, sl=slx):
                            prev = _gp_prev(nc, ctx)
                            blk = hal[bass.ts(prev, 1280), :]
                            return g.dma_start(
                                out=xw_t2[:, sl],
                                in_=blk[QT0 * 128:(QT0 + 1) * 128, :])
                        p.wt("g", "v")
                        p.dma("g", ldhal9)
                        p.wt("v", "dg")
                    else:
                        src = (xw[T * 128:(T + 1) * 128, :] if ctx is None
                               else xw[(T - 10) * 128:(T - 9) * 128, :])
                        p.dma("y", lambda y, src=src, sl=slx: y.dma_start(
                            out=xw_t2[:, sl], in_=src))
                    for nb, bank in ((0, psA), (1, psB)):
                        psum_guard(bank)
                        p.wt("t", "s"); p.wt("t", "dg")
                        for k in range(4):
                            p.on("t", lambda t, k=k, T=T, nb=nb, bank=bank:
                                 t.matmul(
                                     out=bank[:],
                                     lhsT=oT[:, k * NQT * 128 +
                                             (T - QT0) * 128:
                                             k * NQT * 128 +
                                             (T - QT0 + 1) * 128],
                                     rhs=wo_bf[:, k * D + nb * 512:
                                               k * D + nb * 512 + 512],
                                     start=(k == 0), stop=(k == 3)),
                                 mark="pe" if k == 3 else None)
                    p.wt("v", "pe"); p.wt("v", "ds")
                    slm = slice((T % 2) * D, (T % 2) * D + D)
                    for nb, bank in ((0, psA), (1, psB)):
                        p.on("v", lambda v, nb=nb, bank=bank, sl=slx:
                             v.tensor_tensor(
                                 out=xm_t[:, sl.start + nb * 512:
                                          sl.start + nb * 512 + 512],
                                 in0=bank[:],
                                 in1=xw_t2[:, sl.start + nb * 512:
                                           sl.start + nb * 512 + 512],
                                 op=ALU.add), mark="v")
                        set_rd(bank, "v")
                    vpad(); p.wt("y", "v")
                    p.dma("y", lambda y, T=T, sl=slx: y.dma_start(
                        out=xmid_sp[(T - QT0) * 128:(T - QT0 + 1) * 128, :],
                        in_=xm_t[:, sl]))
                    if DEBUG2:
                        p.dma("y", lambda y, T=T, sl=slx: y.dma_start(
                            out=d2["d2_xmid"][(T - QT0) * 128:
                                              (T - QT0 + 1) * 128, :],
                            in_=xm_t[:, sl]))

        with contextlib.ExitStack() as st3:
            s3 = lambda nm, sh, dt=BF16: st3.enter_context(
                nc.sbuf_tensor(tag + nm, sh, dt))
            lnT = s3("lnT", [128, 8 * S])
            hh = s3("hh", [128, MC * S])
            w2_bf = s3("w2_bf", [128, MC * D])
            w1s = s3("w1s", [128, 2 * 8 * 128])
            xs2 = st3.enter_context(nc.sbuf_tensor(tag + "xs2", [128, D], F32))
            tnormA = st3.enter_context(
                nc.sbuf_tensor(tag + "tnormA", [128, D], F32))
            tnormB = lnsc
            ln_bf = s3("ln_bf", [128, D])
            gbuf = st3.enter_context(nc.sbuf_tensor(tag + "gbuf", [128, 512], F32))
            x2t = st3.enter_context(nc.sbuf_tensor(tag + "x2t", [128, D], F32))
            x2u = tnormA
            xm_t3 = st3.enter_context(
                nc.sbuf_tensor(tag + "xm_t3", [128, D], F32))
            xr_t = st3.enter_context(
                nc.sbuf_tensor(tag + "xr_t", [128, D], F32))
            tfix = gbuf
            p.wt("g", "pe"); p.wt("g", "v"); p.wt("g", "s")
            p.dma("g", lambda g: g.dma_start(out=w2_bf[:], in_=w2_r[:, :]))
            for tt in range(16):
                r0 = 128 + tt * 128
                p.wt("y", "v"); p.wt("y", "s")
                p.dma("y", lambda y, r0=r0: y.dma_start(
                    out=xs2[:, 0:512], in_=xmid_sp[r0:r0 + 128, 0:512]))
                p.dma("y", lambda y, r0=r0: y.dma_start(
                    out=xs2[:, 512:1024],
                    in_=xmid_sp[r0 - 1:r0 + 127, 512:1024]))
                p.wt("v", "ds"); p.wt("s", "ds")
                if tt == 0:
                    p.on("v", lambda v: v.tensor_scalar(
                        out=tfix[0:1, 0:512], in0=xs2[0:1, 512:1024],
                        scalar1=hsc2_sb[0:1, 0:1], scalar2=None,
                        op0=ALU.mult), mark="v")
                    p.wt("v", "v")
                    p.on("v", lambda v: v.tensor_copy(
                        out=xs2[0:1, 512:1024], in_=tfix[0:1, 0:512]),
                        mark="v")
                    p.wt("v", "v"); p.wt("s", "v")
                ln_stats(p, xs2[:], lnsc[:], cols, D)
                p.on("v", lambda v: v.tensor_scalar(
                    out=tnormA[:], in0=xs2[:], scalar1=cols["rstd"],
                    scalar2=cols["nmr"], op0=ALU.mult, op1=ALU.add))
                p.on("v", lambda v: v.tensor_tensor(
                    out=tnormB[:], in0=tnormA[:], in1=lng_b[:], op=ALU.mult))
                p.on("v", lambda v: v.tensor_tensor(
                    out=ln_bf[:], in0=tnormB[:], in1=lnb2_b[:], op=ALU.add),
                    mark="v")
                p.wt("t", "v")
                psum_guard(psT)
                for j in range(8):
                    p.on("t", lambda t, j=j: t.transpose(
                        out=psT[:, j * 128:(j + 1) * 128],
                        in_=ln_bf[:, j * 128:(j + 1) * 128],
                        identity=ident[:]), mark="pe" if j == 7 else None)
                p.wt("s", "pe")
                for j in range(8):
                    p.on("s", lambda s, j=j, tt=tt: s.copy(
                        out=lnT[:, j * S + tt * 128:j * S + (tt + 1) * 128],
                        in_=psT[:, j * 128:(j + 1) * 128]), mark="s")
                set_rd(psT, "s")
            if DEBUG2:
                p.wt("y", "s")
                p.dma("y", lambda y: y.dma_start(out=d2["d2_lnT"][:, :],
                                                 in_=lnT[:]))
            pe_hist = {}
            pe_b2start = p.ct["pe"]
            for mc in range(MC):
                if mc >= 1:
                    p.wt("g", "pe", pe_hist.get(mc - 1, pe_b2start))
                else:
                    p.wt("g", "pe", pe_b2start)
                p.dma("g", lambda g, mc=mc: g.dma_start(
                    out=w1s[:, 0:1024], in_=w1a_r[mc, :, :]))
                p.dma("g", lambda g, mc=mc: g.dma_start(
                    out=w1s[:, 1024:2048], in_=w1g_r[mc, :, :]))
                p.wt("t", "dg"); p.wt("t", "s")
                for tc in range(4):
                    for gi, bank in ((0, psA), (1, psB)):
                        psum_guard(bank)
                        for k in range(8):
                            p.on("t", lambda t, k=k, tc=tc, gi=gi, bank=bank:
                                 t.matmul(
                                     out=bank[:],
                                     lhsT=w1s[:, gi * 1024 + k * 128:
                                              gi * 1024 + (k + 1) * 128],
                                     rhs=lnT[:, k * S + tc * 512:
                                             k * S + (tc + 1) * 512],
                                     start=(k == 0), stop=(k == 7)),
                                 mark="pe" if k == 7 else None)
                    p.wt("s", "pe")
                    p.on("s", lambda s, mc=mc: s.activation(
                        out=gbuf[:], in_=psB[:], func=AF.Gelu,
                        bias=b1g_sb[:, mc:mc + 1]), mark="s")
                    set_rd(psB, "s")
                    p.wt("v", "s")
                    p.on("v", lambda v, mc=mc, tc=tc: v.scalar_tensor_tensor(
                        out=hh[:, mc * S + tc * 512:mc * S + (tc + 1) * 512],
                        in0=psA[:], scalar=b1a_sb[:, mc:mc + 1], in1=gbuf[:],
                        op0=ALU.add, op1=ALU.mult), mark="v")
                    set_rd(psA, "v")
                pe_hist[mc] = p.ct["pe"]
            if DEBUG2:
                p.wt("y", "v")
                p.dma("y", lambda y: y.dma_start(out=d2["d2_hh"][:, :],
                                                 in_=hh[:, 0:S]))
            p.wt("t", "dg")
            for tt in range(16):
                r0 = 128 + tt * 128
                p.wt("y", "v")
                p.dma("y", lambda y, r0=r0: y.dma_start(
                    out=xm_t3[:], in_=xmid_sp[r0:r0 + 128, :]))
                if post_ln:
                    p.dma("y", lambda y, tt=tt: y.dma_start(
                        out=xr_t[:], in_=xres[tt * 128:(tt + 1) * 128, :]))
                for nb, bank in ((0, psA), (1, psB)):
                    psum_guard(bank)
                    p.wt("t", "v")
                    p.on("t", lambda t, nb=nb, bank=bank: t.matmul(
                        out=bank[:], lhsT=ones_bf[:, :],
                        rhs=b2_bf[0:1, nb * 512:nb * 512 + 512],
                        start=True, stop=False))
                    for mc in range(MC):
                        p.on("t", lambda t, mc=mc, tt=tt, nb=nb, bank=bank:
                             t.matmul(
                                 out=bank[:],
                                 lhsT=hh[:, mc * S + tt * 128:
                                         mc * S + (tt + 1) * 128],
                                 rhs=w2_bf[:, mc * D + nb * 512:
                                           mc * D + nb * 512 + 512],
                                 start=False, stop=(mc == MC - 1)),
                             mark="pe" if mc == MC - 1 else None)
                p.wt("v", "pe"); p.wt("v", "ds"); p.wt("v", "s")
                for nb, bank in ((0, psA), (1, psB)):
                    p.on("v", lambda v, nb=nb, bank=bank:
                         v.tensor_tensor(
                             out=x2t[:, nb * 512:nb * 512 + 512],
                             in0=bank[:],
                             in1=xm_t3[:, nb * 512:nb * 512 + 512],
                             op=ALU.add), mark="v")
                    set_rd(bank, "v")
                if post_ln:
                    p.on("v", lambda v: v.tensor_tensor(
                        out=x2u[:], in0=x2t[:], in1=xr_t[:], op=ALU.add),
                        mark="v")
                    ln_stats(p, x2u[:], lnsc[:], cols, D)
                    p.on("v", lambda v: v.tensor_scalar(
                        out=x2t[:], in0=x2u[:], scalar1=cols["rstd"],
                        scalar2=cols["nmr"], op0=ALU.mult, op1=ALU.add),
                        mark="v")
                vpad(); p.wt("y", "v")
                p.dma("y", lambda y, tt=tt: y.dma_start(
                    out=x2[tt * 128:(tt + 1) * 128, :], in_=x2t[:]))

        p.on("v", lambda v: v.memset(lncol[:, 9:10], 0.0), mark="v")
        p.on("v", lambda v: v.memset(lncol[:, 9:10], 0.0), mark="v")
        p.on("s", lambda s_: s_.activation(out=lncol[:, 10:11],
                                           in_=lncol[:, 9:10],
                                           func=AF.Square), mark="s")
        p.on("s", lambda s_: s_.activation(out=lncol[:, 11:12],
                                           in_=lncol[:, 9:10],
                                           func=AF.Square), mark="s")
        if ctx is None:
            p.wt("y", "v"); p.wt("y", "s")
            p.wt("y", "ds")
            p.wt("y", "dg")
            p.replay(block)
            for g in p.guards:
                g.__exit__(None, None, None)
    for g in reversed(_stage_guards):
        g.__exit__(None, None, None)
    return nc


def _layer_inputs(x_full, lw, post_ln, xres_full=None):
    f32 = np.float32
    wqkv_r = np.ascontiguousarray(
        np.asarray(lw["wqkv"], f32).reshape(8, 128, 3 * INNER)
        .transpose(1, 0, 2).reshape(128, 8 * 3 * INNER))
    wo_rr = np.ascontiguousarray(
        np.asarray(lw["wo"], f32).reshape(4, 128, D).transpose(1, 0, 2)
        .reshape(128, 4 * D))
    HID = 2730
    w1 = np.asarray(lw["w1"], f32)
    w1a = np.zeros((D, HID2), f32); w1a[:, :HID] = w1[:, :HID]
    w1g = np.zeros((D, HID2), f32); w1g[:, :HID] = w1[:, HID:]
    b1 = np.asarray(lw["b1"], f32)
    b1a = np.zeros(HID2, f32); b1a[:HID] = b1[:HID]
    b1g = np.zeros(HID2, f32); b1g[:HID] = b1[HID:]
    w2p = np.zeros((HID2, D), f32); w2p[:HID] = np.asarray(lw["w2"], f32)
    rearr1 = lambda w: np.ascontiguousarray(
        w.reshape(8, 128, MC, 128).transpose(2, 1, 0, 3)
        .reshape(MC, 128, 8 * 128))
    w2_rr = np.ascontiguousarray(
        w2p.reshape(MC, 128, D).transpose(1, 0, 2).reshape(128, MC * D))
    bcol = lambda b: np.ascontiguousarray(b.reshape(MC, 128).T)
    tile8 = lambda a: np.tile(a, (1, 8))

    common = dict(
        wqkv_r=wqkv_r,
        qs_row=tile8(np.asarray(lw["q_scale"], f32).reshape(1, DH)),
        ks_row=tile8(np.asarray(lw["k_scale"], f32).reshape(1, DH)),
        wo_r=wo_rr, lng=np.asarray(lw["ln_g"], f32).reshape(1, D),
        lnb2=np.asarray(lw["ln_b"], f32).reshape(1, D),
        w1a_r=rearr1(w1a), w1g_r=rearr1(w1g),
        b1a_c=bcol(b1a), b1g_c=bcol(b1g),
        w2_r=w2_rr, b2row=np.asarray(lw["b2"], f32).reshape(1, D),
        ident=np.eye(128).astype(BF),
        ones1=np.ones((1, 128), f32))
    half_idx = np.arange(0, DH, 2, dtype=np.float64)
    inv_freq = 1.0 / 10000.0 ** (half_idx / DH)
    base = (half_idx + 0.4 * DH) / (1.4 * DH)
    ii = np.arange(128)[:, None]
    jj = np.arange(640)[None, :]
    std = ((jj >= ii) & (jj <= ii + 512)).astype(f32)
    maps = []
    for c in range(NCORE):
        lo = c * S - 1280
        xwin = np.zeros((EXTT, D), f32)
        s0 = max(0, -lo)
        xwin[s0:] = x_full[lo + s0:lo + EXTT]
        t_abs = np.arange(lo, lo + EXTT, dtype=np.float64)
        t_c = np.clip(t_abs, 0, None)
        freqs = t_c[:, None] * inv_freq[None, :]
        cosc = np.concatenate([np.cos(freqs)] * 2, 1)
        sinc = np.concatenate([np.sin(freqs)] * 2, 1)
        power = (t_c - T_FULL // 2) / (512 // 2)
        scale = np.concatenate([base[None, :] ** power[:, None]] * 2, 1)
        QS = 8.0
        m = dict(common)
        m.update(
            xw=xwin,
            tq_cs=tile8((cosc * scale * QS).astype(f32)),
            tq_ss=tile8((sinc * scale * QS).astype(f32)),
            tk_cs=tile8((cosc / scale).astype(f32)),
            tk_ss=tile8((sinc / scale).astype(f32)),
            hsc2=np.array([[0.0 if c == 0 else 1.0]], f32))
        mk = np.stack([std] * 5)
        if c == 0:
            for ti in range(4):
                mk[1 + ti] = std * (jj >= 512 - 128 * ti)
        m["msk"] = np.ascontiguousarray(mk.astype(f32))
        if post_ln:
            m["xres"] = np.ascontiguousarray(xres_full[c * S:(c + 1) * S])
        maps.append({k: np.ascontiguousarray(v) if k != "ident" else v
                     for k, v in m.items()})
    return maps


_LAYER_NC = {}


def run_local_layer(x_full, lw, post_ln, xres_full=None, full=False):
    if post_ln not in _LAYER_NC:
        _LAYER_NC[post_ln] = build_local_layer(post_ln)
    nc = _LAYER_NC[post_ln]
    maps = _layer_inputs(x_full, lw, post_ln, xres_full)
    res = run_bass_kernel_spmd(nc, maps, core_ids=list(range(NCORE)))
    if full:
        return res.results
    return np.concatenate([np.asarray(r["x2"]) for r in res.results], axis=0)

G4 = 2
H4 = 16
NT4 = 10


def build_stage4(ctx=None):
    tag = "" if ctx is None else ctx["tag"]
    if ctx is None:
        nc = bass.Bass("TRN2", target_bir_lowering=False, debug=False,
                       num_devices=NCORE)
        din = lambda nm, sh, dt=F32: nc.dram_tensor(nm, sh, dt,
                                                    kind="ExternalInput").ap()
        xg = din("xg", [NT4 * 128, D])
        xr3 = din("xr3", [S, D])
        xg_bf = None
    else:
        nc = ctx["nc"]
        din = lambda nm, sh, dt=F32: ctx["io"][nm]
        xr3 = ctx["x_in"]
        xg = None
        xg_bf = nc.dram_tensor(tag + "xg_bf", [NT4 * 128, D], BF16)
    Wq1r = din("Wq1r", [128, 8 * D]); Wq2r = din("Wq2r", [128, 8 * D])
    Wk1r = din("Wk1r", [128, 8 * 128]); Wk2r = din("Wk2r", [128, 8 * 128])
    Wvr = din("Wvr", [128, 8 * 128])
    Woutr = din("Woutr", [128, 8 * D])
    boutrow = din("boutrow", [1, D])
    lqk = din("lqk", [DTR, 4])
    t4c = din("t4c", [NT4 * 128, 512]); t4s = din("t4s", [NT4 * 128, 512])
    t4cq = din("t4cq", [NT4 * 128, 512]); t4sq = din("t4sq", [NT4 * 128, 512])
    if ctx is None:
        ident_in = nc.dram_tensor("ident", [128, 128], BF16,
                                  kind="ExternalInput").ap()
        x4 = nc.dram_tensor("x4", [S, D], F32, kind="ExternalOutput").ap()
    else:
        ident_in = ctx["io"]["ident"]
        x4 = ctx["x_out"]
    ones_in = din("ones1", [1, 128])

    _stage_guards = []

    def sb(nm, sh, dt=F32):
        g = nc.sbuf_tensor(tag + nm, sh, dt)
        h = g.__enter__()
        _stage_guards.append(g)
        return h
    ident = sb("identb", [128, 128], BF16)
    ones1 = sb("ones1b", [1, 128])
    ones_bf = sb("ones_bfb", [1, 128], BF16)
    bout_bf = sb("bout_bfb", [1, D], BF16)
    lqk_sb = sb("lqk_sb", [DTR, 4])
    lamw = sb("lamw", [1, 8])
    nlam_col = sb("nlam_col", [128, 1])
    xgT = sb("xgT", [128, 8 * NT4 * 128], BF16)
    q1T = sb("q1T", [128, 8 * NT4 * 128], BF16)
    q2T = sb("q2T", [128, 8 * NT4 * 128], BF16)
    k1T = sb("k1T", [128, NT4 * 128], BF16)
    k2T = sb("k2T", [128, NT4 * 128], BF16)
    vb4 = sb("vb4", [128, NT4 * 128], BF16)
    Wq1b = sb("Wq1b", [128, 8 * D], BF16)
    Wq2b = sb("Wq2b", [128, 8 * D], BF16)
    Wk1b = sb("Wk1b", [128, 8 * 128], BF16)
    Wk2b = sb("Wk2b", [128, 8 * 128], BF16)
    Wvb = sb("Wvb", [128, 8 * 128], BF16)
    Woutb = sb("Woutb", [128, 8 * D], BF16)
    xt4 = sb("xt4s", [128, 2 * D])
    qbuf = sb("qbuf4", [128, D])
    qr4 = sb("qr4", [128, D])
    qrb = sb("qrb4", [128, D], BF16)
    tA = sb("tA4", [128, 512]); tB = sb("tB4", [128, 512])
    tbl_c = sb("tbl_c4", [128, 512]); tbl_s = sb("tbl_s4", [128, 512])
    P4 = sb("P4", [128, 256], BF16)
    PT4 = sb("PT4", [128, 256], BF16)
    o2s = sb("o2s", [128, 64]); o1s = sb("o1s", [128, 64])
    o_acc = sb("o_acc", [128, D])
    o_bf = sb("o_bf4", [128, D], BF16)
    oT4 = sb("oT4", [128, 8 * 128], BF16)
    acol = sb("acol4", [128, 8])
    xr_t = sb("xr_t4", [128, D])
    xo_t = sb("xo_t4", [128, D])
    lnsc = sb("lnsc4", [128, D]); lncol = sb("lncol4", [128, 12])
    padv = sb("padv", [128, 96]); pads = sb("pads", [128, 32])

    if ctx is None:
        psA = nc.alloc_psum_tensor("psA4", [128, 512], F32)
        psB = nc.alloc_psum_tensor("psB4", [128, 512], F32)
        psT = nc.alloc_psum_tensor("psT4", [128, 1024], BF16)
        psS = nc.alloc_psum_tensor("psS4", [128, 256], F32)
        psO = nc.alloc_psum_tensor("psO4", [128, 128], F32)
    else:
        psA, psB, psT = ctx["psA"], ctx["psB"], ctx["psT"]
        psS, psO = ctx["ps_sc"], ctx["ps_o"]

    with (nc.Block() if ctx is None else contextlib.nullcontext()) as block:
        p = P(nc) if ctx is None else ctx["p"]

        def vpad():
            p.on("v", lambda v: v.memset(padv[:], 0.0), mark="v")

        def spad():
            p.on("s", lambda s_: s_.activation(out=pads[:],
                                               in_=padv[:, 0:pads.shape[1]],
                                               func=AF.Square), mark="s")
        cols = dict(sum=lncol[:, 0:1], ssq=lncol[:, 1:2], mean=lncol[:, 2:3],
                    nmean=lncol[:, 3:4], ex2=lncol[:, 4:5], nvar=lncol[:, 5:6],
                    lnv=lncol[:, 6:7], rstd=lncol[:, 7:8], nmr=lncol[:, 8:9])
        last_rd = ({id(b): ("v", 0) for b in (psA, psB, psT, psS, psO)}
                   if ctx is None else ctx["last_rd"])

        def psum_guard(bank):
            p.wt("t", last_rd[id(bank)][0], last_rd[id(bank)][1])

        def set_rd(bank, eng):
            last_rd[id(bank)] = (eng, p.ct[eng])

        p.dma("g", lambda g: g.dma_start(out=ident[:], in_=ident_in[:, :]))
        p.dma("g", lambda g: g.dma_start(out=ones_bf[:], in_=ones_in[:, :]))
        p.dma("g", lambda g: g.dma_start(out=bout_bf[:], in_=boutrow[:, :]))
        for dst, src in [(Wq1b, Wq1r), (Wq2b, Wq2r), (Wk1b, Wk1r),
                         (Wk2b, Wk2r), (Wvb, Wvr), (Woutb, Woutr)]:
            p.dma("g", lambda g, d=dst, s_=src: g.dma_start(out=d[:],
                                                            in_=s_[:, :]))
        p.dma("y", lambda y: y.dma_start(out=ones1[0:1, :], in_=ones_in[:, :]))
        p.dma("y", lambda y: y.dma_start(out=lqk_sb[:], in_=lqk[:, :]))
        p.wt("t", "ds"); p.wt("t", "dg")
        psum_guard(psB)
        p.on("t", lambda t: t.matmul(out=psB[0:1, 0:1],
                                     lhsT=lqk_sb[:, 0:1], rhs=lqk_sb[:, 1:2],
                                     start=True, stop=True))
        p.on("t", lambda t: t.matmul(out=psB[0:1, 1:2],
                                     lhsT=lqk_sb[:, 2:3], rhs=lqk_sb[:, 3:4],
                                     start=True, stop=True), mark="pe")
        p.wt("s", "pe")
        p.on("s", lambda s: s.activation(out=lamw[0:1, 0:2],
                                         in_=psB[0:1, 0:2], func=AF.Exp),
             mark="s")
        set_rd(psB, "s")
        p.wt("v", "s")
        p.on("v", lambda v: v.tensor_tensor(out=lamw[0:1, 2:3],
                                            in0=lamw[0:1, 0:1],
                                            in1=lamw[0:1, 1:2],
                                            op=ALU.subtract), mark="v")
        p.wt("v", "v")
        p.on("v", lambda v: v.tensor_scalar_add(out=lamw[0:1, 3:4],
                                                in0=lamw[0:1, 2:3],
                                                scalar1=0.8), mark="v")
        p.wt("v", "v")
        p.on("v", lambda v: v.tensor_scalar_max(out=lamw[0:1, 4:5],
                                                in0=lamw[0:1, 3:4],
                                                scalar1=0.1), mark="v")
        p.wt("v", "v")
        p.on("v", lambda v: v.tensor_scalar_min(out=lamw[0:1, 5:6],
                                                in0=lamw[0:1, 4:5],
                                                scalar1=0.9), mark="v")
        p.wt("v", "v")
        p.on("v", lambda v: v.tensor_scalar_mul(out=lamw[0:1, 6:7],
                                                in0=lamw[0:1, 5:6],
                                                scalar1=-1.0), mark="v")
        p.wt("t", "v")
        psum_guard(psB)
        p.on("t", lambda t: t.matmul(out=psB[:, 0:1], lhsT=ones1[:, :],
                                     rhs=lamw[0:1, 6:7], start=True,
                                     stop=True), mark="pe")
        p.wt("v", "pe")
        p.on("v", lambda v: v.tensor_copy(out=nlam_col[:], in_=psB[:, 0:1]),
             mark="v")
        set_rd(psB, "v")

        if ctx is not None:
            def ldwin(g, part):
                cid = _gp_cid(nc, ctx)
                if part == 0:
                    return g.dma_start(out=xg_bf[0:1024, :],
                                       in_=ctx["x3g"][bass.ts(cid, 1024), :])
                return g.dma_start(
                    out=xg_bf[1024:1280, :],
                    in_=ctx["x3g"][bass.ts(4 * cid + 4, 256), :])
            p.dma("g", lambda g: ldwin(g, 0))
            p.dma("g", lambda g: ldwin(g, 1))
        for T in range(NT4):
            slx = slice((T % 2) * D, (T % 2) * D + D)
            p.wt("y", "v")
            if ctx is None:
                p.dma("y", lambda y, T=T, sl=slx: y.dma_start(
                    out=xt4[:, sl], in_=xg[T * 128:(T + 1) * 128, :]))
            else:
                p.wt("g", "v")
                p.dma("g", lambda g, T=T, sl=slx: g.dma_start(
                    out=xt4[:, sl], in_=xg_bf[T * 128:(T + 1) * 128, :]))
                p.wt("v", "dg")
            p.wt("v", "ds"); p.wt("v", "pe")
            p.on("v", lambda v, sl=slx: v.tensor_copy(out=qr4[:],
                                                      in_=xt4[:, sl]))
            p.on("v", lambda v: v.tensor_copy(out=qrb[:], in_=qr4[:]),
                 mark="v")
            p.wt("t", "v")
            psum_guard(psT)
            for j in range(8):
                p.on("t", lambda t, j=j: t.transpose(
                    out=psT[:, j * 128:(j + 1) * 128],
                    in_=qrb[:, j * 128:(j + 1) * 128], identity=ident[:]),
                    mark="pe" if j == 7 else None)
            p.wt("s", "pe")
            for j in range(8):
                p.on("s", lambda s, j=j, T=T: s.copy(
                    out=xgT[:, j * NT4 * 128 + T * 128:
                            j * NT4 * 128 + (T + 1) * 128],
                    in_=psT[:, j * 128:(j + 1) * 128]), mark="s")
            set_rd(psT, "s")

        def rope4(T, dstT, tq, nheads):
            wide = nheads * DH
            p.wt("y", "v")
            p.dma("y", lambda y, T=T, tq=tq: y.dma_start(
                out=tbl_c[:], in_=(t4cq if tq else t4c)[T * 128:
                                                        (T + 1) * 128, :]))
            p.dma("y", lambda y, T=T, tq=tq: y.dma_start(
                out=tbl_s[:], in_=(t4sq if tq else t4s)[T * 128:
                                                        (T + 1) * 128, :]))
            p.wt("v", "ds")
            qe = qbuf[:, 0:wide].rearrange("p (h d two) -> p h d two",
                                           two=2, d=32)[:, :, :, 0]
            qo = qbuf[:, 0:wide].rearrange("p (h d two) -> p h d two",
                                           two=2, d=32)[:, :, :, 1]
            cr = tbl_c[:, 0:nheads * 32].rearrange("p (h d) -> p h d", d=32)
            sr = tbl_s[:, 0:nheads * 32].rearrange("p (h d) -> p h d", d=32)
            tAv = tA[:, 0:nheads * 32].rearrange("p (h d) -> p h d", d=32)
            tBv = tB[:, 0:nheads * 32].rearrange("p (h d) -> p h d", d=32)
            qre = qr4[:, 0:wide].rearrange("p (h s d) -> p h s d",
                                           s=2, d=32)[:, :, 0]
            qro = qr4[:, 0:wide].rearrange("p (h s d) -> p h s d",
                                           s=2, d=32)[:, :, 1]
            p.on("v", lambda v: v.tensor_tensor(out=tAv, in0=qe, in1=cr,
                                                op=ALU.mult))
            p.on("v", lambda v: v.tensor_tensor(out=tBv, in0=qo, in1=sr,
                                                op=ALU.mult), mark="v")
            p.on("v", lambda v: v.tensor_tensor(out=qre, in0=tAv, in1=tBv,
                                                op=ALU.subtract), mark="v")
            p.on("v", lambda v: v.tensor_tensor(out=tAv, in0=qe, in1=sr,
                                                op=ALU.mult))
            p.on("v", lambda v: v.tensor_tensor(out=tBv, in0=qo, in1=cr,
                                                op=ALU.mult), mark="v")
            p.on("v", lambda v: v.tensor_tensor(out=qro, in0=tAv, in1=tBv,
                                                op=ALU.add))
            p.on("v", lambda v: v.tensor_copy(out=qrb[:, 0:wide],
                                              in_=qr4[:, 0:wide]), mark="v")
            p.wt("t", "v")
            psum_guard(psT)
            nj = wide // 128
            for j in range(nj):
                p.on("t", lambda t, j=j: t.transpose(
                    out=psT[:, j * 128:(j + 1) * 128],
                    in_=qrb[:, j * 128:(j + 1) * 128], identity=ident[:]),
                    mark="pe" if j == nj - 1 else None)
            p.wt("s", "pe")
            for j in range(nj):
                p.on("s", lambda s, j=j, T=T, dstT=dstT: s.copy(
                    out=dstT[:, j * NT4 * 128 + T * 128:
                            j * NT4 * 128 + (T + 1) * 128],
                    in_=psT[:, j * 128:(j + 1) * 128]), mark="s")
            set_rd(psT, "s")

        for T in range(NT4):
            for qi, (Wb, dstT) in enumerate(((Wq1b, q1T), (Wq2b, q2T))):
                for nb, bank in ((0, psA), (1, psB)):
                    psum_guard(bank)
                    p.wt("t", "s")
                    for k in range(8):
                        p.on("t", lambda t, k=k, T=T, nb=nb, bank=bank,
                             Wb=Wb: t.matmul(
                                 out=bank[:],
                                 lhsT=xgT[:, k * NT4 * 128 + T * 128:
                                          k * NT4 * 128 + (T + 1) * 128],
                                 rhs=Wb[:, k * D + nb * 512:
                                        k * D + nb * 512 + 512],
                                 start=(k == 0), stop=(k == 7)),
                             mark="pe" if k == 7 else None)
                p.wt("v", "pe"); p.wt("v", "s")
                p.on("v", lambda v: v.tensor_copy(out=qbuf[:, 0:512],
                                                  in_=psA[:]))
                p.on("v", lambda v: v.tensor_copy(out=qbuf[:, 512:1024],
                                                  in_=psB[:]), mark="v")
                set_rd(psA, "v"); set_rd(psB, "v")
                rope4(T, dstT, 1, H4)
            for src, dstT, tq in ((Wk1b, k1T, 0), (Wk2b, k2T, 0),
                                  (Wvb, vb4, -1)):
                psum_guard(psO)
                p.wt("t", "s"); p.wt("t", "v")
                for k in range(8):
                    p.on("t", lambda t, k=k, T=T, src=src: t.matmul(
                        out=psO[:, 0:128],
                        lhsT=xgT[:, k * NT4 * 128 + T * 128:
                                 k * NT4 * 128 + (T + 1) * 128],
                        rhs=src[:, k * 128:(k + 1) * 128],
                        start=(k == 0), stop=(k == 7)),
                        mark="pe" if k == 7 else None)
                p.wt("v", "pe"); p.wt("v", "s")
                if tq == -1:
                    p.on("v", lambda v, T=T: v.tensor_copy(
                        out=vb4[:, T * 128:(T + 1) * 128],
                        in_=psO[:, 0:128]), mark="v")
                    set_rd(psO, "v")
                else:
                    p.on("v", lambda v: v.tensor_copy(
                        out=qbuf[:, 0:128], in_=psO[:, 0:128]), mark="v")
                    set_rd(psO, "v")
                    rope4(T, dstT, tq, G4)

        for wl in range(8):
            for qt in range(2):
                Ti = wl + qt
                Tq = 2 * wl + qt
                for h in range(H4):
                    g = h % G4
                    for br, (qTx, kTx) in enumerate(((q1T, k1T),
                                                     (q2T, k2T))):
                        pt = h // 2
                        pb = (h % 2) * 64
                        psum_guard(psS)
                        p.wt("t", "s"); p.wt("t", "v")
                        p.on("t", lambda t, Ti=Ti, wl=wl, pt=pt, pb=pb,
                             qTx=qTx, kTx=kTx, g=g: t.matmul(
                                 out=psS[:, 0:256],
                                 lhsT=qTx[pb:pb + 64,
                                          pt * NT4 * 128 + Ti * 128:
                                          pt * NT4 * 128 + (Ti + 1) * 128],
                                 rhs=kTx[64 * g:64 * g + 64,
                                         wl * 128:wl * 128 + 256],
                                 start=True, stop=True), mark="pe")
                        p.wt("v", "pe"); p.wt("v", "s")
                        p.on("v", lambda v, br=br: v.reduce_max(
                            out=acol[:, br:br + 1], in_=psS[:, 0:256],
                            axis=AX, negate=True), mark="v")
                        p.wt("s", "v")
                        p.on("s", lambda s, br=br: s.activation(
                            out=P4[:], in_=psS[:, 0:256], func=AF.Exp,
                            bias=acol[:, br:br + 1],
                            accum_out=acol[:, 2 + br:3 + br]), mark="s")
                        set_rd(psS, "s")
                        p.wt("v", "s")
                        p.on("v", lambda v, br=br: v.reciprocal(
                            out=acol[:, 4 + br:5 + br],
                            in_=acol[:, 2 + br:3 + br]), mark="v")
                        p.wt("t", "s")
                        psum_guard(psT)
                        for j in range(2):
                            p.on("t", lambda t, j=j: t.transpose(
                                out=psT[:, j * 128:(j + 1) * 128],
                                in_=P4[:, j * 128:(j + 1) * 128],
                                identity=ident[:]),
                                mark="pe" if j == 1 else None)
                        p.wt("s", "pe")
                        p.on("s", lambda s: s.copy(out=PT4[:],
                                                   in_=psT[:, 0:256]),
                             mark="s")
                        set_rd(psT, "s")
                        psum_guard(psO)
                        p.wt("t", "s")
                        for j in range(2):
                            p.on("t", lambda t, j=j, wl=wl, g=g: t.matmul(
                                out=psO[:, 0:64],
                                lhsT=PT4[:, j * 128:(j + 1) * 128],
                                rhs=vb4[:, (wl + j) * 128 + 64 * g:
                                        (wl + j) * 128 + 64 * g + 64],
                                start=(j == 0), stop=(j == 1)),
                                mark="pe" if j == 1 else None)
                        p.wt("v", "pe"); p.wt("v", "v")
                        dst = o1s if br == 0 else o2s
                        p.on("v", lambda v, br=br, dst=dst: v.tensor_scalar(
                            out=dst[:], in0=psO[:, 0:64],
                            scalar1=acol[:, 4 + br:5 + br], scalar2=None,
                            op0=ALU.mult), mark="v")
                        set_rd(psO, "v")
                    p.wt("v", "v")
                    p.on("v", lambda v, h=h: v.scalar_tensor_tensor(
                        out=o_acc[:, h * DH:(h + 1) * DH], in0=o2s[:],
                        scalar=nlam_col[:, 0:1], in1=o1s[:],
                        op0=ALU.mult, op1=ALU.add), mark="v")
                p.on("v", lambda v: v.tensor_copy(out=o_bf[:], in_=o_acc[:]),
                     mark="v")
                p.wt("t", "v")
                psum_guard(psT)
                for j in range(8):
                    p.on("t", lambda t, j=j: t.transpose(
                        out=psT[:, j * 128:(j + 1) * 128],
                        in_=o_bf[:, j * 128:(j + 1) * 128],
                        identity=ident[:]), mark="pe" if j == 7 else None)
                p.wt("s", "pe")
                for j in range(8):
                    p.on("s", lambda s, j=j: s.copy(
                        out=oT4[:, j * 128:(j + 1) * 128],
                        in_=psT[:, j * 128:(j + 1) * 128]), mark="s")
                set_rd(psT, "s")
                p.wt("y", "v"); p.wt("y", "s")
                p.wt("y", "ds")
                p.dma("y", lambda y, Tq=Tq: y.dma_start(
                    out=xr_t[:], in_=xr3[Tq * 128:(Tq + 1) * 128, :]))
                for nb, bank in ((0, psA), (1, psB)):
                    psum_guard(bank)
                    p.wt("t", "s")
                    p.on("t", lambda t, nb=nb, bank=bank: t.matmul(
                        out=bank[:], lhsT=ones_bf[:, :],
                        rhs=bout_bf[0:1, nb * 512:nb * 512 + 512],
                        start=True, stop=False))
                    for j in range(8):
                        p.on("t", lambda t, j=j, nb=nb, bank=bank: t.matmul(
                            out=bank[:],
                            lhsT=oT4[:, j * 128:(j + 1) * 128],
                            rhs=Woutb[:, j * D + nb * 512:
                                      j * D + nb * 512 + 512],
                            start=False, stop=(j == 7)),
                            mark="pe" if j == 7 else None)
                p.wt("v", "pe"); p.wt("v", "ds"); p.wt("v", "s")
                for nb, bank in ((0, psA), (1, psB)):
                    p.on("v", lambda v, nb=nb, bank=bank: v.tensor_tensor(
                        out=xo_t[:, nb * 512:nb * 512 + 512], in0=bank[:],
                        in1=xr_t[:, nb * 512:nb * 512 + 512], op=ALU.add),
                        mark="v")
                    set_rd(bank, "v")
                ln_stats(p, xo_t[:], lnsc[:], cols, D)
                p.on("v", lambda v: v.tensor_scalar(
                    out=xr_t[:], in0=xo_t[:], scalar1=cols["rstd"],
                    scalar2=cols["nmr"], op0=ALU.mult, op1=ALU.add),
                    mark="v")
                vpad(); p.wt("y", "v")
                p.dma("y", lambda y, Tq=Tq: y.dma_start(
                    out=x4[Tq * 128:(Tq + 1) * 128, :], in_=xr_t[:]))

        p.on("v", lambda v: v.memset(lncol[:, 9:10], 0.0), mark="v")
        p.on("v", lambda v: v.memset(lncol[:, 9:10], 0.0), mark="v")
        p.on("s", lambda s_: s_.activation(out=lncol[:, 10:11],
                                           in_=lncol[:, 9:10],
                                           func=AF.Square), mark="s")
        p.on("s", lambda s_: s_.activation(out=lncol[:, 11:12],
                                           in_=lncol[:, 9:10],
                                           func=AF.Square), mark="s")
        if ctx is None:
            p.wt("y", "v"); p.wt("y", "s")
            p.wt("y", "ds")
            p.wt("y", "dg")
            p.replay(block)
            for g in p.guards:
                g.__exit__(None, None, None)
    for g in reversed(_stage_guards):
        g.__exit__(None, None, None)
    return nc


def _stage4_inputs(x3_full, w4):
    f32 = np.float32
    r8 = lambda w, n: np.ascontiguousarray(
        np.asarray(w, f32).reshape(8, 128, n).transpose(1, 0, 2)
        .reshape(128, 8 * n))
    common = dict(
        Wq1r=r8(w4["Wq1"], D), Wq2r=r8(w4["Wq2"], D),
        Wk1r=r8(w4["Wk1"], 128), Wk2r=r8(w4["Wk2"], 128),
        Wvr=r8(w4["Wv"], 128), Woutr=r8(w4["Wout"], D),
        boutrow=np.asarray(w4["bout"], f32).reshape(1, D),
        lqk=np.ascontiguousarray(np.stack(
            [np.asarray(w4[k], f32) for k in ("lq1", "lk1", "lq2", "lk2")],
            axis=1)),
        ident=np.eye(128).astype(BF),
        ones1=np.ones((1, 128), f32))
    div = np.exp(np.arange(0, DH, 2, dtype=np.float64) *
                 (-np.log(10000.0) / DH))
    maps = []
    for c in range(NCORE):
        lo = c * 1024
        pos = np.arange(lo, lo + NT4 * 128, dtype=np.float64)
        sin = np.sin(pos[:, None] * div)
        cos = np.cos(pos[:, None] * div)
        t16 = lambda a: np.ascontiguousarray(
            np.tile(a, (1, H4)).astype(f32))
        m = dict(common)
        m.update(xg=np.ascontiguousarray(x3_full[lo:lo + NT4 * 128]),
                 xr3=np.ascontiguousarray(x3_full[c * S:(c + 1) * S]),
                 t4c=t16(cos), t4s=t16(sin),
                 t4cq=t16(cos * 0.125), t4sq=t16(sin * 0.125))
        maps.append({k: np.ascontiguousarray(v) if k != "ident" else v
                     for k, v in m.items()})
    return maps


def run_stage4(x3_full, w4):
    nc = build_stage4()
    maps = _stage4_inputs(x3_full, w4)
    res = run_bass_kernel_spmd(nc, maps, core_ids=list(range(NCORE)))
    return np.concatenate([np.asarray(r["x4"]) for r in res.results], axis=0)

S1_KEYS = [("xs", [S, D]), ("xh", [128, D]), ("hsc", [128, 1]),
           ("lnw", [1, D]), ("lnb", [1, D]), ("Win", [128, 8 * 2 * DI]),
           ("convw", [128, NM * 4]), ("convb", [128, NM]),
           ("Wxp", [128, NM * 96]), ("Wdt", [DTR, DI]), ("bdt", [128, NM]),
           ("Alog", [128, NM * NST]), ("Dvec", [128, NM]),
           ("Wout", [128, NM * D])]
LL_KEYS = [("wqkv_r", [128, 8 * 3 * INNER]), ("qs_row", [1, INNER]),
           ("ks_row", [1, INNER]), ("wo_r", [128, 4 * D]),
           ("lng", [1, D]), ("lnb2", [1, D]),
           ("w1a_r", [MC, 128, 8 * 128]), ("w1g_r", [MC, 128, 8 * 128]),
           ("b1a_c", [128, MC]), ("b1g_c", [128, MC]),
           ("w2_r", [128, MC * D]), ("b2row", [1, D])]
LL_SHARED = [("tq_cs", [EXTT, INNER]), ("tq_ss", [EXTT, INNER]),
             ("tk_cs", [EXTT, INNER]), ("tk_ss", [EXTT, INNER]),
             ("msk", [5, 128, 640]), ("hsc2", [1, 1]), ("hscH", [128, 1])]
S4_KEYS = [("Wq1r", [128, 8 * D]), ("Wq2r", [128, 8 * D]),
           ("Wk1r", [128, 8 * 128]), ("Wk2r", [128, 8 * 128]),
           ("Wvr", [128, 8 * 128]), ("Woutr", [128, 8 * D]),
           ("boutrow", [1, D]), ("lqk", [DTR, 4]),
           ("t4c", [NT4 * 128, 512]), ("t4s", [NT4 * 128, 512]),
           ("t4cq", [NT4 * 128, 512]), ("t4sq", [NT4 * 128, 512])]


def build_fused():
    nc = bass.Bass("TRN2", target_bir_lowering=False, debug=False,
                   num_devices=NCORE)
    din = lambda nm, sh, dt=F32: nc.dram_tensor(nm, sh, dt,
                                                kind="ExternalInput").ap()
    ident_ap = nc.dram_tensor("ident", [128, 128], BF16,
                              kind="ExternalInput").ap()
    ones_ap = din("ones1", [1, 128])
    io_s1 = {k: din("s1_" + k, sh) for k, sh in S1_KEYS}
    io_s1.update(ident=ident_ap, ones1=ones_ap)
    shared_ll = {k: din(k, sh) for k, sh in LL_SHARED}
    io_ll = {}
    for pfx in ("l0a_", "l0b_", "l1a_", "l1b_"):
        d = {k: din(pfx + k, sh) for k, sh in LL_KEYS}
        d.update(shared_ll)
        d.update(ident=ident_ap, ones1=ones_ap)
        io_ll[pfx] = d
    io_s4 = {k: din("s4_" + k, sh) for k, sh in S4_KEYS}
    io_s4.update(ident=ident_ap, ones1=ones_ap)
    out = nc.dram_tensor("out", [S, D], F32, kind="ExternalOutput").ap()

    x1_d = nc.dram_tensor("x1_d", [S, D], F32)
    t1_d = nc.dram_tensor("t1_d", [S, D], F32)
    x2_d = nc.dram_tensor("x2_d", [S, D], F32)
    x3_d = nc.dram_tensor("x3_d", [S, D], F32)
    t2_d = nc.dram_tensor("t2_d", [S, D], F32)
    hals = []
    for i in range(4):
        hin = nc.dram_tensor(f"hal{i}_in", [1280, D], BF16)
        hout = nc.dram_tensor(f"hal{i}_out", [NCORE * 1280, D], BF16,
                              addr_space="Shared")
        hals.append((hin, hout))
    x3g_in = nc.dram_tensor("x3g_in", [S, D], BF16)
    x3g_out = nc.dram_tensor("x3g_out", [NCORE * S, D], BF16,
                             addr_space="Shared")

    psA = nc.alloc_psum_tensor("psA", [128, 512], F32)
    psB = nc.alloc_psum_tensor("psB", [128, 512], F32)
    psC = nc.alloc_psum_tensor("psC", [128, 512], F32)
    psO = nc.alloc_psum_tensor("psO", [128, 512], F32)
    psT = nc.alloc_psum_tensor("psT", [128, 1024], BF16)
    ps_sc = nc.alloc_psum_tensor("ps_sc", [128, 640], F32)

    with nc.Block() as block:
        nc.cache_partition_id()
        p = P(nc)
        last_rd = {id(b): ("v", 0) for b in (psA, psB, psC, psO, psT, ps_sc)}
        base = dict(nc=nc, p=p, last_rd=last_rd, psA=psA, psB=psB, psC=psC,
                    psO=psO, ps_o=psO, psT=psT, ps_sc=ps_sc, regcache={})

        def join():
            for e in ("t", "v", "s", "g", "y"):
                for sem in ("pe", "v", "s", "ds", "dg", "c"):
                    p.wt(e, sem)

        def halo_ex(src_d, pair):
            hin, hout = pair
            p.wt("g", "ds"); p.wt("g", "v"); p.wt("g", "s")
            p.dma("g", lambda g: g.dma_start(out=hin[:, :],
                                             in_=src_d[S - 1280:S, :]))
            p.wt("g", "dg")
            p.on("g", lambda g: g.collective_compute(
                "AllGather", ALU.bypass, replica_groups=[list(range(NCORE))],
                ins=[hin[:, :]], outs=[hout[:, :]]), mark="c")
            p.wt("g", "c"); p.wt("y", "c")

        build_stage1(ctx=dict(base, tag="s1_", io=io_s1, x_out=x1_d))
        join()
        halo_ex(x1_d, hals[0])
        build_local_layer(False, ctx=dict(base, tag="l0a_", io=io_ll["l0a_"],
                                          x_in=x1_d, hal=hals[0][1],
                                          x_out=t1_d))
        join()
        halo_ex(t1_d, hals[1])
        build_local_layer(True, ctx=dict(base, tag="l0b_", io=io_ll["l0b_"],
                                         x_in=t1_d, hal=hals[1][1],
                                         xres=x1_d, x_out=x2_d))
        join()
        p.wt("g", "ds"); p.wt("g", "v"); p.wt("g", "s")
        p.dma("g", lambda g: g.dma_start(out=x3g_in[:, :], in_=x2_d[:, :]))
        p.wt("g", "dg")
        p.on("g", lambda g: g.collective_compute(
            "AllGather", ALU.bypass, replica_groups=[list(range(NCORE))],
            ins=[x3g_in[:, :]], outs=[x3g_out[:, :]]), mark="c")
        p.wt("g", "c"); p.wt("y", "c")
        build_stage4(ctx=dict(base, tag="s4_", io=io_s4, x_in=x2_d,
                              x3g=x3g_out, x_out=x3_d))
        join()
        halo_ex(x3_d, hals[2])
        build_local_layer(False, ctx=dict(base, tag="l1a_", io=io_ll["l1a_"],
                                          x_in=x3_d, hal=hals[2][1],
                                          x_out=t2_d))
        join()
        halo_ex(t2_d, hals[3])
        build_local_layer(True, ctx=dict(base, tag="l1b_", io=io_ll["l1b_"],
                                         x_in=t2_d, hal=hals[3][1],
                                         xres=x3_d, x_out=out))
        join()
        p.replay(block)
        for g in p.guards:
            g.__exit__(None, None, None)
    return nc


def _fused_inputs(inputs):
    x = np.ascontiguousarray(np.asarray(inputs["L0_attn_x"], np.float32)[0])
    w1m = {k[8:]: inputs[k] for k in inputs
           if k.startswith("L0_attn_") and k != "L0_attn_x"}
    l0f = {k[7:]: np.asarray(inputs[k]) for k in inputs
           if k.startswith("L0_ffn_")}
    w4 = {k[8:]: np.asarray(inputs[k]) for k in inputs
          if k.startswith("L1_attn_")}
    l1f = {k[7:]: np.asarray(inputs[k]) for k in inputs
           if k.startswith("L1_ffn_")}
    lsel = lambda lf, i: {k: v[i] for k, v in lf.items()}
    s1 = _stage1_inputs(x, w1m)
    zeros = np.zeros((T_FULL, D), np.float32)
    lls = {pfx: _layer_inputs(zeros, lw, False)
           for pfx, lw in (("l0a_", lsel(l0f, 0)), ("l0b_", lsel(l0f, 1)),
                           ("l1a_", lsel(l1f, 0)), ("l1b_", lsel(l1f, 1)))}
    s4 = _stage4_inputs(zeros, w4)
    shared_keys = {k for k, _ in LL_SHARED}
    maps = []
    for c in range(NCORE):
        m = {}
        for k, v in s1[c].items():
            if k in ("ident", "ones1"):
                m[k] = v
            else:
                m["s1_" + k] = v
        for pfx in ("l0a_", "l0b_", "l1a_", "l1b_"):
            for k, v in lls[pfx][c].items():
                if k in ("ident", "ones1", "xw", "xres"):
                    continue
                if k in shared_keys:
                    m[k] = v
                else:
                    m[pfx + k] = v
        m["hscH"] = np.full((128, 1), 0.0 if c == 0 else 1.0, np.float32)
        for k, v in s4[c].items():
            if k in ("ident", "ones1", "xg", "xr3"):
                continue
            m["s4_" + k] = v
        maps.append(m)
    return maps


_FUSED_NC = []


def kernel_fused(**inputs):
    if not _FUSED_NC:
        _FUSED_NC.append(build_fused())
    nc = _FUSED_NC[0]
    maps = _fused_inputs(inputs)
    res = run_bass_kernel_spmd(nc, maps, core_ids=list(range(NCORE)))
    outp = np.concatenate([np.asarray(r["out"]) for r in res.results], axis=0)
    return outp.reshape(1, T_FULL, D)


# version 08
# speedup vs baseline: 1.0950x; 1.0950x over previous
import contextlib
import numpy as np
import ml_dtypes
import concourse.bass as bass
import concourse.mybir as mybir
from concourse.bass_utils import run_bass_kernel_spmd

F32 = mybir.dt.float32
BF16 = mybir.dt.bfloat16
AF = mybir.ActivationFunctionType
ALU = mybir.AluOpType
AX = mybir.AxisListType.X
BF = ml_dtypes.bfloat16
ET = mybir.EngineType

NCORE = 8
T_FULL = 16384
S = 2048
D = 1024
DI = 2048
NST = 16
DTR = 64
TC = 512
NCH = S // TC
NM = DI // 128
EPS = 1e-5
DEBUG = False
DEBUG2 = False
SER_A1 = False
SER_DELTA = False
SER_A2 = False
SER_B = False


class P:

    ENGS = ("t", "v", "s", "g", "y")

    def __init__(self, nc):
        self.nc = nc
        self.th = {e: [] for e in P.ENGS}
        self.sem = {}
        self.ct = {}
        self.guards = []
        for n in ("dg", "ds", "pe", "v", "s", "c"):
            g = nc.semaphore(n)
            self.sem[n] = g.__enter__()
            self.guards.append(g)
            self.ct[n] = 0

    def on(self, e, fn, mark=None, k=1):
        if mark is not None:
            self.ct[mark] += k
        sem = self.sem.get(mark) if mark is not None else None

        def thunk(eng, fn=fn, sem=sem, k=k):
            ins = fn(eng)
            if sem is not None:
                ins.then_inc(sem, k)
        self.th[e].append(thunk)
        return self.ct[mark] if mark is not None else None

    def dma(self, e, fn):
        return self.on(e, fn, mark="ds" if e == "y" else "dg", k=16)

    def wt(self, e, name, val=None):
        v = self.ct[name] if val is None else val
        sem = self.sem[name]
        self.th[e].append(lambda eng, sem=sem, v=v: eng.wait_ge(sem, v))

    def replay(self, block):
        for e, deco in (("y", block.sync), ("g", block.gpsimd),
                        ("t", block.tensor), ("v", block.vector),
                        ("s", block.scalar)):
            def runner(eng, e=e):
                for thunk in self.th[e]:
                    thunk(eng)
            deco(runner)


def ln_stats(p, xt_ap, scratch, cols, width, eps=EPS):
    p.wt("s", "v")
    p.on("v", lambda v: v.reduce_sum(out=cols["sum"], in_=xt_ap, axis=AX),
         mark="v")
    p.on("s", lambda s: s.activation(out=scratch, in_=xt_ap, func=AF.Square,
                                     accum_out=cols["ssq"]), mark="s")
    p.wt("v", "s")
    p.wt("v", "v")
    p.on("v", lambda v: v.tensor_scalar_mul(out=cols["mean"], in0=cols["sum"],
                                            scalar1=1.0 / width))
    p.on("v", lambda v: v.tensor_scalar_mul(out=cols["nmean"],
                                            in0=cols["sum"],
                                            scalar1=-1.0 / width))
    p.on("v", lambda v: v.tensor_scalar(out=cols["ex2"], in0=cols["ssq"],
                                        scalar1=1.0 / width, scalar2=eps,
                                        op0=ALU.mult, op1=ALU.add), mark="v")
    p.wt("v", "v")
    p.on("v", lambda v: v.scalar_tensor_tensor(
        out=cols["nvar"], in0=cols["mean"], scalar=cols["mean"],
        in1=cols["ex2"], op0=ALU.mult, op1=ALU.subtract), mark="v")
    p.wt("s", "v")
    p.on("s", lambda s: s.activation(out=cols["lnv"], in_=cols["nvar"],
                                     func=AF.Ln, scale=-1.0))
    p.on("s", lambda s: s.activation(out=cols["rstd"], in_=cols["lnv"],
                                     func=AF.Exp, scale=-0.5), mark="s")
    p.wt("v", "s")
    p.on("v", lambda v: v.tensor_scalar(
        out=cols["nmr"], in0=cols["nmean"], scalar1=cols["rstd"],
        scalar2=None, op0=ALU.mult), mark="v")
    p.wt("v", "v")



def _gp_prev(nc, ctx):
    rc = ctx.get("regcache")
    if rc is None:
        rc = {}
        ctx["regcache"] = rc
    if "prev" not in rc:
        cid = nc.partition_id([ET.Pool])
        rc["cid"] = cid
        rc["prev"] = (cid + 7) % 8
    return rc["prev"]


def _gp_cid(nc, ctx):
    _gp_prev(nc, ctx)
    return ctx["regcache"]["cid"]


def build_stage1(ctx=None):
    tag = "" if ctx is None else ctx["tag"]
    if ctx is None:
        nc = bass.Bass("TRN2", target_bir_lowering=False, debug=False,
                       num_devices=NCORE)
        din = lambda nm, sh, dt=F32: nc.dram_tensor(nm, sh, dt,
                                                    kind="ExternalInput").ap()
    else:
        nc = ctx["nc"]
        din = lambda nm, sh, dt=F32: ctx["io"][nm]
    xs = din("xs", [S, D])
    xh = din("xh", [128, D])
    hsc = din("hsc", [128, 1])
    lnw = din("lnw", [1, D]); lnb = din("lnb", [1, D])
    Win = din("Win", [128, 8 * 2 * DI])
    convw = din("convw", [128, NM * 4]); convb = din("convb", [128, NM])
    Wxp = din("Wxp", [128, NM * 96])
    Wdt = din("Wdt", [DTR, DI]); bdt = din("bdt", [128, NM])
    Alog = din("Alog", [128, NM * NST]); Dvec = din("Dvec", [128, NM])
    Wout = din("Wout", [128, NM * D])
    if ctx is None:
        ident_in = nc.dram_tensor("ident", [128, 128], BF16,
                                  kind="ExternalInput").ap()
        ones_in = din("ones1", [1, 128])
        x1 = nc.dram_tensor("x1", [S, D], F32, kind="ExternalOutput").ap()
        x1b = nc.dram_tensor("x1b", [S, D], F32,
                             kind="ExternalOutput").ap()
        xr_d = nc.dram_tensor("xr_d", [S, D], F32,
                              kind="ExternalOutput").ap()
        yf_d = nc.dram_tensor("yf_d", [NCH * 128, NM * TC], BF16,
                              kind="ExternalOutput").ap()
    else:
        ident_in = ctx["io"]["ident"]
        ones_in = ctx["io"]["ones1"]
        x1 = ctx["x_out"]
        x1b = nc.dram_tensor(tag + "x1b", [S, D], F32)
        xr_d = nc.dram_tensor(tag + "xr_d", [S, D], F32)
        yf_d = nc.dram_tensor(tag + "yf_d", [NCH * 128, NM * TC], BF16)
    dbg = {}
    if DEBUG:
        for nm, sh, dt in [("dbg_normT", [128, 8 * TC], BF16),
                           ("dbg_u", [128, NM * TC], BF16),
                           ("dbg_dT", [128, NM * TC], BF16),
                           ("dbg_dt", [DTR, TC], BF16),
                           ("dbg_BC", [32, TC], F32),
                           ("dbg_hfin", [128, 256], F32),
                           ("dbg_decay", [128, 256], F32),
                           ("dbg_hstate", [128, 256], F32),
                           ("dbg_ball", [128, NST * TC], BF16),
                           ("dbg_Bbc", [128, NST * TC], BF16),
                           ("dbg_aall", [128, NST * TC], BF16),
                           ("dbg_hall", [128, NST * TC], BF16),
                           ("dbg_yfin", [128, NM * TC], BF16),
                           ("dbg_hs4", [NCH, 128, NM * NST], F32),
                           ("dbg_yf4", [NCH, 128, NM * TC], BF16),
                           ("dbg_xr", [128, 4 * D], F32)]:
            dbg[nm] = nc.dram_tensor(nm, sh, dt, kind="ExternalOutput").ap()

    uT_sp = nc.dram_tensor(tag + "uT_sp", [128, NM, S], BF16)
    szT_sp = nc.dram_tensor(tag + "szT_sp", [128, NM, S], BF16)
    dT_sp = nc.dram_tensor(tag + "dT_sp", [128, NM, S], BF16)
    b_sp = nc.dram_tensor(tag + "b_sp", [NCH, NM, 128, NST * TC], BF16)
    BC_d = nc.dram_tensor(tag + "BC_d", [32, S], F32)
    cc_in = nc.dram_tensor(tag + "cc_in", [256, NM * NST], F32)
    cc_out = nc.dram_tensor(tag + "cc_out", [256 * NCORE, NM * NST], F32,
                            addr_space="Shared")
    g_bounce = nc.dram_tensor(tag + "g_bounce", [128, NCORE * 256], F32)

    _stage_guards = []

    def sb(nm, sh, dt=F32):
        g = nc.sbuf_tensor(tag + nm, sh, dt)
        h = g.__enter__()
        _stage_guards.append(g)
        return h
    ident = sb("identb", [128, 128], BF16)
    ones1 = sb("ones1b", [1, 128])
    cw_sb = sb("cw_sb", [128, NM * 4]); cb_sb = sb("cb_sb", [128, NM])
    bdt_sb = sb("bdt_sb", [128, NM]); D_sb = sb("D_sb", [128, NM])
    hsc_sb = sb("hsc_sb", [128, 1])
    negA = sb("negA", [128, NM * NST]); A_sb = sb("A_sb", [128, NM * NST])
    AdT = sb("AdT", [128, NM * NST])
    dcum_start = sb("dcum_start", [128, NCH * NM])
    dc_carry = sb("dc_carry", [128, NM])
    hfin = sb("hfin", [128, NM * NST]); decay = sb("decay", [128, NM * NST])
    hacc = sb("hacc", [128, NM * NST]); h_state = sb("h_state", [128, NM * NST])
    lnsc = sb("lnsc", [128, D]); lncol = sb("lncol", [128, 12])
    padv = sb("padv", [128, 384]); pads = sb("pads", [128, 96])
    ones_tc = sb("ones_tc", [128, TC])
    dcum_f = sb("dcum_f", [128, 2 * TC])
    bcrow = sb("bcrow", [1, 2 * TC])

    if ctx is None:
        ps0 = nc.alloc_psum_tensor("ps0", [128, 512], F32)
        ps1 = nc.alloc_psum_tensor("ps1", [128, 512], F32)
        psT = nc.alloc_psum_tensor("psT", [128, 1024], BF16)
        ps2 = nc.alloc_psum_tensor("ps2", [128, 512], F32)
        ps3 = nc.alloc_psum_tensor("ps3", [128, 512], F32)
    else:
        ps0, ps1, ps2, ps3, psT = (ctx["psA"], ctx["psB"], ctx["psC"],
                                   ctx["psO"], ctx["psT"])
    psmm = [ps0, ps1]

    with (nc.Block() if ctx is None else contextlib.nullcontext()) as block:
        p = P(nc) if ctx is None else ctx["p"]

        def vpad():
            p.on("v", lambda v: v.memset(padv[:], 0.0), mark="v")

        def spad():
            p.on("s", lambda s_: s_.activation(out=pads[:],
                                               in_=padv[:, 0:pads.shape[1]],
                                               func=AF.Square), mark="s")
        cols = dict(sum=lncol[:, 0:1], ssq=lncol[:, 1:2], mean=lncol[:, 2:3],
                    nmean=lncol[:, 3:4], ex2=lncol[:, 4:5], nvar=lncol[:, 5:6],
                    lnv=lncol[:, 6:7], rstd=lncol[:, 7:8], nmr=lncol[:, 8:9])
        last_rd = ({id(b): ("v", 0) for b in (ps0, ps1, psT, ps2, ps3)}
                   if ctx is None else ctx["last_rd"])

        def psum_guard(bank):
            p.wt("t", last_rd[id(bank)][0], last_rd[id(bank)][1])

        def set_rd(bank, eng):
            last_rd[id(bank)] = (eng, p.ct[eng])

        def ln_apply_tile(xt_ap, out_bf):
            p.wt("v", "pe")
            p.on("v", lambda v: v.tensor_scalar(
                out=tnorm[:], in0=xt_ap, scalar1=cols["rstd"],
                scalar2=cols["nmr"], op0=ALU.mult, op1=ALU.add), mark="v")
            p.on("v", lambda v: v.tensor_tensor(out=lnsc[:], in0=tnorm[:],
                                                in1=lnw_b[:], op=ALU.mult))
            p.on("v", lambda v, o=out_bf: v.tensor_tensor(
                out=o, in0=lnsc[:], in1=lnb_b[:], op=ALU.add), mark="v")

        def transposes(src_bf, nj=8):
            p.wt("t", "v")
            psum_guard(psT)
            for j in range(nj):
                p.on("t", lambda t, j=j: t.transpose(
                    out=psT[:, j * 128:(j + 1) * 128],
                    in_=src_bf[:, j * 128:(j + 1) * 128], identity=ident[:]),
                    mark="pe" if j == nj - 1 else None)
            p.wt("s", "pe")

        with contextlib.ExitStack() as a1st:
            a1sb = lambda nm, sh, dt=F32: a1st.enter_context(
                nc.sbuf_tensor(tag + nm, sh, dt))
            Win_bf = a1sb("Win_bf", [128, 8 * 2 * DI], BF16)
            lnw_b = a1sb("lnw_b", [128, D])
            lnb_b = a1sb("lnb_b", [128, D])
            normT = a1sb("normT", [128, 8 * TC], BF16)
            arena = a1sb("arena", [128, NM * 515], BF16)
            uT_bf = a1sb("uT_bf", [128, NM * TC], BF16)
            dT_bf = a1sb("dT_bf", [128, NM * TC], BF16)
            Wxp_bf = a1sb("Wxp_bf", [128, NM * 96], BF16)
            Wdt_bf = a1sb("Wdt_bf", [DTR, DI], BF16)
            dt_bf = a1sb("dt_bf", [DTR, S], BF16)
            BC_f = a1st.enter_context(nc.sbuf_tensor(tag + "BC_f", [32, S], F32))
            xt = a1st.enter_context(nc.sbuf_tensor(tag + "xt", [128, 2 * D], F32))
            tnorm = a1st.enter_context(nc.sbuf_tensor(tag + "tnorm", [128, D], F32))
            norm_bf = a1sb("norm_bf", [128, D], BF16)
            hnT = a1sb("hnT", [128, 8 * 3], BF16)
            halo_sav = a1st.enter_context(
                nc.sbuf_tensor(tag + "halo_sav", [128, NM * 3], F32))
            cbuf0 = a1st.enter_context(
                nc.sbuf_tensor(tag + "cbuf0", [128, 2 * TC], F32))
            cbuf1 = a1st.enter_context(
                nc.sbuf_tensor(tag + "cbuf1", [128, 2 * TC], F32))
            spl0 = a1st.enter_context(
                nc.sbuf_tensor(tag + "spl0", [128, 2 * TC], F32))
            spl1 = a1st.enter_context(
                nc.sbuf_tensor(tag + "spl1", [128, 2 * TC], F32))
            spl2 = a1st.enter_context(nc.sbuf_tensor(tag + "spl2", [128, TC], F32))
            sz_small = a1sb("sz_small", [128, 2 * TC], BF16)

            p.dma("g", lambda g: g.dma_start(out=Win_bf[:], in_=Win[:, :]))
            p.dma("g", lambda g: g.dma_start(out=Wxp_bf[:], in_=Wxp[:, :]))
            p.dma("g", lambda g: g.dma_start(out=Wdt_bf[:], in_=Wdt[:, :]))
            p.dma("g", lambda g: g.dma_start(out=ident[:], in_=ident_in[:, :]))
            for dst, src in [(cw_sb, convw), (cb_sb, convb), (bdt_sb, bdt),
                             (D_sb, Dvec), (hsc_sb, hsc), (ones1, ones_in),
                             (AdT, Alog)]:
                p.dma("y", lambda y, d=dst, s_=src: y.dma_start(out=d[:],
                                                                in_=s_[:, :]))
            p.dma("y", lambda y: y.dma_start(out=lnw_b[0:1, :], in_=lnw[:, :]))
            p.dma("y", lambda y: y.dma_start(out=lnb_b[0:1, :], in_=lnb[:, :]))
            p.wt("v", "ds")
            p.on("v", lambda v: v.memset(ones_tc[:], 1.0))
            p.on("v", lambda v: v.memset(h_state[:], 0.0))
            p.on("v", lambda v: v.memset(hfin[:], 0.0))
            p.on("v", lambda v: v.memset(dc_carry[:], 0.0), mark="v")
            p.wt("s", "ds")
            p.on("s", lambda s: s.activation(out=negA[:], in_=AdT[:],
                                             func=AF.Exp), mark="s")
            p.wt("v", "s")
            p.on("v", lambda v: v.tensor_scalar_mul(out=A_sb[:], in0=negA[:],
                                                    scalar1=-1.0), mark="v")
            p.wt("t", "ds")
            for dstb in (lnw_b, lnb_b):
                for half in range(2):
                    sl = slice(half * 512, (half + 1) * 512)
                    psum_guard(ps2)
                    p.on("t", lambda t, d=dstb, sl=sl: t.matmul(
                        out=ps2[:], lhsT=ones1[:, :], rhs=d[0:1, sl],
                        start=True, stop=True), mark="pe")
                    p.wt("v", "pe")
                    p.on("v", lambda v, d=dstb, sl=sl: v.tensor_copy(
                        out=d[:, sl], in_=ps2[:]), mark="v")
                    set_rd(ps2, "v")

            p.dma("y", lambda y: y.dma_start(out=xt[:, 0:D], in_=xh[:, :]))
            p.wt("v", "ds"); p.wt("s", "ds")
            ln_stats(p, xt[:, 0:D], lnsc[:], cols, D)
            ln_apply_tile(xt[:, 0:D], norm_bf[:])
            transposes(norm_bf)
            for j in range(8):
                p.on("s", lambda s, j=j: s.copy(
                    out=hnT[:, j * 3:(j + 1) * 3],
                    in_=psT[:, j * 128 + 125:j * 128 + 128]), mark="s")
            set_rd(psT, "s")
            p.wt("t", "dg"); p.wt("t", "s")
            for m in range(NM):
                psum_guard(ps2)
                for k in range(8):
                    p.on("t", lambda t, m=m, k=k: t.matmul(
                        out=ps2[:, 0:3],
                        lhsT=Win_bf[:, k * 2 * DI + m * 128:
                                    k * 2 * DI + (m + 1) * 128],
                        rhs=hnT[:, k * 3:(k + 1) * 3],
                        start=(k == 0), stop=(k == 7)),
                        mark="pe" if k == 7 else None)
                p.wt("v", "pe")
                p.on("v", lambda v, m=m: v.tensor_scalar(
                    out=halo_sav[:, m * 3:(m + 1) * 3], in0=ps2[:, 0:3],
                    scalar1=hsc_sb[:, 0:1], scalar2=None, op0=ALU.mult),
                    mark="v")
                set_rd(ps2, "v")

            s_after_silu = {}
            for ck in range(NCH):
                p.wt("v", "ds"); p.wt("s", "ds")
                p.wt("v", "pe"); p.wt("s", "pe")
                for m in range(NM):
                    p.on("v", lambda v, m=m: v.tensor_copy(
                        out=arena[:, m * 515:m * 515 + 3],
                        in_=halo_sav[:, m * 3:(m + 1) * 3]),
                        mark="v" if m == NM - 1 else None)
                for kt in range(4):
                    tile = 4 * ck + kt
                    sl_x = slice((kt % 2) * D, (kt % 2) * D + D)
                    p.wt("y", "v"); p.wt("y", "s")
                    p.dma("y", lambda y, t_=tile, sl=sl_x: y.dma_start(
                        out=xt[:, sl], in_=xs[t_ * 128:(t_ + 1) * 128, :]))
                    p.wt("v", "ds"); p.wt("s", "ds")
                    ln_stats(p, xt[:, sl_x], lnsc[:], cols, D)
                    ln_apply_tile(xt[:, sl_x], norm_bf[:])
                    transposes(norm_bf)
                    for j in range(8):
                        p.on("s", lambda s, j=j, kt=kt: s.copy(
                            out=normT[:, j * TC + kt * 128:
                                      j * TC + kt * 128 + 128],
                            in_=psT[:, j * 128:(j + 1) * 128]), mark="s")
                    set_rd(psT, "s")
                p.wt("t", "s")
                for m in range(32):
                    bank = psmm[m % 2]
                    psum_guard(bank)
                    if SER_A1:
                        p.wt("t", "s")
                    for k in range(8):
                        p.on("t", lambda t, m=m, k=k, b=bank: t.matmul(
                            out=b[:],
                            lhsT=Win_bf[:, k * 2 * DI + m * 128:
                                        k * 2 * DI + (m + 1) * 128],
                            rhs=normT[:, k * TC:(k + 1) * TC],
                            start=(k == 0), stop=(k == 7)),
                            mark="pe" if k == 7 else None)
                    if m < NM:
                        gm1 = ck * NM + m
                        sm1 = m % 2
                        c0 = cbuf0[:, sm1 * TC:(sm1 + 1) * TC]
                        c1 = cbuf1[:, sm1 * TC:(sm1 + 1) * TC]
                        p.wt("v", "pe")
                        if gm1 >= 2:
                            p.wt("v", "s", s_after_silu[gm1 - 2])
                        p.on("v", lambda v, m=m, b=bank: v.tensor_copy(
                            out=arena[:, m * 515 + 3:m * 515 + 515],
                            in_=b[:]), mark="v")
                        set_rd(bank, "v")
                        p.on("v", lambda v, m=m, c0=c0: v.tensor_scalar(
                            out=c0, in0=arena[:, m * 515:m * 515 + TC],
                            scalar1=cw_sb[:, m * 4:m * 4 + 1], scalar2=None,
                            op0=ALU.mult))
                        for j in range(1, 4):
                            si = c0 if j % 2 == 1 else c1
                            so = c1 if j % 2 == 1 else c0
                            p.on("v", lambda v, m=m, j=j, si=si, so=so:
                                 v.scalar_tensor_tensor(
                                     out=so,
                                     in0=arena[:, m * 515 + j:
                                               m * 515 + j + TC],
                                     scalar=cw_sb[:, m * 4 + j:m * 4 + j + 1],
                                     in1=si, op0=ALU.mult, op1=ALU.add),
                                 mark="v" if j == 3 else None)
                        p.on("v", lambda v, m=m: v.tensor_copy(
                            out=halo_sav[:, m * 3:(m + 1) * 3],
                            in_=arena[:, m * 515 + 512:m * 515 + 515]),
                            mark="v")
                        p.wt("s", "v")
                        p.on("s", lambda s, m=m, c1=c1: s.activation(
                            out=uT_bf[:, m * TC:(m + 1) * TC], in_=c1,
                            func=AF.Silu, bias=cb_sb[:, m:m + 1]), mark="s")
                        s_after_silu[gm1] = p.ct["s"]
                        spad(); p.wt("y", "s")
                        p.dma("y", lambda y, m=m, ck=ck: y.dma_start(
                            out=uT_sp[:, m, ck * TC:(ck + 1) * TC],
                            in_=uT_bf[:, m * TC:(m + 1) * TC]))
                    else:
                        mz = m - NM
                        slz = slice((mz % 2) * TC, (mz % 2) * TC + TC)
                        p.wt("s", "pe")
                        p.wt("s", "ds")
                        p.on("s", lambda s, b=bank, sl=slz: s.activation(
                            out=sz_small[:, sl], in_=b[:], func=AF.Silu),
                            mark="s")
                        set_rd(bank, "s")
                        spad(); p.wt("y", "s")
                        p.dma("y", lambda y, mz=mz, ck=ck, sl=slz:
                              y.dma_start(
                                  out=szT_sp[:, mz, ck * TC:(ck + 1) * TC],
                                  in_=sz_small[:, sl]))
                psum_guard(ps2)
                p.wt("t", "s")
                for m in range(NM):
                    p.on("t", lambda t, m=m: t.matmul(
                        out=ps2[0:96, :],
                        lhsT=Wxp_bf[:, m * 96:(m + 1) * 96],
                        rhs=uT_bf[:, m * TC:(m + 1) * TC],
                        start=(m == 0), stop=(m == NM - 1)),
                        mark="pe" if m == NM - 1 else None)
                p.wt("v", "pe")
                p.on("v", lambda v, ck=ck: v.tensor_copy(
                    out=dt_bf[:, ck * TC:(ck + 1) * TC], in_=ps2[0:64, :]))
                p.on("v", lambda v, ck=ck: v.tensor_copy(
                    out=BC_f[:, ck * TC:(ck + 1) * TC], in_=ps2[64:96, :]),
                    mark="v")
                set_rd(ps2, "v")
                vpad(); p.wt("y", "v")
                p.dma("y", lambda y, ck=ck: y.dma_start(
                    out=BC_d[:, ck * TC:(ck + 1) * TC],
                    in_=BC_f[:, ck * TC:(ck + 1) * TC]))
                p.wt("t", "v")
                v_after_bias = {}
                sc_after_m = {}
                for m in range(NM):
                    sm = m % 2
                    bank = ps3 if sm == 0 else ps2
                    sp0 = spl0[:, sm * TC:(sm + 1) * TC]
                    sp1 = spl1[:, sm * TC:(sm + 1) * TC]
                    psum_guard(bank)
                    if SER_DELTA:
                        p.wt("t", "v"); p.wt("t", "s")
                    p.on("t", lambda t, m=m, ck=ck, b=bank: t.matmul(
                        out=b[:], lhsT=Wdt_bf[:, m * 128:(m + 1) * 128],
                        rhs=dt_bf[:, ck * TC:(ck + 1) * TC],
                        start=True, stop=True), mark="pe")
                    p.wt("v", "pe")
                    if SER_DELTA:
                        p.wt("v", "s")
                    elif m >= 2:
                        p.wt("v", "s", sc_after_m[m - 2])
                    p.on("v", lambda v, m=m, b=bank, o=sp0: v.tensor_scalar(
                        out=o, in0=b[:], scalar1=bdt_sb[:, m:m + 1],
                        scalar2=None, op0=ALU.add), mark="v")
                    v_after_bias[m] = p.ct["v"]
                    set_rd(bank, "v")
                    p.wt("s", "v", v_after_bias[m])
                    p.on("s", lambda s, i=sp0, o=sp1: s.activation(
                        out=o, in_=i, func=AF.Abs))
                    p.on("s", lambda s, i=sp1: s.activation(
                        out=spl2[:], in_=i, func=AF.Exp, scale=-1.0))
                    p.on("s", lambda s, o=sp1: s.activation(
                        out=o, in_=spl2[:], func=AF.Ln, bias=1.0),
                         mark="s")
                    sc_after_m[m] = p.ct["s"]
                    p.wt("v", "s")
                    p.on("v", lambda v, i=sp0: v.tensor_scalar_max(
                        out=cbuf0[:, 0:TC], in0=i, scalar1=0.0))
                    p.on("v", lambda v, m=m, i=sp1: v.tensor_tensor(
                        out=dT_bf[:, m * TC:(m + 1) * TC],
                        in0=cbuf0[:, 0:TC],
                        in1=i, op=ALU.add), mark="v")
                    p.on("v", lambda v, m=m, ck=ck: v.tensor_copy(
                        out=dcum_start[:, ck * NM + m:ck * NM + m + 1],
                        in_=dc_carry[:, m:m + 1]))
                    p.on("v", lambda v, m=m: v.reduce_sum(
                        out=dcum_f[:, 0:1], in_=dT_bf[:, m * TC:(m + 1) * TC],
                        axis=AX), mark="v")
                    p.wt("v", "v")
                    p.on("v", lambda v, m=m: v.tensor_tensor(
                        out=dcum_f[:, 1:2], in0=dc_carry[:, m:m + 1],
                        in1=dcum_f[:, 0:1], op=ALU.add), mark="v")
                    p.wt("v", "v")
                    p.on("v", lambda v, m=m: v.tensor_copy(
                        out=dc_carry[:, m:m + 1], in_=dcum_f[:, 1:2]),
                        mark="v")
                    p.wt("v", "v")
                    vpad(); p.wt("y", "v")
                    p.dma("y", lambda y, m=m, ck=ck: y.dma_start(
                        out=dT_sp[:, m, ck * TC:(ck + 1) * TC],
                        in_=dT_bf[:, m * TC:(m + 1) * TC]))
                if DEBUG and ck == 0:
                    p.wt("y", "v"); p.wt("y", "s")
                    p.dma("y", lambda y: y.dma_start(out=dbg["dbg_normT"][:, :],
                                                     in_=normT[:]))
                    p.dma("y", lambda y: y.dma_start(out=dbg["dbg_u"][:, :],
                                                     in_=uT_bf[:]))
                    p.dma("y", lambda y: y.dma_start(out=dbg["dbg_dT"][:, :],
                                                     in_=dT_bf[:]))
                    p.dma("y", lambda y: y.dma_start(out=dbg["dbg_dt"][:, :],
                                                     in_=dt_bf[:, 0:TC]))
                    p.dma("y", lambda y: y.dma_start(out=dbg["dbg_BC"][:, :],
                                                     in_=BC_f[:, 0:TC]))

        p.wt("v", "v")
        for m in range(NM):
            p.on("v", lambda v, m=m: v.tensor_scalar(
                out=AdT[:, m * NST:(m + 1) * NST],
                in0=A_sb[:, m * NST:(m + 1) * NST],
                scalar1=dc_carry[:, m:m + 1], scalar2=None, op0=ALU.mult),
                mark="v")
        p.wt("s", "v")
        for m in range(NM):
            p.on("s", lambda s, m=m: s.activation(
                out=decay[:, m * NST:(m + 1) * NST],
                in_=A_sb[:, m * NST:(m + 1) * NST], func=AF.Exp,
                scale=dc_carry[:, m:m + 1]), mark="s")

        with contextlib.ExitStack() as a2st:
            a2sb = lambda nm, sh, dt=BF16: a2st.enter_context(
                nc.sbuf_tensor(tag + nm, sh, dt))
            Bbc = a2sb("Bbc", [128, NST * TC])
            b_all = a2sb("b_all", [128, 2 * NST * TC])
            du_bf = a2sb("du_bf", [128, 2 * TC])
            w_bf = a2sb("w_bf", [128, 2 * NST * TC])
            sttout = a2sb("sttout", [128, TC])
            uT_rd = a2sb("uT_rd", [128, 2 * TC])
            dT_rd = a2sb("dT_rd", [128, 2 * TC])
            v_after_du = {}
            sc_after_exp = {}
            v_after_stt = {}
            ds_after_load = {}
            ds_after_spill = {}
            for ck in range(NCH):
                for n in range(NST):
                    slb = slice((n % 2) * TC, (n % 2) * TC + TC)
                    p.wt("y", "pe")
                    p.dma("y", lambda y, n=n, ck=ck, sl=slb: y.dma_start(
                        out=bcrow[0:1, sl],
                        in_=BC_d[n:n + 1, ck * TC:(ck + 1) * TC]))
                    bank = ps2 if n % 2 == 0 else ps3
                    psum_guard(bank)
                    p.wt("t", "ds")
                    p.on("t", lambda t, b=bank, sl=slb: t.matmul(
                        out=b[:], lhsT=ones1[:, :], rhs=bcrow[0:1, sl],
                        start=True, stop=True), mark="pe")
                    p.wt("v", "pe")
                    p.on("v", lambda v, n=n, b=bank: v.tensor_copy(
                        out=Bbc[:, n * TC:(n + 1) * TC], in_=b[:]), mark="v")
                    set_rd(bank, "v")
                for m in range(NM):
                    gm = ck * NM + m
                    slm = slice((m % 2) * TC, (m % 2) * TC + TC)
                    wsl = (gm % 2) * NST * TC
                    bsl = (gm % 2) * NST * TC
                    if SER_A2:
                        p.wt("y", "v")
                    elif gm >= 2:
                        p.wt("y", "v", v_after_du[gm - 2])
                    p.dma("y", lambda y, m=m, ck=ck, sl=slm: y.dma_start(
                        out=uT_rd[:, sl],
                        in_=uT_sp[:, m, ck * TC:(ck + 1) * TC]))
                    p.dma("y", lambda y, m=m, ck=ck, sl=slm: y.dma_start(
                        out=dT_rd[:, sl],
                        in_=dT_sp[:, m, ck * TC:(ck + 1) * TC]))
                    ds_after_load[gm] = p.ct["ds"]
                    if SER_A2:
                        p.wt("v", "ds"); p.wt("v", "s")
                    else:
                        p.wt("v", "ds", ds_after_load[gm])
                        if gm >= 2:
                            p.wt("v", "s", sc_after_exp[gm - 2])
                    p.on("v", lambda v, m=m, ck=ck, sl=slm:
                         v.tensor_tensor_scan(
                             out=dcum_f[:, sl], data0=ones_tc[:],
                             data1=dT_rd[:, sl],
                             initial=dcum_start[:, ck * NM + m:
                                                ck * NM + m + 1],
                             op0=ALU.mult, op1=ALU.add))
                    p.on("v", lambda v, sl=slm: v.tensor_tensor(
                        out=du_bf[:, sl], in0=dT_rd[:, sl],
                        in1=uT_rd[:, sl], op=ALU.mult), mark="v")
                    v_after_du[gm] = p.ct["v"]
                    if SER_A2:
                        p.wt("s", "v")
                    else:
                        p.wt("s", "v", v_after_du[gm])
                        if gm >= 2:
                            p.wt("s", "v", v_after_stt[gm - 2])
                    for n in range(NST):
                        mn = m * NST + n
                        p.on("s", lambda s, mn=mn, sl=slm, n=n, w0=wsl:
                             s.activation(
                                 out=w_bf[:, w0 + n * TC:w0 + (n + 1) * TC],
                                 in_=dcum_f[:, sl],
                                 func=AF.Exp, scale=negA[:, mn:mn + 1],
                                 bias=AdT[:, mn:mn + 1]),
                             mark="s" if n == NST - 1 else None)
                    sc_after_exp[gm] = p.ct["s"]
                    if SER_A2:
                        p.wt("v", "s"); p.wt("v", "ds")
                    else:
                        p.wt("v", "s", sc_after_exp[gm])
                        if gm >= 2:
                            p.wt("v", "ds", ds_after_spill[gm - 2])
                    for n in range(NST):
                        mn = m * NST + n
                        p.on("v", lambda v, n=n, sl=slm, b0=bsl:
                             v.tensor_tensor(
                                 out=b_all[:, b0 + n * TC:b0 + (n + 1) * TC],
                                 in0=du_bf[:, sl],
                                 in1=Bbc[:, n * TC:(n + 1) * TC],
                                 op=ALU.mult))
                        p.on("v", lambda v, mn=mn, n=n, w0=wsl, b0=bsl:
                             v.scalar_tensor_tensor(
                                 out=sttout[:], in0=w_bf[:, w0 + n * TC:
                                                         w0 + (n + 1) * TC],
                                 scalar=1.0,
                                 in1=b_all[:, b0 + n * TC:b0 + (n + 1) * TC],
                                 op0=ALU.mult, op1=ALU.mult,
                                 accum_out=hacc[:, mn:mn + 1]),
                             mark="v" if n == NST - 1 else None)
                    v_after_stt[gm] = p.ct["v"]
                    vpad(); p.wt("y", "v")
                    p.dma("y", lambda y, m=m, ck=ck, b0=bsl: y.dma_start(
                        out=b_sp[ck, m, :, :],
                        in_=b_all[:, b0:b0 + NST * TC]))
                    ds_after_spill[gm] = p.ct["ds"]
                    if DEBUG and ck == 0 and m == 0:
                        p.dma("y", lambda y, b0=bsl: y.dma_start(
                            out=dbg["dbg_ball"][:, :],
                            in_=b_all[:, b0:b0 + NST * TC]))
                        p.dma("y", lambda y: y.dma_start(
                            out=dbg["dbg_Bbc"][:, :], in_=Bbc[:]))
                p.wt("v", "v")
                p.on("v", lambda v: v.tensor_tensor(
                    out=hfin[:], in0=hfin[:], in1=hacc[:], op=ALU.add),
                    mark="v")

        p.on("v", lambda v: v.tensor_copy(out=hacc[:], in_=hfin[:]), mark="v")
        p.wt("v", "v")
        p.on("v", lambda v: v.tensor_copy(out=AdT[:], in_=hacc[:]), mark="v")
        p.wt("y", "v"); p.wt("y", "s")
        p.dma("y", lambda y: y.dma_start(out=cc_in[0:128, :], in_=decay[:]))
        p.dma("y", lambda y: y.dma_start(out=cc_in[128:256, :], in_=hfin[:]))
        p.wt("g", "ds")
        p.on("g", lambda g: g.collective_compute(
            "AllGather", ALU.bypass, replica_groups=[list(range(NCORE))],
            ins=[cc_in[:, :]], outs=[cc_out[:, :]]), mark="c")
        p.wt("g", "c")
        with contextlib.ExitStack() as cbst:
            dechf = cbst.enter_context(
                nc.sbuf_tensor(tag + "dechf", [128, NCORE * 512], F32))
            gall = cbst.enter_context(
                nc.sbuf_tensor(tag + "gall", [128, NCORE * 256], F32))
            gtmp = cbst.enter_context(
                nc.sbuf_tensor(tag + "gtmp", [128, 256], F32))
            for r in range(NCORE):
                p.dma("g", lambda g, r=r: g.dma_start(
                    out=dechf[:, r * 512:r * 512 + 256],
                    in_=cc_out[256 * r:256 * r + 128, :]))
                p.dma("g", lambda g, r=r: g.dma_start(
                    out=dechf[:, r * 512 + 256:r * 512 + 512],
                    in_=cc_out[256 * r + 128:256 * r + 256, :]))
            p.wt("v", "dg")
            p.on("v", lambda v: v.memset(gall[:, 0:256], 0.0), mark="v")
            for r in range(1, NCORE):
                p.on("v", lambda v, r=r: v.tensor_tensor(
                    out=gtmp[:],
                    in0=dechf[:, (r - 1) * 512:(r - 1) * 512 + 256],
                    in1=gall[:, (r - 1) * 256:r * 256], op=ALU.mult),
                    mark="v")
                p.wt("v", "v")
                p.on("v", lambda v, r=r: v.tensor_tensor(
                    out=gall[:, r * 256:(r + 1) * 256], in0=gtmp[:],
                    in1=dechf[:, (r - 1) * 512 + 256:(r - 1) * 512 + 512],
                    op=ALU.add), mark="v")
                p.wt("v", "v")
            vpad(); p.wt("y", "v")
            p.dma("y", lambda y: y.dma_start(out=g_bounce[:, :], in_=gall[:]))
            p.wt("g", "ds")

            def pick_hstate(g):
                cid = (_gp_cid(nc, ctx) if ctx is not None
                       else nc.partition_id([ET.Pool]))
                return g.dma_start(out=h_state[:, :],
                                   in_=g_bounce[:, bass.ts(cid, 256)])
            p.dma("g", pick_hstate)

        if DEBUG:
            p.wt("y", "dg")
            p.dma("y", lambda y: y.dma_start(out=dbg["dbg_decay"][:, :],
                                             in_=decay[:]))
            p.dma("y", lambda y: y.dma_start(out=dbg["dbg_hfin"][:, :],
                                             in_=hfin[:]))
            p.dma("y", lambda y: y.dma_start(out=dbg["dbg_hstate"][:, :],
                                             in_=h_state[:]))
        with contextlib.ExitStack() as bst:
            bsb = lambda nm, sh, dt=BF16: bst.enter_context(
                nc.sbuf_tensor(tag + nm, sh, dt))
            CbcB = bsb("CbcB", [128, NST * TC])
            b_allB = bsb("b_allB", [128, 2 * NST * TC])
            tmp_allB = bsb("tmp_allB", [128, NST * TC // 2])
            a_allB = bst.enter_context(
                nc.sbuf_tensor(tag + "a_allB", [128, NST * TC], F32))
            h_allB = bst.enter_context(
                nc.sbuf_tensor(tag + "h_allB", [128, NST * TC], F32))
            v_after_yfin = {}
            v_after_scans = {}
            sc_after_aexp = {}
            ds_after_loadB = {}
            ds_after_out = {}
            uT_rdB = bsb("uT_rdB", [128, 2 * TC])
            dT_rdB = bsb("dT_rdB", [128, 2 * TC])
            sz_rdB = bsb("sz_rdB", [128, 2 * TC])
            y_ch = bst.enter_context(nc.sbuf_tensor(tag + "y_chB", [128, TC], F32))
            y_ch2 = bst.enter_context(
                nc.sbuf_tensor(tag + "y_ch2B", [128, TC], F32))
            yfin = bsb("yfinB", [128, NM * TC])
            t1f = bst.enter_context(nc.sbuf_tensor(tag + "t1fB", [128, TC], F32))
            Wo_bf = bsb("Wo_bfB", [128, 2 * 512])
            xt4 = bst.enter_context(nc.sbuf_tensor(tag + "xt4B", [128, 4 * D], F32))
            xr4 = bst.enter_context(nc.sbuf_tensor(tag + "xr4B", [128, 4 * D], F32))
            p.wt("v", "dg")
            for ck in range(NCH):
                for grp, dst in ((1, CbcB),):
                    for n in range(NST):
                        slb = slice((n % 2) * TC, (n % 2) * TC + TC)
                        row = grp * NST + n
                        p.wt("y", "pe")
                        p.dma("y", lambda y, row=row, ck=ck, sl=slb:
                              y.dma_start(
                                  out=bcrow[0:1, sl],
                                  in_=BC_d[row:row + 1,
                                           ck * TC:(ck + 1) * TC]))
                        bank = ps2 if n % 2 == 0 else ps3
                        psum_guard(bank)
                        p.wt("t", "ds")
                        p.on("t", lambda t, b=bank, sl=slb: t.matmul(
                            out=b[:], lhsT=ones1[:, :], rhs=bcrow[0:1, sl],
                            start=True, stop=True), mark="pe")
                        p.wt("v", "pe")
                        p.on("v", lambda v, n=n, b=bank, d=dst:
                             v.tensor_copy(
                                 out=d[:, n * TC:(n + 1) * TC], in_=b[:]),
                             mark="v")
                        set_rd(bank, "v")
                p.wt("v", "pe")
                v_chunk0 = p.ct["v"]
                for m in range(NM):
                    gm = ck * NM + m
                    slm = slice((m % 2) * TC, (m % 2) * TC + TC)
                    b0 = (gm % 2) * NST * TC
                    if SER_B or gm < 2:
                        p.wt("y", "v"); p.wt("y", "s")
                    else:
                        p.wt("y", "v", v_after_yfin[gm - 2])
                        p.wt("y", "s", sc_after_aexp[gm - 2])
                    p.dma("y", lambda y, m=m, ck=ck, sl=slm: y.dma_start(
                        out=uT_rdB[:, sl],
                        in_=uT_sp[:, m, ck * TC:(ck + 1) * TC]))
                    p.dma("y", lambda y, m=m, ck=ck, sl=slm: y.dma_start(
                        out=dT_rdB[:, sl],
                        in_=dT_sp[:, m, ck * TC:(ck + 1) * TC]))
                    p.dma("y", lambda y, m=m, ck=ck, sl=slm: y.dma_start(
                        out=sz_rdB[:, sl],
                        in_=szT_sp[:, m, ck * TC:(ck + 1) * TC]))
                    p.dma("y", lambda y, m=m, ck=ck, b0=b0: y.dma_start(
                        out=b_allB[:, b0:b0 + NST * TC],
                        in_=b_sp[ck, m, :, :]))
                    ds_after_loadB[gm] = p.ct["ds"]
                    if SER_B:
                        p.wt("s", "ds"); p.wt("s", "v")
                    else:
                        p.wt("s", "ds", ds_after_loadB[gm])
                        if gm >= 1:
                            p.wt("s", "v", v_after_scans[gm - 1])
                    for n in range(NST):
                        mn = m * NST + n
                        p.on("s", lambda s, n=n, mn=mn, sl=slm: s.activation(
                            out=a_allB[:, n * TC:(n + 1) * TC],
                            in_=dT_rdB[:, sl], func=AF.Exp,
                            scale=A_sb[:, mn:mn + 1]),
                            mark="s" if n == NST - 1 else None)
                    sc_after_aexp[gm] = p.ct["s"]
                    if SER_B:
                        p.wt("v", "s"); p.wt("v", "ds")
                    else:
                        p.wt("v", "s", sc_after_aexp[gm])
                        p.wt("v", "ds", ds_after_loadB[gm])
                    if m == 0:
                        p.wt("v", "v", v_chunk0)
                    for n in range(NST):
                        mn = m * NST + n
                        p.on("v", lambda v, n=n, mn=mn, b0=b0:
                             v.tensor_tensor_scan(
                                 out=h_allB[:, n * TC:(n + 1) * TC],
                                 data0=a_allB[:, n * TC:(n + 1) * TC],
                                 data1=b_allB[:, b0 + n * TC:
                                              b0 + (n + 1) * TC],
                                 initial=h_state[:, mn:mn + 1],
                                 op0=ALU.mult, op1=ALU.add),
                            mark="v" if n == NST - 1 else None)
                    v_after_scans[gm] = p.ct["v"]
                    p.wt("v", "v")
                    HW = NST * TC // 2
                    for hf in range(2):
                        p.on("v", lambda v, hf=hf: v.tensor_tensor(
                            out=tmp_allB[:],
                            in0=h_allB[:, hf * HW:(hf + 1) * HW],
                            in1=CbcB[:, hf * HW:(hf + 1) * HW],
                            op=ALU.mult))
                        tv = tmp_allB[:, :].rearrange("p (n t) -> p t n",
                                                      n=NST // 2)
                        p.on("v", lambda v, hf=hf, tv=tv: v.reduce_sum(
                            out=(y_ch[:, :] if hf == 0 else y_ch2[:, :]),
                            in_=tv, axis=AX), mark="v")
                        p.wt("v", "v")
                    hv = h_allB[:, :].rearrange("p (n t) -> p n t", n=NST)
                    p.on("v", lambda v, m=m, hv=hv: v.tensor_copy(
                        out=h_state[:, m * NST:(m + 1) * NST],
                        in_=hv[:, :, TC - 1]))
                    p.on("v", lambda v, m=m, sl=slm: v.scalar_tensor_tensor(
                        out=t1f[:], in0=uT_rdB[:, sl],
                        scalar=D_sb[:, m:m + 1],
                        in1=y_ch[:, :],
                        op0=ALU.mult, op1=ALU.add))
                    p.on("v", lambda v: v.tensor_tensor(
                        out=y_ch[:, :], in0=t1f[:], in1=y_ch2[:, :],
                        op=ALU.add))
                    p.on("v", lambda v, m=m, sl=slm: v.tensor_tensor(
                        out=yfin[:, m * TC:(m + 1) * TC],
                        in0=y_ch[:, :],
                        in1=sz_rdB[:, sl], op=ALU.mult), mark="v")
                    v_after_yfin[gm] = p.ct["v"]
                    if DEBUG and ck == 0 and m == 0:
                        p.wt("y", "v")
                        p.dma("g", lambda g: g.dma_start(
                            out=dbg["dbg_aall"][:, :], in_=a_allB[:]))
                        p.dma("g", lambda g: g.dma_start(
                            out=dbg["dbg_hall"][:, :], in_=h_allB[:]))
                if DEBUG and ck == 0:
                    p.wt("y", "v")
                    p.dma("y", lambda y: y.dma_start(
                        out=dbg["dbg_yfin"][:, :], in_=yfin[:]))
                if DEBUG:
                    p.wt("y", "v")
                    p.dma("y", lambda y, ck=ck: y.dma_start(
                        out=dbg["dbg_hs4"][ck, :, :], in_=h_state[:]))
                    p.dma("y", lambda y, ck=ck: y.dma_start(
                        out=dbg["dbg_yf4"][ck, :, :], in_=yfin[:]))
                vpad(); vpad()
                p.wt("y", "v")
                p.dma("y", lambda y, ck=ck: y.dma_start(
                    out=yf_d[ck * 128:(ck + 1) * 128, :], in_=yfin[:]))
                vpad(); vpad(); vpad(); vpad()
                if ck > 0:
                    p.wt("g", "ds", ds_after_out[ck - 1])
                p.wt("g", "v")
                for tb in range(4):
                    tile = ck * 4 + tb
                    p.dma("g", lambda g, t_=tile, tb=tb: g.dma_start(
                        out=xt4[:, tb * D:(tb + 1) * D],
                        in_=xs[t_ * 128:(t_ + 1) * 128, :]))
                pes_after_m = {}
                pe_chunk_start = p.ct["pe"]
                for half in range(2):
                    for m in range(NM):
                        slw = slice((m % 2) * 512, (m % 2) * 512 + 512)
                        if m >= 2:
                            p.wt("g", "pe", pes_after_m[(half, m - 2)])
                        elif half == 1:
                            p.wt("g", "pe", pes_after_m[(0, NM - 2 + m)])
                        else:
                            p.wt("g", "pe", pe_chunk_start)
                        p.dma("g", lambda g, m=m, half=half, sl=slw:
                              g.dma_start(
                                  out=Wo_bf[:, sl],
                                  in_=Wout[:, m * D + half * 512:
                                           m * D + half * 512 + 512]))
                        p.wt("t", "dg")
                        if m == 0:
                            for tb in range(4):
                                bank = [ps0, ps1, ps2, ps3][tb]
                                psum_guard(bank)
                            p.wt("t", "v")
                        for tb in range(4):
                            bank = [ps0, ps1, ps2, ps3][tb]
                            p.on("t", lambda t, m=m, tb=tb, b=bank, sl=slw:
                                 t.matmul(
                                     out=b[:],
                                     lhsT=yfin[:, m * TC + tb * 128:
                                               m * TC + (tb + 1) * 128],
                                     rhs=Wo_bf[:, sl],
                                     start=(m == 0), stop=(m == NM - 1)),
                                 mark="pe" if tb == 3 else None)
                        pes_after_m[(half, m)] = p.ct["pe"]
                    p.wt("v", "pe"); p.wt("v", "ds"); p.wt("v", "s")
                    p.wt("v", "dg")
                    for tb in range(4):
                        bank = [ps0, ps1, ps2, ps3][tb]
                        p.on("v", lambda v, tb=tb, b=bank, half=half:
                             v.scalar_tensor_tensor(
                                 out=xr4[:, tb * D + half * 512:
                                         tb * D + half * 512 + 512],
                                 in0=xt4[:, tb * D + half * 512:
                                         tb * D + half * 512 + 512],
                                 scalar=2.0, in1=b[:],
                                 op0=ALU.mult, op1=ALU.add), mark="v")
                        set_rd(bank, "v")
                if DEBUG and ck == 0:
                    p.wt("y", "v")
                    p.dma("y", lambda y: y.dma_start(
                        out=dbg["dbg_xr"][:, :], in_=xr4[:]))
                for tb in range(4):
                    xap = xr4[:, tb * D:(tb + 1) * D]
                    ln_stats(p, xap, lnsc[:], cols, D)
                    p.wt("v", "ds")
                    p.on("v", lambda v, xap=xap, tb=tb: v.tensor_scalar(
                        out=xt4[:, tb * D:(tb + 1) * D], in0=xap,
                        scalar1=cols["rstd"], scalar2=cols["nmr"],
                        op0=ALU.mult, op1=ALU.add), mark="v")
                vpad(); vpad()
                p.wt("y", "v")
                p.dma("y", lambda y, ck=ck: y.dma_start(
                    out=xr_d[ck * 512:(ck + 1) * 512, :].rearrange(
                        "(t p) c -> p t c", p=128),
                    in_=xr4[:, :].rearrange("p (t c) -> p t c", t=4)))
                p.dma("y", lambda y, ck=ck: y.dma_start(
                    out=x1[ck * 512:(ck + 1) * 512, :].rearrange(
                        "(t p) c -> p t c", p=128),
                    in_=xt4[:, :].rearrange("p (t c) -> p t c", t=4)))
                p.dma("y", lambda y, ck=ck: y.dma_start(
                    out=x1b[ck * 512:(ck + 1) * 512, :].rearrange(
                        "(t p) c -> p t c", p=128),
                    in_=xt4[:, :].rearrange("p (t c) -> p t c", t=4)))
                ds_after_out[ck] = p.ct["ds"]

        p.on("v", lambda v: v.memset(lncol[:, 9:10], 0.0), mark="v")
        p.on("v", lambda v: v.memset(lncol[:, 9:10], 0.0), mark="v")
        p.on("s", lambda s_: s_.activation(out=lncol[:, 10:11],
                                           in_=lncol[:, 9:10],
                                           func=AF.Square), mark="s")
        p.on("s", lambda s_: s_.activation(out=lncol[:, 11:12],
                                           in_=lncol[:, 9:10],
                                           func=AF.Square), mark="s")
        if ctx is None:
            p.wt("y", "v"); p.wt("y", "s")
            p.wt("y", "ds")
            p.wt("y", "dg")
            p.replay(block)
            for g in p.guards:
                g.__exit__(None, None, None)
    for g in reversed(_stage_guards):
        g.__exit__(None, None, None)
    return nc


def _stage1_inputs(x_full, w):
    ident = np.eye(128).astype(BF)
    ones1 = np.ones((1, 128), np.float32)
    conv_w = np.ascontiguousarray(
        w["conv_w"].reshape(DI, 4).reshape(NM, 128, 4).transpose(1, 0, 2)
        .reshape(128, NM * 4))
    r128 = lambda a: np.ascontiguousarray(np.asarray(a).reshape(NM, 128).T)
    Wxp_r = np.ascontiguousarray(
        np.asarray(w["W_xproj"]).reshape(NM, 128, 96).transpose(1, 0, 2)
        .reshape(128, NM * 96))
    Alog_r = np.ascontiguousarray(
        np.asarray(w["A_log"]).reshape(NM, 128, NST).transpose(1, 0, 2)
        .reshape(128, NM * NST))
    Wout_r = np.ascontiguousarray(
        np.asarray(w["W_out"]).reshape(NM, 128, D).transpose(1, 0, 2)
        .reshape(128, NM * D))
    common = dict(
        lnw=np.asarray(w["ln_w"]).reshape(1, D),
        lnb=np.asarray(w["ln_b"]).reshape(1, D),
        Win=np.ascontiguousarray(
            np.asarray(w["W_in"]).reshape(8, 128, 2 * DI).transpose(1, 0, 2)
            .reshape(128, 8 * 2 * DI)), convw=conv_w,
        convb=r128(w["conv_b"]), Wxp=Wxp_r, Wdt=np.asarray(w["W_dt"]),
        bdt=r128(w["b_dt"]), Alog=Alog_r, Dvec=r128(w["D"]), Wout=Wout_r,
        ident=ident, ones1=ones1)
    maps = []
    for c in range(NCORE):
        xsl = x_full[c * S:(c + 1) * S]
        if c == 0:
            xhalo = np.zeros((128, D), np.float32)
            hs = np.zeros((128, 1), np.float32)
        else:
            xhalo = x_full[c * S - 128:c * S]
            hs = np.ones((128, 1), np.float32)
        m = dict(common)
        m.update(xs=xsl, xh=xhalo, hsc=hs)
        maps.append({k: np.ascontiguousarray(np.asarray(v, np.float32))
                     if k != "ident" else v for k, v in m.items()})
    return maps


def run_stage1(x_full, w, full=False):
    nc = build_stage1()
    maps = _stage1_inputs(x_full, w)
    res = run_bass_kernel_spmd(nc, maps, core_ids=list(range(NCORE)))
    if full:
        return res.results
    return np.concatenate([np.asarray(r["x1"]) for r in res.results], axis=0)


USE_FUSED = True


def kernel(**inputs):
    if USE_FUSED:
        return kernel_fused(**inputs)
    x = np.ascontiguousarray(np.asarray(inputs["L0_attn_x"],
                                        np.float32)[0])
    w1m = {k[len("L0_attn_"):]: inputs[k] for k in inputs
           if k.startswith("L0_attn_") and k != "L0_attn_x"}
    l0f = {k[len("L0_ffn_"):]: np.asarray(inputs[k]) for k in inputs
           if k.startswith("L0_ffn_")}
    w4 = {k[len("L1_attn_"):]: np.asarray(inputs[k]) for k in inputs
          if k.startswith("L1_attn_")}
    l1f = {k[len("L1_ffn_"):]: np.asarray(inputs[k]) for k in inputs
           if k.startswith("L1_ffn_")}
    lsel = lambda lf, i: {k: v[i] for k, v in lf.items()}

    x1 = run_stage1(x, w1m)
    t1 = run_local_layer(x1, lsel(l0f, 0), post_ln=False)
    x2 = run_local_layer(t1, lsel(l0f, 1), post_ln=True, xres_full=x1)
    x3 = run_stage4(x2, w4)
    t2 = run_local_layer(x3, lsel(l1f, 0), post_ln=False)
    x4 = run_local_layer(t2, lsel(l1f, 1), post_ln=True, xres_full=x3)
    return x4.reshape(1, T_FULL, D)



EXT = 26
EXTT = EXT * 128
QT0 = 9
KT0 = 2
NQT = EXT - QT0
NKT = EXT - KT0
HEADS = 8
DH = 64
INNER = 512
MC = 22
HID2 = MC * 128


def build_local_layer(post_ln, ctx=None):
    tag = "" if ctx is None else ctx["tag"]
    if ctx is None:
        nc = bass.Bass("TRN2", target_bir_lowering=False, debug=False,
                       num_devices=NCORE)
        din = lambda nm, sh, dt=F32: nc.dram_tensor(nm, sh, dt,
                                                    kind="ExternalInput").ap()
        xw = din("xw", [EXTT, D])
        xres = din("xres", [S, D]) if post_ln else None
    else:
        nc = ctx["nc"]
        din = lambda nm, sh, dt=F32: ctx["io"][nm]
        xw = ctx["x_in"]
        hal = ctx["hal"]
        xres = ctx.get("xres")
    hsc2 = din("hsc2", [1, 1])
    wqkv_r = din("wqkv_r", [128, 8 * 3 * INNER])
    qs_row = din("qs_row", [1, INNER]); ks_row = din("ks_row", [1, INNER])
    wo_r = din("wo_r", [128, 4 * D])
    lng = din("lng", [1, D]); lnb2 = din("lnb2", [1, D])
    w1a_r = din("w1a_r", [MC, 128, 8 * 128])
    w1g_r = din("w1g_r", [MC, 128, 8 * 128])
    b1a_c = din("b1a_c", [128, MC]); b1g_c = din("b1g_c", [128, MC])
    w2_r = din("w2_r", [128, MC * D])
    b2row = din("b2row", [1, D])
    tq_cs = din("tq_cs", [EXTT, INNER]); tq_ss = din("tq_ss", [EXTT, INNER])
    tk_cs = din("tk_cs", [EXTT, INNER]); tk_ss = din("tk_ss", [EXTT, INNER])
    msk = din("msk", [5, 128, 640])
    if ctx is None:
        ident_in = nc.dram_tensor("ident", [128, 128], BF16,
                                  kind="ExternalInput").ap()
        x2 = nc.dram_tensor("x2", [S, D], F32, kind="ExternalOutput").ap()
    else:
        ident_in = ctx["io"]["ident"]
        x2 = ctx["x_out"]
    ones_in = din("ones1", [1, 128])
    xmid_sp = nc.dram_tensor(tag + "xmid_sp", [NQT * 128, D], F32)
    d2 = {}
    if DEBUG2:
        for nm, sh, dt in [("d2_qT", [128, 4 * NQT * 128], BF16),
                           ("d2_kT", [128, 4 * NKT * 128], BF16),
                           ("d2_vbuf", [128, NKT * INNER], BF16),
                           ("d2_P", [128, 640], F32),
                           ("d2_Pm", [128, 640], BF16),
                           ("d2_otok", [128, 512], F32),
                           ("d2_xmid", [NQT * 128, D], F32),
                           ("d2_lnT", [128, 8 * S], BF16),
                           ("d2_xT", [128, 8 * EXTT], BF16),
                           ("d2_wq", [128, 8 * 3 * INNER], BF16),
                           ("d2_qbuf", [128, 512], F32),
                           ("d2_v2", [128, 512], BF16),
                           ("d2_xT2", [128, 8 * EXTT], BF16),
                           ("d2_hh", [128, S], BF16)]:
            d2[nm] = nc.dram_tensor(nm, sh, dt, kind="ExternalOutput").ap()

    _stage_guards = []

    def sb(nm, sh, dt=F32):
        g = nc.sbuf_tensor(tag + nm, sh, dt)
        h = g.__enter__()
        _stage_guards.append(g)
        return h
    ident = sb("identb", [128, 128], BF16)
    ones1 = sb("ones1b", [1, 128])
    ones_bf = sb("ones_bfb", [1, 128], BF16)
    b2_bf = sb("b2_bfb", [1, D], BF16)
    lng_b = sb("lng_b", [128, D]); lnb2_b = sb("lnb2_b", [128, D])
    b1a_sb = sb("b1a_sb", [128, MC]); b1g_sb = sb("b1g_sb", [128, MC])
    hsc2_sb = sb("hsc2_sb", [1, 1])
    lnsc = sb("lnsc2", [128, D]); lncol = sb("lncol2", [128, 12])
    padv = sb("padv", [128, 96]); pads = sb("pads", [128, 32])

    if ctx is None:
        psA = nc.alloc_psum_tensor("psA", [128, 512], F32)
        psB = nc.alloc_psum_tensor("psB", [128, 512], F32)
        psC = nc.alloc_psum_tensor("psC", [128, 512], F32)
        psT = nc.alloc_psum_tensor("psTl", [128, 1024], BF16)
        ps_sc = nc.alloc_psum_tensor("ps_sc", [128, 640], F32)
        ps_o = nc.alloc_psum_tensor("ps_o", [128, 128], F32)
    else:
        psA, psB, psC, psT = (ctx["psA"], ctx["psB"], ctx["psC"], ctx["psT"])
        ps_sc, ps_o = ctx["ps_sc"], ctx["ps_o"]

    with (nc.Block() if ctx is None else contextlib.nullcontext()) as block:
        p = P(nc) if ctx is None else ctx["p"]

        def vpad():
            p.on("v", lambda v: v.memset(padv[:], 0.0), mark="v")

        def spad():
            p.on("s", lambda s_: s_.activation(out=pads[:],
                                               in_=padv[:, 0:pads.shape[1]],
                                               func=AF.Square), mark="s")
        cols = dict(sum=lncol[:, 0:1], ssq=lncol[:, 1:2], mean=lncol[:, 2:3],
                    nmean=lncol[:, 3:4], ex2=lncol[:, 4:5], nvar=lncol[:, 5:6],
                    lnv=lncol[:, 6:7], rstd=lncol[:, 7:8], nmr=lncol[:, 8:9])
        last_rd = ({id(b): ("v", 0) for b in (psA, psB, psC, psT, ps_sc,
                                              ps_o)}
                   if ctx is None else ctx["last_rd"])

        def psum_guard(bank):
            p.wt("t", last_rd[id(bank)][0], last_rd[id(bank)][1])

        def set_rd(bank, eng):
            last_rd[id(bank)] = (eng, p.ct[eng])

        def bcast_row(dst, width):
            for off in range(0, width, 512):
                wdt = min(512, width - off)
                psum_guard(psA)
                p.on("t", lambda t, dst=dst, off=off, wdt=wdt: t.matmul(
                    out=psA[:, 0:wdt], lhsT=ones1[:, :],
                    rhs=dst[0:1, off:off + wdt], start=True, stop=True),
                    mark="pe")
                p.wt("v", "pe")
                p.on("v", lambda v, dst=dst, off=off, wdt=wdt: v.tensor_copy(
                    out=dst[:, off:off + wdt], in_=psA[:, 0:wdt]), mark="v")
                set_rd(psA, "v")

        def hv(buf, half):
            return buf[:, :].rearrange("p (h s d) -> p h s d",
                                       h=HEADS, s=2)[:, :, half]

        p.dma("g", lambda g: g.dma_start(out=ident[:], in_=ident_in[:, :]))
        p.dma("g", lambda g: g.dma_start(out=ones_bf[:], in_=ones_in[:, :]))
        p.dma("g", lambda g: g.dma_start(out=b2_bf[:], in_=b2row[:, :]))
        for dst, src, row in [(ones1, ones_in, True), (lng_b, lng, True),
                              (lnb2_b, lnb2, True), (b1a_sb, b1a_c, False),
                              (b1g_sb, b1g_c, False), (hsc2_sb, hsc2, True)]:
            p.dma("y", lambda y, d=dst, s_=src, row=row: y.dma_start(
                out=d[0:1, :] if row else d[:], in_=s_[:, :]))
        p.wt("t", "ds"); p.wt("t", "dg")
        for dst, wdt in [(lng_b, D), (lnb2_b, D)]:
            bcast_row(dst, wdt)

        with contextlib.ExitStack() as stQ:
            sQ = lambda nm, sh, dt=BF16: stQ.enter_context(
                nc.sbuf_tensor(tag + nm, sh, dt))
            qs_b = stQ.enter_context(
                nc.sbuf_tensor(tag + "qs_b", [128, INNER], F32))
            ks_b = stQ.enter_context(
                nc.sbuf_tensor(tag + "ks_b", [128, INNER], F32))
            msk_sb = stQ.enter_context(
                nc.sbuf_tensor(tag + "msk_sb", [128, 5 * 640], F32))
            qT = sQ("qT", [128, 4 * NQT * 128])
            p.dma("y", lambda y: y.dma_start(out=qs_b[0:1, :],
                                             in_=qs_row[:, :]))
            p.dma("y", lambda y: y.dma_start(out=ks_b[0:1, :],
                                             in_=ks_row[:, :]))
            for i in range(5):
                p.dma("y", lambda y, i=i: y.dma_start(
                    out=msk_sb[:, i * 640:(i + 1) * 640], in_=msk[i, :, :]))
            p.wt("t", "ds")
            for dst, wdt in [(qs_b, INNER), (ks_b, INNER)]:
                bcast_row(dst, wdt)
            kT = sQ("kT", [128, 4 * NKT * 128])
            vbuf = sQ("vbuf", [128, NKT * INNER])

            with contextlib.ExitStack() as st1:
                s1 = lambda nm, sh, dt=BF16: st1.enter_context(
                    nc.sbuf_tensor(tag + nm, sh, dt))
                xT = s1("xT", [128, 8 * EXTT])
                wqkv_bf = s1("wqkv_bf", [128, 8 * 3 * INNER])
                xw_t = st1.enter_context(
                    nc.sbuf_tensor(tag + "xw_t", [128, 2 * D], F32))
                xw_tb = st1.enter_context(
                    nc.sbuf_tensor(tag + "xw_tb", [128, D], BF16))
                if ctx is not None:
                    hscH = st1.enter_context(
                        nc.sbuf_tensor(tag + "hscH", [128, 1], F32))
                    p.dma("y", lambda y: y.dma_start(
                        out=hscH[:], in_=ctx["io"]["hscH"]))
                xbf = s1("xbf", [128, D])
                qbuf = st1.enter_context(
                    nc.sbuf_tensor(tag + "qbuf", [128, 512], F32))
                sqb = st1.enter_context(
                    nc.sbuf_tensor(tag + "sqb", [128, 512], F32))
                nq = st1.enter_context(nc.sbuf_tensor(tag + "nq", [128, 32], F32))
                t1b = st1.enter_context(
                    nc.sbuf_tensor(tag + "t1b", [128, 512], F32))
                t2b = st1.enter_context(
                    nc.sbuf_tensor(tag + "t2b", [128, 512], F32))
                qnb = st1.enter_context(
                    nc.sbuf_tensor(tag + "qnb", [128, 512], F32))
                qrbf = s1("qrbf", [128, 512])
                tbl = st1.enter_context(
                    nc.sbuf_tensor(tag + "tbl", [128, 2 * 512], F32))
                p.dma("g", lambda g: g.dma_start(out=wqkv_bf[:],
                                                 in_=wqkv_r[:, :]))

                for T in range(EXT):
                    slx = slice((T % 2) * D, (T % 2) * D + D)
                    p.wt("y", "v")
                    if ctx is not None and T < 10:
                        def ldhal(g, T=T):
                            prev = _gp_prev(nc, ctx)
                            blk = hal[bass.ts(prev, 1280), :]
                            return g.dma_start(
                                out=xw_tb[:],
                                in_=blk[T * 128:(T + 1) * 128, :])
                        p.wt("g", "v")
                        p.dma("g", ldhal)
                        p.wt("v", "dg"); p.wt("v", "pe")
                        p.on("v", lambda v: v.tensor_scalar(
                            out=xbf[:], in0=xw_tb[:], scalar1=hscH[:, 0:1],
                            scalar2=None, op0=ALU.mult), mark="v")
                    else:
                        src = (xw[T * 128:(T + 1) * 128, :] if ctx is None
                               else xw[(T - 10) * 128:(T - 9) * 128, :])
                        p.dma("y", lambda y, src=src, sl=slx: y.dma_start(
                            out=xw_t[:, sl], in_=src))
                        p.wt("v", "ds"); p.wt("v", "pe")
                        p.on("v", lambda v, sl=slx: v.tensor_copy(
                            out=xbf[:], in_=xw_t[:, sl]), mark="v")
                    p.wt("t", "v")
                    psum_guard(psT)
                    for j in range(8):
                        p.on("t", lambda t, j=j: t.transpose(
                            out=psT[:, j * 128:(j + 1) * 128],
                            in_=xbf[:, j * 128:(j + 1) * 128],
                            identity=ident[:]),
                            mark="pe" if j == 7 else None)
                    p.wt("s", "pe")
                    for j in range(8):
                        p.on("s", lambda s, j=j, T=T: s.copy(
                            out=xT[:, j * EXTT + T * 128:
                                   j * EXTT + (T + 1) * 128],
                            in_=psT[:, j * 128:(j + 1) * 128]), mark="s")
                    set_rd(psT, "s")

                def qk_post(T, src_psum, which):
                    qsb = qs_b if which == "q" else ks_b
                    tcs = tq_cs if which == "q" else tk_cs
                    tss = tq_ss if which == "q" else tk_ss
                    dstT = qT if which == "q" else kT
                    t0 = QT0 if which == "q" else KT0
                    stride = NQT * 128 if which == "q" else NKT * 128
                    p.wt("v", "pe"); p.wt("v", "s")
                    p.on("v", lambda v: v.tensor_copy(out=qbuf[:],
                                                      in_=src_psum[:]),
                         mark="v")
                    set_rd(src_psum, "v")
                    p.wt("s", "v")
                    p.on("s", lambda s: s.activation(out=sqb[:], in_=qbuf[:],
                                                     func=AF.Square),
                         mark="s")
                    p.wt("v", "s")
                    sqv = sqb[:, :].rearrange("p (h d) -> p h d", h=HEADS)
                    p.on("v", lambda v, sqv=sqv: v.reduce_sum(
                        out=nq[:, 0:8], in_=sqv, axis=AX), mark="v")
                    p.wt("v", "v")
                    p.on("v", lambda v: v.tensor_scalar_max(
                        out=nq[:, 8:16], in0=nq[:, 0:8], scalar1=1e-24),
                        mark="v")
                    p.wt("s", "v")
                    p.on("s", lambda s: s.activation(
                        out=nq[:, 16:24], in_=nq[:, 8:16], func=AF.Ln))
                    p.on("s", lambda s: s.activation(
                        out=nq[:, 24:32], in_=nq[:, 16:24], func=AF.Exp,
                        scale=-0.5), mark="s")
                    p.wt("v", "s")
                    for h in range(HEADS):
                        p.on("v", lambda v, h=h, qsb=qsb:
                             v.scalar_tensor_tensor(
                                 out=qnb[:, h * DH:(h + 1) * DH],
                                 in0=qbuf[:, h * DH:(h + 1) * DH],
                                 scalar=nq[:, 24 + h:25 + h],
                                 in1=qsb[:, h * DH:(h + 1) * DH],
                                 op0=ALU.mult, op1=ALU.mult),
                             mark="v" if h == HEADS - 1 else None)
                    p.wt("y", "v")
                    p.dma("y", lambda y, T=T, tcs=tcs: y.dma_start(
                        out=tbl[:, 0:512], in_=tcs[T * 128:(T + 1) * 128, :]))
                    p.dma("y", lambda y, T=T, tss=tss: y.dma_start(
                        out=tbl[:, 512:1024],
                        in_=tss[T * 128:(T + 1) * 128, :]))
                    p.wt("v", "ds")
                    p.on("v", lambda v: v.tensor_tensor(
                        out=t1b[:], in0=qnb[:], in1=tbl[:, 0:512],
                        op=ALU.mult))
                    ssv = lambda half: tbl[:, 512:1024].rearrange(
                        "p (h s d) -> p h s d", h=HEADS, s=2)[:, :, half]
                    p.on("v", lambda v: v.tensor_tensor(
                        out=hv(t2b, 0), in0=hv(qnb, 1), in1=ssv(0),
                        op=ALU.mult))
                    p.on("v", lambda v: v.tensor_tensor(
                        out=hv(t2b, 1), in0=hv(qnb, 0), in1=ssv(1),
                        op=ALU.mult), mark="v")
                    p.on("v", lambda v: v.tensor_tensor(
                        out=hv(qrbf, 0), in0=hv(t1b, 0), in1=hv(t2b, 0),
                        op=ALU.subtract))
                    p.on("v", lambda v: v.tensor_tensor(
                        out=hv(qrbf, 1), in0=hv(t1b, 1), in1=hv(t2b, 1),
                        op=ALU.add), mark="v")
                    p.wt("t", "v")
                    psum_guard(psT)
                    for j in range(4):
                        p.on("t", lambda t, j=j: t.transpose(
                            out=psT[:, j * 128:(j + 1) * 128],
                            in_=qrbf[:, j * 128:(j + 1) * 128],
                            identity=ident[:]),
                            mark="pe" if j == 3 else None)
                    p.wt("s", "pe")
                    for j in range(4):
                        p.on("s", lambda s, j=j, T=T, dstT=dstT, t0=t0,
                             stride=stride: s.copy(
                                 out=dstT[:, j * stride + (T - t0) * 128:
                                          j * stride + (T - t0 + 1) * 128],
                                 in_=psT[:, j * 128:(j + 1) * 128]),
                             mark="s")
                    set_rd(psT, "s")

                p.wt("t", "dg")
                for T in range(KT0, EXT):
                    for nb, bank in ((0, psA), (1, psB), (2, psC)):
                        psum_guard(bank)
                        p.wt("t", "s")
                        for k in range(8):
                            coff = T * 128 - (1 if k >= 4 else 0)
                            p.on("t", lambda t, k=k, nb=nb, bank=bank,
                                 coff=coff: t.matmul(
                                     out=bank[:],
                                     lhsT=xT[:, k * EXTT + coff:
                                             k * EXTT + coff + 128],
                                     rhs=wqkv_bf[:,
                                                 k * 3 * INNER + nb * 512:
                                                 k * 3 * INNER + nb * 512
                                                 + 512],
                                     start=(k == 0), stop=(k == 7)),
                                 mark="pe" if k == 7 else None)
                    p.wt("v", "pe")
                    p.on("v", lambda v, T=T: v.tensor_copy(
                        out=vbuf[:, (T - KT0) * INNER:
                                 (T - KT0 + 1) * INNER],
                        in_=psC[:]), mark="v")
                    set_rd(psC, "v")
                    if DEBUG2 and T == KT0:
                        p.wt("y", "v"); p.wt("y", "s")
                        p.dma("y", lambda y: y.dma_start(
                            out=d2["d2_v2"][:, :], in_=vbuf[:, 0:512]))
                        p.dma("y", lambda y: y.dma_start(
                            out=d2["d2_xT2"][:, :], in_=xT[:]))
                    if DEBUG2 and T == 10:
                        p.wt("y", "v"); p.wt("y", "s")
                        p.dma("y", lambda y: y.dma_start(
                            out=d2["d2_wq"][:, :], in_=wqkv_bf[:]))
                        p.on("v", lambda v: v.tensor_copy(
                            out=qbuf[:], in_=psA[:]), mark="v")
                        p.wt("y", "v")
                        p.dma("y", lambda y: y.dma_start(
                            out=d2["d2_qbuf"][:, :], in_=qbuf[:]))
                    qk_post(T, psB, "k")
                    if T >= QT0:
                        qk_post(T, psA, "q")
                    else:
                        p.wt("v", "pe"); p.wt("v", "s")
                        p.on("v", lambda v: v.tensor_copy(out=qbuf[:],
                                                          in_=psA[:]),
                             mark="v")
                        set_rd(psA, "v")

            if DEBUG2:
                p.wt("y", "s"); p.wt("y", "v")
                p.dma("y", lambda y: y.dma_start(out=d2["d2_xT"][:, :],
                                                 in_=xT[:]))
                p.dma("y", lambda y: y.dma_start(out=d2["d2_qT"][:, :],
                                                 in_=qT[:]))
                p.dma("y", lambda y: y.dma_start(out=d2["d2_kT"][:, :],
                                                 in_=kT[:]))
                p.dma("y", lambda y: y.dma_start(out=d2["d2_vbuf"][:, :],
                                                 in_=vbuf[:]))
            with contextlib.ExitStack() as st2:
                s2 = lambda nm, sh, dt=BF16: st2.enter_context(
                    nc.sbuf_tensor(tag + nm, sh, dt))
                P_f32 = st2.enter_context(
                    nc.sbuf_tensor(tag + "P_f32", [128, 640], F32))
                Pm_bf = s2("Pm_bf", [128, 640])
                PT_sb = s2("PT_sb", [128, 640])
                o_tok = st2.enter_context(
                    nc.sbuf_tensor(tag + "o_tok", [128, 512], F32))
                obf = s2("obf", [128, 512])
                oT = s2("oT", [128, 4 * NQT * 128])
                wo_bf = s2("wo_bf", [128, 4 * D])
                acol = st2.enter_context(
                    nc.sbuf_tensor(tag + "acol2", [128, 4], F32))
                xm_t = st2.enter_context(
                    nc.sbuf_tensor(tag + "xm_t", [128, 2 * D], F32))
                xw_t2 = st2.enter_context(
                    nc.sbuf_tensor(tag + "xw_t2", [128, 2 * D], F32))
                p.wt("g", "pe"); p.wt("g", "v"); p.wt("g", "s")
                p.dma("g", lambda g: g.dma_start(out=wo_bf[:], in_=wo_r[:, :]))
                for T in range(QT0, EXT):
                    for h in range(HEADS):
                        pt = h // 2
                        pb = (h % 2) * 64
                        msel = 0
                        tl = T - 10
                        if 0 <= tl <= 3:
                            msel = 1 + tl
                        psum_guard(ps_sc)
                        p.wt("t", "dg")
                        for so, sw in ((0, 512), (512, 128)):
                            p.on("t", lambda t, T=T, pt=pt, pb=pb, so=so,
                                 sw=sw: t.matmul(
                                     out=ps_sc[:, so:so + sw],
                                     lhsT=qT[pb:pb + 64,
                                             pt * NQT * 128 + (T - QT0) * 128:
                                             pt * NQT * 128 +
                                             (T - QT0 + 1) * 128],
                                     rhs=kT[pb:pb + 64,
                                            pt * NKT * 128 + (T - 6) * 128
                                            + so:
                                            pt * NKT * 128 + (T - 6) * 128
                                            + so + sw],
                                     start=True, stop=True),
                                 mark="pe" if so else None)
                        p.wt("v", "pe"); p.wt("v", "s")
                        p.on("v", lambda v: v.reduce_max(
                            out=acol[:, 0:1], in_=ps_sc[:, :], axis=AX,
                            negate=True), mark="v")
                        p.wt("s", "v")
                        p.on("s", lambda s: s.activation(
                            out=P_f32[:], in_=ps_sc[:, :], func=AF.Exp,
                            bias=acol[:, 0:1]), mark="s")
                        set_rd(ps_sc, "s")
                        p.wt("v", "s")
                        p.on("v", lambda v, msel=msel: v.scalar_tensor_tensor(
                            out=Pm_bf[:], in0=P_f32[:], scalar=1.0,
                            in1=msk_sb[:, msel * 640:(msel + 1) * 640],
                            op0=ALU.mult, op1=ALU.mult,
                            accum_out=acol[:, 1:2]), mark="v")
                        p.wt("v", "v")
                        p.on("v", lambda v: v.reciprocal(
                            out=acol[:, 2:3], in_=acol[:, 1:2]), mark="v")
                        if DEBUG2 and T == 10 and h == 0:
                            p.wt("y", "v"); p.wt("y", "s")
                            p.dma("y", lambda y: y.dma_start(
                                out=d2["d2_P"][:, :], in_=P_f32[:]))
                            p.dma("y", lambda y: y.dma_start(
                                out=d2["d2_Pm"][:, :], in_=Pm_bf[:]))
                        p.wt("t", "v")
                        psum_guard(psT)
                        for j in range(5):
                            p.on("t", lambda t, j=j: t.transpose(
                                out=psT[:, j * 128:(j + 1) * 128],
                                in_=Pm_bf[:, j * 128:(j + 1) * 128],
                                identity=ident[:]),
                                mark="pe" if j == 4 else None)
                        p.wt("s", "pe")
                        p.on("s", lambda s: s.copy(out=PT_sb[:],
                                                   in_=psT[:, 0:640]),
                             mark="s")
                        set_rd(psT, "s")
                        psum_guard(ps_o)
                        p.wt("t", "s")
                        for j in range(5):
                            p.on("t", lambda t, j=j, T=T, h=h: t.matmul(
                                out=ps_o[:, 0:64],
                                lhsT=PT_sb[:, j * 128:(j + 1) * 128],
                                rhs=vbuf[:, (T - 6 + j) * INNER + h * DH:
                                         (T - 6 + j) * INNER + h * DH + 64],
                                start=(j == 0), stop=(j == 4)),
                                mark="pe" if j == 4 else None)
                        p.wt("v", "pe")
                        p.on("v", lambda v, h=h: v.tensor_scalar(
                            out=o_tok[:, h * DH:(h + 1) * DH],
                            in0=ps_o[:, 0:64], scalar1=acol[:, 2:3],
                            scalar2=None, op0=ALU.mult), mark="v")
                        set_rd(ps_o, "v")
                    if DEBUG2 and T == 10:
                        p.wt("y", "v")
                        p.dma("y", lambda y: y.dma_start(
                            out=d2["d2_otok"][:, :], in_=o_tok[:]))
                    p.on("v", lambda v: v.tensor_copy(out=obf[:],
                                                      in_=o_tok[:]),
                         mark="v")
                    p.wt("t", "v")
                    psum_guard(psT)
                    for j in range(4):
                        p.on("t", lambda t, j=j: t.transpose(
                            out=psT[:, j * 128:(j + 1) * 128],
                            in_=obf[:, j * 128:(j + 1) * 128],
                            identity=ident[:]), mark="pe" if j == 3 else None)
                    p.wt("s", "pe")
                    for j in range(4):
                        p.on("s", lambda s, j=j, T=T: s.copy(
                            out=oT[:, j * NQT * 128 + (T - QT0) * 128:
                                   j * NQT * 128 + (T - QT0 + 1) * 128],
                            in_=psT[:, j * 128:(j + 1) * 128]), mark="s")
                    set_rd(psT, "s")
                for T in range(QT0, EXT):
                    slx = slice((T % 2) * D, (T % 2) * D + D)
                    p.wt("y", "v")
                    if ctx is not None and T == QT0:
                        def ldhal9(g, sl=slx):
                            prev = _gp_prev(nc, ctx)
                            blk = hal[bass.ts(prev, 1280), :]
                            return g.dma_start(
                                out=xw_t2[:, sl],
                                in_=blk[QT0 * 128:(QT0 + 1) * 128, :])
                        p.wt("g", "v")
                        p.dma("g", ldhal9)
                        p.wt("v", "dg")
                    else:
                        src = (xw[T * 128:(T + 1) * 128, :] if ctx is None
                               else xw[(T - 10) * 128:(T - 9) * 128, :])
                        p.dma("y", lambda y, src=src, sl=slx: y.dma_start(
                            out=xw_t2[:, sl], in_=src))
                    for nb, bank in ((0, psA), (1, psB)):
                        psum_guard(bank)
                        p.wt("t", "s"); p.wt("t", "dg")
                        for k in range(4):
                            p.on("t", lambda t, k=k, T=T, nb=nb, bank=bank:
                                 t.matmul(
                                     out=bank[:],
                                     lhsT=oT[:, k * NQT * 128 +
                                             (T - QT0) * 128:
                                             k * NQT * 128 +
                                             (T - QT0 + 1) * 128],
                                     rhs=wo_bf[:, k * D + nb * 512:
                                               k * D + nb * 512 + 512],
                                     start=(k == 0), stop=(k == 3)),
                                 mark="pe" if k == 3 else None)
                    p.wt("v", "pe"); p.wt("v", "ds")
                    slm = slice((T % 2) * D, (T % 2) * D + D)
                    for nb, bank in ((0, psA), (1, psB)):
                        p.on("v", lambda v, nb=nb, bank=bank, sl=slx:
                             v.tensor_tensor(
                                 out=xm_t[:, sl.start + nb * 512:
                                          sl.start + nb * 512 + 512],
                                 in0=bank[:],
                                 in1=xw_t2[:, sl.start + nb * 512:
                                           sl.start + nb * 512 + 512],
                                 op=ALU.add), mark="v")
                        set_rd(bank, "v")
                    vpad(); p.wt("y", "v")
                    p.dma("y", lambda y, T=T, sl=slx: y.dma_start(
                        out=xmid_sp[(T - QT0) * 128:(T - QT0 + 1) * 128, :],
                        in_=xm_t[:, sl]))
                    if DEBUG2:
                        p.dma("y", lambda y, T=T, sl=slx: y.dma_start(
                            out=d2["d2_xmid"][(T - QT0) * 128:
                                              (T - QT0 + 1) * 128, :],
                            in_=xm_t[:, sl]))

        with contextlib.ExitStack() as st3:
            s3 = lambda nm, sh, dt=BF16: st3.enter_context(
                nc.sbuf_tensor(tag + nm, sh, dt))
            lnT = s3("lnT", [128, 8 * S])
            hh = s3("hh", [128, MC * S])
            w2_bf = s3("w2_bf", [128, MC * D])
            w1s = s3("w1s", [128, 2 * 8 * 128])
            xs2 = st3.enter_context(nc.sbuf_tensor(tag + "xs2", [128, D], F32))
            tnormA = st3.enter_context(
                nc.sbuf_tensor(tag + "tnormA", [128, D], F32))
            tnormB = lnsc
            ln_bf = s3("ln_bf", [128, D])
            gbuf = st3.enter_context(nc.sbuf_tensor(tag + "gbuf", [128, 512], F32))
            x2t = st3.enter_context(nc.sbuf_tensor(tag + "x2t", [128, D], F32))
            x2u = tnormA
            xm_t3 = st3.enter_context(
                nc.sbuf_tensor(tag + "xm_t3", [128, D], F32))
            xr_t = st3.enter_context(
                nc.sbuf_tensor(tag + "xr_t", [128, D], F32))
            tfix = gbuf
            p.wt("g", "pe"); p.wt("g", "v"); p.wt("g", "s")
            p.dma("g", lambda g: g.dma_start(out=w2_bf[:], in_=w2_r[:, :]))
            for tt in range(16):
                r0 = 128 + tt * 128
                p.wt("y", "v"); p.wt("y", "s")
                p.dma("y", lambda y, r0=r0: y.dma_start(
                    out=xs2[:, 0:512], in_=xmid_sp[r0:r0 + 128, 0:512]))
                p.dma("y", lambda y, r0=r0: y.dma_start(
                    out=xs2[:, 512:1024],
                    in_=xmid_sp[r0 - 1:r0 + 127, 512:1024]))
                p.wt("v", "ds"); p.wt("s", "ds")
                if tt == 0:
                    p.on("v", lambda v: v.tensor_scalar(
                        out=tfix[0:1, 0:512], in0=xs2[0:1, 512:1024],
                        scalar1=hsc2_sb[0:1, 0:1], scalar2=None,
                        op0=ALU.mult), mark="v")
                    p.wt("v", "v")
                    p.on("v", lambda v: v.tensor_copy(
                        out=xs2[0:1, 512:1024], in_=tfix[0:1, 0:512]),
                        mark="v")
                    p.wt("v", "v"); p.wt("s", "v")
                ln_stats(p, xs2[:], lnsc[:], cols, D)
                p.on("v", lambda v: v.tensor_scalar(
                    out=tnormA[:], in0=xs2[:], scalar1=cols["rstd"],
                    scalar2=cols["nmr"], op0=ALU.mult, op1=ALU.add))
                p.on("v", lambda v: v.tensor_tensor(
                    out=tnormB[:], in0=tnormA[:], in1=lng_b[:], op=ALU.mult))
                p.on("v", lambda v: v.tensor_tensor(
                    out=ln_bf[:], in0=tnormB[:], in1=lnb2_b[:], op=ALU.add),
                    mark="v")
                p.wt("t", "v")
                psum_guard(psT)
                for j in range(8):
                    p.on("t", lambda t, j=j: t.transpose(
                        out=psT[:, j * 128:(j + 1) * 128],
                        in_=ln_bf[:, j * 128:(j + 1) * 128],
                        identity=ident[:]), mark="pe" if j == 7 else None)
                p.wt("s", "pe")
                for j in range(8):
                    p.on("s", lambda s, j=j, tt=tt: s.copy(
                        out=lnT[:, j * S + tt * 128:j * S + (tt + 1) * 128],
                        in_=psT[:, j * 128:(j + 1) * 128]), mark="s")
                set_rd(psT, "s")
            if DEBUG2:
                p.wt("y", "s")
                p.dma("y", lambda y: y.dma_start(out=d2["d2_lnT"][:, :],
                                                 in_=lnT[:]))
            pe_hist = {}
            pe_b2start = p.ct["pe"]
            for mc in range(MC):
                if mc >= 1:
                    p.wt("g", "pe", pe_hist.get(mc - 1, pe_b2start))
                else:
                    p.wt("g", "pe", pe_b2start)
                p.dma("g", lambda g, mc=mc: g.dma_start(
                    out=w1s[:, 0:1024], in_=w1a_r[mc, :, :]))
                p.dma("g", lambda g, mc=mc: g.dma_start(
                    out=w1s[:, 1024:2048], in_=w1g_r[mc, :, :]))
                p.wt("t", "dg"); p.wt("t", "s")
                for tc in range(4):
                    for gi, bank in ((0, psA), (1, psB)):
                        psum_guard(bank)
                        for k in range(8):
                            p.on("t", lambda t, k=k, tc=tc, gi=gi, bank=bank:
                                 t.matmul(
                                     out=bank[:],
                                     lhsT=w1s[:, gi * 1024 + k * 128:
                                              gi * 1024 + (k + 1) * 128],
                                     rhs=lnT[:, k * S + tc * 512:
                                             k * S + (tc + 1) * 512],
                                     start=(k == 0), stop=(k == 7)),
                                 mark="pe" if k == 7 else None)
                    p.wt("s", "pe")
                    p.on("s", lambda s, mc=mc: s.activation(
                        out=gbuf[:], in_=psB[:], func=AF.Gelu,
                        bias=b1g_sb[:, mc:mc + 1]), mark="s")
                    set_rd(psB, "s")
                    p.wt("v", "s")
                    p.on("v", lambda v, mc=mc, tc=tc: v.scalar_tensor_tensor(
                        out=hh[:, mc * S + tc * 512:mc * S + (tc + 1) * 512],
                        in0=psA[:], scalar=b1a_sb[:, mc:mc + 1], in1=gbuf[:],
                        op0=ALU.add, op1=ALU.mult), mark="v")
                    set_rd(psA, "v")
                pe_hist[mc] = p.ct["pe"]
            if DEBUG2:
                p.wt("y", "v")
                p.dma("y", lambda y: y.dma_start(out=d2["d2_hh"][:, :],
                                                 in_=hh[:, 0:S]))
            p.wt("t", "dg")
            for tt in range(16):
                r0 = 128 + tt * 128
                p.wt("y", "v")
                p.dma("y", lambda y, r0=r0: y.dma_start(
                    out=xm_t3[:], in_=xmid_sp[r0:r0 + 128, :]))
                if post_ln:
                    p.dma("y", lambda y, tt=tt: y.dma_start(
                        out=xr_t[:], in_=xres[tt * 128:(tt + 1) * 128, :]))
                for nb, bank in ((0, psA), (1, psB)):
                    psum_guard(bank)
                    p.wt("t", "v")
                    p.on("t", lambda t, nb=nb, bank=bank: t.matmul(
                        out=bank[:], lhsT=ones_bf[:, :],
                        rhs=b2_bf[0:1, nb * 512:nb * 512 + 512],
                        start=True, stop=False))
                    for mc in range(MC):
                        p.on("t", lambda t, mc=mc, tt=tt, nb=nb, bank=bank:
                             t.matmul(
                                 out=bank[:],
                                 lhsT=hh[:, mc * S + tt * 128:
                                         mc * S + (tt + 1) * 128],
                                 rhs=w2_bf[:, mc * D + nb * 512:
                                           mc * D + nb * 512 + 512],
                                 start=False, stop=(mc == MC - 1)),
                             mark="pe" if mc == MC - 1 else None)
                p.wt("v", "pe"); p.wt("v", "ds"); p.wt("v", "s")
                for nb, bank in ((0, psA), (1, psB)):
                    p.on("v", lambda v, nb=nb, bank=bank:
                         v.tensor_tensor(
                             out=x2t[:, nb * 512:nb * 512 + 512],
                             in0=bank[:],
                             in1=xm_t3[:, nb * 512:nb * 512 + 512],
                             op=ALU.add), mark="v")
                    set_rd(bank, "v")
                if post_ln:
                    p.on("v", lambda v: v.tensor_tensor(
                        out=x2u[:], in0=x2t[:], in1=xr_t[:], op=ALU.add),
                        mark="v")
                    ln_stats(p, x2u[:], lnsc[:], cols, D)
                    p.on("v", lambda v: v.tensor_scalar(
                        out=x2t[:], in0=x2u[:], scalar1=cols["rstd"],
                        scalar2=cols["nmr"], op0=ALU.mult, op1=ALU.add),
                        mark="v")
                vpad(); p.wt("y", "v")
                p.dma("y", lambda y, tt=tt: y.dma_start(
                    out=x2[tt * 128:(tt + 1) * 128, :], in_=x2t[:]))

        p.on("v", lambda v: v.memset(lncol[:, 9:10], 0.0), mark="v")
        p.on("v", lambda v: v.memset(lncol[:, 9:10], 0.0), mark="v")
        p.on("s", lambda s_: s_.activation(out=lncol[:, 10:11],
                                           in_=lncol[:, 9:10],
                                           func=AF.Square), mark="s")
        p.on("s", lambda s_: s_.activation(out=lncol[:, 11:12],
                                           in_=lncol[:, 9:10],
                                           func=AF.Square), mark="s")
        if ctx is None:
            p.wt("y", "v"); p.wt("y", "s")
            p.wt("y", "ds")
            p.wt("y", "dg")
            p.replay(block)
            for g in p.guards:
                g.__exit__(None, None, None)
    for g in reversed(_stage_guards):
        g.__exit__(None, None, None)
    return nc


def _layer_inputs(x_full, lw, post_ln, xres_full=None):
    f32 = np.float32
    wqkv_r = np.ascontiguousarray(
        np.asarray(lw["wqkv"], f32).reshape(8, 128, 3 * INNER)
        .transpose(1, 0, 2).reshape(128, 8 * 3 * INNER))
    wo_rr = np.ascontiguousarray(
        np.asarray(lw["wo"], f32).reshape(4, 128, D).transpose(1, 0, 2)
        .reshape(128, 4 * D))
    HID = 2730
    w1 = np.asarray(lw["w1"], f32)
    w1a = np.zeros((D, HID2), f32); w1a[:, :HID] = w1[:, :HID]
    w1g = np.zeros((D, HID2), f32); w1g[:, :HID] = w1[:, HID:]
    b1 = np.asarray(lw["b1"], f32)
    b1a = np.zeros(HID2, f32); b1a[:HID] = b1[:HID]
    b1g = np.zeros(HID2, f32); b1g[:HID] = b1[HID:]
    w2p = np.zeros((HID2, D), f32); w2p[:HID] = np.asarray(lw["w2"], f32)
    rearr1 = lambda w: np.ascontiguousarray(
        w.reshape(8, 128, MC, 128).transpose(2, 1, 0, 3)
        .reshape(MC, 128, 8 * 128))
    w2_rr = np.ascontiguousarray(
        w2p.reshape(MC, 128, D).transpose(1, 0, 2).reshape(128, MC * D))
    bcol = lambda b: np.ascontiguousarray(b.reshape(MC, 128).T)
    tile8 = lambda a: np.tile(a, (1, 8))

    common = dict(
        wqkv_r=wqkv_r,
        qs_row=tile8(np.asarray(lw["q_scale"], f32).reshape(1, DH)),
        ks_row=tile8(np.asarray(lw["k_scale"], f32).reshape(1, DH)),
        wo_r=wo_rr, lng=np.asarray(lw["ln_g"], f32).reshape(1, D),
        lnb2=np.asarray(lw["ln_b"], f32).reshape(1, D),
        w1a_r=rearr1(w1a), w1g_r=rearr1(w1g),
        b1a_c=bcol(b1a), b1g_c=bcol(b1g),
        w2_r=w2_rr, b2row=np.asarray(lw["b2"], f32).reshape(1, D),
        ident=np.eye(128).astype(BF),
        ones1=np.ones((1, 128), f32))
    half_idx = np.arange(0, DH, 2, dtype=np.float64)
    inv_freq = 1.0 / 10000.0 ** (half_idx / DH)
    base = (half_idx + 0.4 * DH) / (1.4 * DH)
    ii = np.arange(128)[:, None]
    jj = np.arange(640)[None, :]
    std = ((jj >= ii) & (jj <= ii + 512)).astype(f32)
    maps = []
    for c in range(NCORE):
        lo = c * S - 1280
        xwin = np.zeros((EXTT, D), f32)
        s0 = max(0, -lo)
        xwin[s0:] = x_full[lo + s0:lo + EXTT]
        t_abs = np.arange(lo, lo + EXTT, dtype=np.float64)
        t_c = np.clip(t_abs, 0, None)
        freqs = t_c[:, None] * inv_freq[None, :]
        cosc = np.concatenate([np.cos(freqs)] * 2, 1)
        sinc = np.concatenate([np.sin(freqs)] * 2, 1)
        power = (t_c - T_FULL // 2) / (512 // 2)
        scale = np.concatenate([base[None, :] ** power[:, None]] * 2, 1)
        QS = 8.0
        m = dict(common)
        m.update(
            xw=xwin,
            tq_cs=tile8((cosc * scale * QS).astype(f32)),
            tq_ss=tile8((sinc * scale * QS).astype(f32)),
            tk_cs=tile8((cosc / scale).astype(f32)),
            tk_ss=tile8((sinc / scale).astype(f32)),
            hsc2=np.array([[0.0 if c == 0 else 1.0]], f32))
        mk = np.stack([std] * 5)
        if c == 0:
            for ti in range(4):
                mk[1 + ti] = std * (jj >= 512 - 128 * ti)
        m["msk"] = np.ascontiguousarray(mk.astype(f32))
        if post_ln:
            m["xres"] = np.ascontiguousarray(xres_full[c * S:(c + 1) * S])
        maps.append({k: np.ascontiguousarray(v) if k != "ident" else v
                     for k, v in m.items()})
    return maps


_LAYER_NC = {}


def run_local_layer(x_full, lw, post_ln, xres_full=None, full=False):
    if post_ln not in _LAYER_NC:
        _LAYER_NC[post_ln] = build_local_layer(post_ln)
    nc = _LAYER_NC[post_ln]
    maps = _layer_inputs(x_full, lw, post_ln, xres_full)
    res = run_bass_kernel_spmd(nc, maps, core_ids=list(range(NCORE)))
    if full:
        return res.results
    return np.concatenate([np.asarray(r["x2"]) for r in res.results], axis=0)

G4 = 2
H4 = 16
NT4 = 10


def build_stage4(ctx=None):
    tag = "" if ctx is None else ctx["tag"]
    if ctx is None:
        nc = bass.Bass("TRN2", target_bir_lowering=False, debug=False,
                       num_devices=NCORE)
        din = lambda nm, sh, dt=F32: nc.dram_tensor(nm, sh, dt,
                                                    kind="ExternalInput").ap()
        xg = din("xg", [NT4 * 128, D])
        xr3 = din("xr3", [S, D])
        xg_bf = None
    else:
        nc = ctx["nc"]
        din = lambda nm, sh, dt=F32: ctx["io"][nm]
        xr3 = ctx["x_in"]
        xg = None
        xg_bf = nc.dram_tensor(tag + "xg_bf", [NT4 * 128, D], BF16)
    Wq1r = din("Wq1r", [128, 8 * D]); Wq2r = din("Wq2r", [128, 8 * D])
    Wk1r = din("Wk1r", [128, 8 * 128]); Wk2r = din("Wk2r", [128, 8 * 128])
    Wvr = din("Wvr", [128, 8 * 128])
    Woutr = din("Woutr", [128, 8 * D])
    boutrow = din("boutrow", [1, D])
    lqk = din("lqk", [DTR, 4])
    t4c = din("t4c", [NT4 * 128, 512]); t4s = din("t4s", [NT4 * 128, 512])
    t4cq = din("t4cq", [NT4 * 128, 512]); t4sq = din("t4sq", [NT4 * 128, 512])
    if ctx is None:
        ident_in = nc.dram_tensor("ident", [128, 128], BF16,
                                  kind="ExternalInput").ap()
        x4 = nc.dram_tensor("x4", [S, D], F32, kind="ExternalOutput").ap()
    else:
        ident_in = ctx["io"]["ident"]
        x4 = ctx["x_out"]
    ones_in = din("ones1", [1, 128])

    _stage_guards = []

    def sb(nm, sh, dt=F32):
        g = nc.sbuf_tensor(tag + nm, sh, dt)
        h = g.__enter__()
        _stage_guards.append(g)
        return h
    ident = sb("identb", [128, 128], BF16)
    ones1 = sb("ones1b", [1, 128])
    ones_bf = sb("ones_bfb", [1, 128], BF16)
    bout_bf = sb("bout_bfb", [1, D], BF16)
    lqk_sb = sb("lqk_sb", [DTR, 4])
    lamw = sb("lamw", [1, 8])
    nlam_col = sb("nlam_col", [128, 1])
    xgT = sb("xgT", [128, 8 * NT4 * 128], BF16)
    q1T = sb("q1T", [128, 8 * NT4 * 128], BF16)
    q2T = sb("q2T", [128, 8 * NT4 * 128], BF16)
    k1T = sb("k1T", [128, NT4 * 128], BF16)
    k2T = sb("k2T", [128, NT4 * 128], BF16)
    vb4 = sb("vb4", [128, NT4 * 128], BF16)
    Wq1b = sb("Wq1b", [128, 8 * D], BF16)
    Wq2b = sb("Wq2b", [128, 8 * D], BF16)
    Wk1b = sb("Wk1b", [128, 8 * 128], BF16)
    Wk2b = sb("Wk2b", [128, 8 * 128], BF16)
    Wvb = sb("Wvb", [128, 8 * 128], BF16)
    Woutb = sb("Woutb", [128, 8 * D], BF16)
    xt4 = sb("xt4s", [128, 2 * D])
    qbuf = sb("qbuf4", [128, D])
    qr4 = sb("qr4", [128, D])
    qrb = sb("qrb4", [128, D], BF16)
    tA = sb("tA4", [128, 512]); tB = sb("tB4", [128, 512])
    tbl_c = sb("tbl_c4", [128, 512]); tbl_s = sb("tbl_s4", [128, 512])
    P4 = sb("P4", [128, 256], BF16)
    PT4 = sb("PT4", [128, 256], BF16)
    o2s = sb("o2s", [128, 64]); o1s = sb("o1s", [128, 64])
    o_acc = sb("o_acc", [128, D])
    o_bf = sb("o_bf4", [128, D], BF16)
    oT4 = sb("oT4", [128, 8 * 128], BF16)
    acol = sb("acol4", [128, 8])
    xr_t = sb("xr_t4", [128, D])
    xo_t = sb("xo_t4", [128, D])
    lnsc = sb("lnsc4", [128, D]); lncol = sb("lncol4", [128, 12])
    padv = sb("padv", [128, 96]); pads = sb("pads", [128, 32])

    if ctx is None:
        psA = nc.alloc_psum_tensor("psA4", [128, 512], F32)
        psB = nc.alloc_psum_tensor("psB4", [128, 512], F32)
        psT = nc.alloc_psum_tensor("psT4", [128, 1024], BF16)
        psS = nc.alloc_psum_tensor("psS4", [128, 256], F32)
        psO = nc.alloc_psum_tensor("psO4", [128, 128], F32)
    else:
        psA, psB, psT = ctx["psA"], ctx["psB"], ctx["psT"]
        psS, psO = ctx["ps_sc"], ctx["ps_o"]

    with (nc.Block() if ctx is None else contextlib.nullcontext()) as block:
        p = P(nc) if ctx is None else ctx["p"]

        def vpad():
            p.on("v", lambda v: v.memset(padv[:], 0.0), mark="v")

        def spad():
            p.on("s", lambda s_: s_.activation(out=pads[:],
                                               in_=padv[:, 0:pads.shape[1]],
                                               func=AF.Square), mark="s")
        cols = dict(sum=lncol[:, 0:1], ssq=lncol[:, 1:2], mean=lncol[:, 2:3],
                    nmean=lncol[:, 3:4], ex2=lncol[:, 4:5], nvar=lncol[:, 5:6],
                    lnv=lncol[:, 6:7], rstd=lncol[:, 7:8], nmr=lncol[:, 8:9])
        last_rd = ({id(b): ("v", 0) for b in (psA, psB, psT, psS, psO)}
                   if ctx is None else ctx["last_rd"])

        def psum_guard(bank):
            p.wt("t", last_rd[id(bank)][0], last_rd[id(bank)][1])

        def set_rd(bank, eng):
            last_rd[id(bank)] = (eng, p.ct[eng])

        p.dma("g", lambda g: g.dma_start(out=ident[:], in_=ident_in[:, :]))
        p.dma("g", lambda g: g.dma_start(out=ones_bf[:], in_=ones_in[:, :]))
        p.dma("g", lambda g: g.dma_start(out=bout_bf[:], in_=boutrow[:, :]))
        for dst, src in [(Wq1b, Wq1r), (Wq2b, Wq2r), (Wk1b, Wk1r),
                         (Wk2b, Wk2r), (Wvb, Wvr), (Woutb, Woutr)]:
            p.dma("g", lambda g, d=dst, s_=src: g.dma_start(out=d[:],
                                                            in_=s_[:, :]))
        p.dma("y", lambda y: y.dma_start(out=ones1[0:1, :], in_=ones_in[:, :]))
        p.dma("y", lambda y: y.dma_start(out=lqk_sb[:], in_=lqk[:, :]))
        p.wt("t", "ds"); p.wt("t", "dg")
        psum_guard(psB)
        p.on("t", lambda t: t.matmul(out=psB[0:1, 0:1],
                                     lhsT=lqk_sb[:, 0:1], rhs=lqk_sb[:, 1:2],
                                     start=True, stop=True))
        p.on("t", lambda t: t.matmul(out=psB[0:1, 1:2],
                                     lhsT=lqk_sb[:, 2:3], rhs=lqk_sb[:, 3:4],
                                     start=True, stop=True), mark="pe")
        p.wt("s", "pe")
        p.on("s", lambda s: s.activation(out=lamw[0:1, 0:2],
                                         in_=psB[0:1, 0:2], func=AF.Exp),
             mark="s")
        set_rd(psB, "s")
        p.wt("v", "s")
        p.on("v", lambda v: v.tensor_tensor(out=lamw[0:1, 2:3],
                                            in0=lamw[0:1, 0:1],
                                            in1=lamw[0:1, 1:2],
                                            op=ALU.subtract), mark="v")
        p.wt("v", "v")
        p.on("v", lambda v: v.tensor_scalar_add(out=lamw[0:1, 3:4],
                                                in0=lamw[0:1, 2:3],
                                                scalar1=0.8), mark="v")
        p.wt("v", "v")
        p.on("v", lambda v: v.tensor_scalar_max(out=lamw[0:1, 4:5],
                                                in0=lamw[0:1, 3:4],
                                                scalar1=0.1), mark="v")
        p.wt("v", "v")
        p.on("v", lambda v: v.tensor_scalar_min(out=lamw[0:1, 5:6],
                                                in0=lamw[0:1, 4:5],
                                                scalar1=0.9), mark="v")
        p.wt("v", "v")
        p.on("v", lambda v: v.tensor_scalar_mul(out=lamw[0:1, 6:7],
                                                in0=lamw[0:1, 5:6],
                                                scalar1=-1.0), mark="v")
        p.wt("t", "v")
        psum_guard(psB)
        p.on("t", lambda t: t.matmul(out=psB[:, 0:1], lhsT=ones1[:, :],
                                     rhs=lamw[0:1, 6:7], start=True,
                                     stop=True), mark="pe")
        p.wt("v", "pe")
        p.on("v", lambda v: v.tensor_copy(out=nlam_col[:], in_=psB[:, 0:1]),
             mark="v")
        set_rd(psB, "v")

        if ctx is not None:
            def ldwin(g, part):
                cid = _gp_cid(nc, ctx)
                if part == 0:
                    return g.dma_start(out=xg_bf[0:1024, :],
                                       in_=ctx["x3g"][bass.ts(cid, 1024), :])
                return g.dma_start(
                    out=xg_bf[1024:1280, :],
                    in_=ctx["x3g"][bass.ts(4 * cid + 4, 256), :])
            p.dma("g", lambda g: ldwin(g, 0))
            p.dma("g", lambda g: ldwin(g, 1))
        for T in range(NT4):
            slx = slice((T % 2) * D, (T % 2) * D + D)
            p.wt("y", "v")
            if ctx is None:
                p.dma("y", lambda y, T=T, sl=slx: y.dma_start(
                    out=xt4[:, sl], in_=xg[T * 128:(T + 1) * 128, :]))
            else:
                p.wt("g", "v")
                p.dma("g", lambda g, T=T, sl=slx: g.dma_start(
                    out=xt4[:, sl], in_=xg_bf[T * 128:(T + 1) * 128, :]))
                p.wt("v", "dg")
            p.wt("v", "ds"); p.wt("v", "pe")
            p.on("v", lambda v, sl=slx: v.tensor_copy(out=qr4[:],
                                                      in_=xt4[:, sl]))
            p.on("v", lambda v: v.tensor_copy(out=qrb[:], in_=qr4[:]),
                 mark="v")
            p.wt("t", "v")
            psum_guard(psT)
            for j in range(8):
                p.on("t", lambda t, j=j: t.transpose(
                    out=psT[:, j * 128:(j + 1) * 128],
                    in_=qrb[:, j * 128:(j + 1) * 128], identity=ident[:]),
                    mark="pe" if j == 7 else None)
            p.wt("s", "pe")
            for j in range(8):
                p.on("s", lambda s, j=j, T=T: s.copy(
                    out=xgT[:, j * NT4 * 128 + T * 128:
                            j * NT4 * 128 + (T + 1) * 128],
                    in_=psT[:, j * 128:(j + 1) * 128]), mark="s")
            set_rd(psT, "s")

        def rope4(T, dstT, tq, nheads):
            wide = nheads * DH
            p.wt("y", "v")
            p.dma("y", lambda y, T=T, tq=tq: y.dma_start(
                out=tbl_c[:], in_=(t4cq if tq else t4c)[T * 128:
                                                        (T + 1) * 128, :]))
            p.dma("y", lambda y, T=T, tq=tq: y.dma_start(
                out=tbl_s[:], in_=(t4sq if tq else t4s)[T * 128:
                                                        (T + 1) * 128, :]))
            p.wt("v", "ds")
            qe = qbuf[:, 0:wide].rearrange("p (h d two) -> p h d two",
                                           two=2, d=32)[:, :, :, 0]
            qo = qbuf[:, 0:wide].rearrange("p (h d two) -> p h d two",
                                           two=2, d=32)[:, :, :, 1]
            cr = tbl_c[:, 0:nheads * 32].rearrange("p (h d) -> p h d", d=32)
            sr = tbl_s[:, 0:nheads * 32].rearrange("p (h d) -> p h d", d=32)
            tAv = tA[:, 0:nheads * 32].rearrange("p (h d) -> p h d", d=32)
            tBv = tB[:, 0:nheads * 32].rearrange("p (h d) -> p h d", d=32)
            qre = qr4[:, 0:wide].rearrange("p (h s d) -> p h s d",
                                           s=2, d=32)[:, :, 0]
            qro = qr4[:, 0:wide].rearrange("p (h s d) -> p h s d",
                                           s=2, d=32)[:, :, 1]
            p.on("v", lambda v: v.tensor_tensor(out=tAv, in0=qe, in1=cr,
                                                op=ALU.mult))
            p.on("v", lambda v: v.tensor_tensor(out=tBv, in0=qo, in1=sr,
                                                op=ALU.mult), mark="v")
            p.on("v", lambda v: v.tensor_tensor(out=qre, in0=tAv, in1=tBv,
                                                op=ALU.subtract), mark="v")
            p.on("v", lambda v: v.tensor_tensor(out=tAv, in0=qe, in1=sr,
                                                op=ALU.mult))
            p.on("v", lambda v: v.tensor_tensor(out=tBv, in0=qo, in1=cr,
                                                op=ALU.mult), mark="v")
            p.on("v", lambda v: v.tensor_tensor(out=qro, in0=tAv, in1=tBv,
                                                op=ALU.add))
            p.on("v", lambda v: v.tensor_copy(out=qrb[:, 0:wide],
                                              in_=qr4[:, 0:wide]), mark="v")
            p.wt("t", "v")
            psum_guard(psT)
            nj = wide // 128
            for j in range(nj):
                p.on("t", lambda t, j=j: t.transpose(
                    out=psT[:, j * 128:(j + 1) * 128],
                    in_=qrb[:, j * 128:(j + 1) * 128], identity=ident[:]),
                    mark="pe" if j == nj - 1 else None)
            p.wt("s", "pe")
            for j in range(nj):
                p.on("s", lambda s, j=j, T=T, dstT=dstT: s.copy(
                    out=dstT[:, j * NT4 * 128 + T * 128:
                            j * NT4 * 128 + (T + 1) * 128],
                    in_=psT[:, j * 128:(j + 1) * 128]), mark="s")
            set_rd(psT, "s")

        for T in range(NT4):
            for qi, (Wb, dstT) in enumerate(((Wq1b, q1T), (Wq2b, q2T))):
                for nb, bank in ((0, psA), (1, psB)):
                    psum_guard(bank)
                    p.wt("t", "s")
                    for k in range(8):
                        p.on("t", lambda t, k=k, T=T, nb=nb, bank=bank,
                             Wb=Wb: t.matmul(
                                 out=bank[:],
                                 lhsT=xgT[:, k * NT4 * 128 + T * 128:
                                          k * NT4 * 128 + (T + 1) * 128],
                                 rhs=Wb[:, k * D + nb * 512:
                                        k * D + nb * 512 + 512],
                                 start=(k == 0), stop=(k == 7)),
                             mark="pe" if k == 7 else None)
                p.wt("v", "pe"); p.wt("v", "s")
                p.on("v", lambda v: v.tensor_copy(out=qbuf[:, 0:512],
                                                  in_=psA[:]))
                p.on("v", lambda v: v.tensor_copy(out=qbuf[:, 512:1024],
                                                  in_=psB[:]), mark="v")
                set_rd(psA, "v"); set_rd(psB, "v")
                rope4(T, dstT, 1, H4)
            for src, dstT, tq in ((Wk1b, k1T, 0), (Wk2b, k2T, 0),
                                  (Wvb, vb4, -1)):
                psum_guard(psO)
                p.wt("t", "s"); p.wt("t", "v")
                for k in range(8):
                    p.on("t", lambda t, k=k, T=T, src=src: t.matmul(
                        out=psO[:, 0:128],
                        lhsT=xgT[:, k * NT4 * 128 + T * 128:
                                 k * NT4 * 128 + (T + 1) * 128],
                        rhs=src[:, k * 128:(k + 1) * 128],
                        start=(k == 0), stop=(k == 7)),
                        mark="pe" if k == 7 else None)
                p.wt("v", "pe"); p.wt("v", "s")
                if tq == -1:
                    p.on("v", lambda v, T=T: v.tensor_copy(
                        out=vb4[:, T * 128:(T + 1) * 128],
                        in_=psO[:, 0:128]), mark="v")
                    set_rd(psO, "v")
                else:
                    p.on("v", lambda v: v.tensor_copy(
                        out=qbuf[:, 0:128], in_=psO[:, 0:128]), mark="v")
                    set_rd(psO, "v")
                    rope4(T, dstT, tq, G4)

        for wl in range(8):
            for qt in range(2):
                Ti = wl + qt
                Tq = 2 * wl + qt
                for h in range(H4):
                    g = h % G4
                    for br, (qTx, kTx) in enumerate(((q1T, k1T),
                                                     (q2T, k2T))):
                        pt = h // 2
                        pb = (h % 2) * 64
                        psum_guard(psS)
                        p.wt("t", "s"); p.wt("t", "v")
                        p.on("t", lambda t, Ti=Ti, wl=wl, pt=pt, pb=pb,
                             qTx=qTx, kTx=kTx, g=g: t.matmul(
                                 out=psS[:, 0:256],
                                 lhsT=qTx[pb:pb + 64,
                                          pt * NT4 * 128 + Ti * 128:
                                          pt * NT4 * 128 + (Ti + 1) * 128],
                                 rhs=kTx[64 * g:64 * g + 64,
                                         wl * 128:wl * 128 + 256],
                                 start=True, stop=True), mark="pe")
                        p.wt("v", "pe"); p.wt("v", "s")
                        p.on("v", lambda v, br=br: v.reduce_max(
                            out=acol[:, br:br + 1], in_=psS[:, 0:256],
                            axis=AX, negate=True), mark="v")
                        p.wt("s", "v")
                        p.on("s", lambda s, br=br: s.activation(
                            out=P4[:], in_=psS[:, 0:256], func=AF.Exp,
                            bias=acol[:, br:br + 1],
                            accum_out=acol[:, 2 + br:3 + br]), mark="s")
                        set_rd(psS, "s")
                        p.wt("v", "s")
                        p.on("v", lambda v, br=br: v.reciprocal(
                            out=acol[:, 4 + br:5 + br],
                            in_=acol[:, 2 + br:3 + br]), mark="v")
                        p.wt("t", "s")
                        psum_guard(psT)
                        for j in range(2):
                            p.on("t", lambda t, j=j: t.transpose(
                                out=psT[:, j * 128:(j + 1) * 128],
                                in_=P4[:, j * 128:(j + 1) * 128],
                                identity=ident[:]),
                                mark="pe" if j == 1 else None)
                        p.wt("s", "pe")
                        p.on("s", lambda s: s.copy(out=PT4[:],
                                                   in_=psT[:, 0:256]),
                             mark="s")
                        set_rd(psT, "s")
                        psum_guard(psO)
                        p.wt("t", "s")
                        for j in range(2):
                            p.on("t", lambda t, j=j, wl=wl, g=g: t.matmul(
                                out=psO[:, 0:64],
                                lhsT=PT4[:, j * 128:(j + 1) * 128],
                                rhs=vb4[:, (wl + j) * 128 + 64 * g:
                                        (wl + j) * 128 + 64 * g + 64],
                                start=(j == 0), stop=(j == 1)),
                                mark="pe" if j == 1 else None)
                        p.wt("v", "pe"); p.wt("v", "v")
                        dst = o1s if br == 0 else o2s
                        p.on("v", lambda v, br=br, dst=dst: v.tensor_scalar(
                            out=dst[:], in0=psO[:, 0:64],
                            scalar1=acol[:, 4 + br:5 + br], scalar2=None,
                            op0=ALU.mult), mark="v")
                        set_rd(psO, "v")
                    p.wt("v", "v")
                    p.on("v", lambda v, h=h: v.scalar_tensor_tensor(
                        out=o_acc[:, h * DH:(h + 1) * DH], in0=o2s[:],
                        scalar=nlam_col[:, 0:1], in1=o1s[:],
                        op0=ALU.mult, op1=ALU.add), mark="v")
                p.on("v", lambda v: v.tensor_copy(out=o_bf[:], in_=o_acc[:]),
                     mark="v")
                p.wt("t", "v")
                psum_guard(psT)
                for j in range(8):
                    p.on("t", lambda t, j=j: t.transpose(
                        out=psT[:, j * 128:(j + 1) * 128],
                        in_=o_bf[:, j * 128:(j + 1) * 128],
                        identity=ident[:]), mark="pe" if j == 7 else None)
                p.wt("s", "pe")
                for j in range(8):
                    p.on("s", lambda s, j=j: s.copy(
                        out=oT4[:, j * 128:(j + 1) * 128],
                        in_=psT[:, j * 128:(j + 1) * 128]), mark="s")
                set_rd(psT, "s")
                p.wt("y", "v"); p.wt("y", "s")
                p.wt("y", "ds")
                p.dma("y", lambda y, Tq=Tq: y.dma_start(
                    out=xr_t[:], in_=xr3[Tq * 128:(Tq + 1) * 128, :]))
                for nb, bank in ((0, psA), (1, psB)):
                    psum_guard(bank)
                    p.wt("t", "s")
                    p.on("t", lambda t, nb=nb, bank=bank: t.matmul(
                        out=bank[:], lhsT=ones_bf[:, :],
                        rhs=bout_bf[0:1, nb * 512:nb * 512 + 512],
                        start=True, stop=False))
                    for j in range(8):
                        p.on("t", lambda t, j=j, nb=nb, bank=bank: t.matmul(
                            out=bank[:],
                            lhsT=oT4[:, j * 128:(j + 1) * 128],
                            rhs=Woutb[:, j * D + nb * 512:
                                      j * D + nb * 512 + 512],
                            start=False, stop=(j == 7)),
                            mark="pe" if j == 7 else None)
                p.wt("v", "pe"); p.wt("v", "ds"); p.wt("v", "s")
                for nb, bank in ((0, psA), (1, psB)):
                    p.on("v", lambda v, nb=nb, bank=bank: v.tensor_tensor(
                        out=xo_t[:, nb * 512:nb * 512 + 512], in0=bank[:],
                        in1=xr_t[:, nb * 512:nb * 512 + 512], op=ALU.add),
                        mark="v")
                    set_rd(bank, "v")
                ln_stats(p, xo_t[:], lnsc[:], cols, D)
                p.on("v", lambda v: v.tensor_scalar(
                    out=xr_t[:], in0=xo_t[:], scalar1=cols["rstd"],
                    scalar2=cols["nmr"], op0=ALU.mult, op1=ALU.add),
                    mark="v")
                vpad(); p.wt("y", "v")
                p.dma("y", lambda y, Tq=Tq: y.dma_start(
                    out=x4[Tq * 128:(Tq + 1) * 128, :], in_=xr_t[:]))

        p.on("v", lambda v: v.memset(lncol[:, 9:10], 0.0), mark="v")
        p.on("v", lambda v: v.memset(lncol[:, 9:10], 0.0), mark="v")
        p.on("s", lambda s_: s_.activation(out=lncol[:, 10:11],
                                           in_=lncol[:, 9:10],
                                           func=AF.Square), mark="s")
        p.on("s", lambda s_: s_.activation(out=lncol[:, 11:12],
                                           in_=lncol[:, 9:10],
                                           func=AF.Square), mark="s")
        if ctx is None:
            p.wt("y", "v"); p.wt("y", "s")
            p.wt("y", "ds")
            p.wt("y", "dg")
            p.replay(block)
            for g in p.guards:
                g.__exit__(None, None, None)
    for g in reversed(_stage_guards):
        g.__exit__(None, None, None)
    return nc


def _stage4_inputs(x3_full, w4):
    f32 = np.float32
    r8 = lambda w, n: np.ascontiguousarray(
        np.asarray(w, f32).reshape(8, 128, n).transpose(1, 0, 2)
        .reshape(128, 8 * n))
    common = dict(
        Wq1r=r8(w4["Wq1"], D), Wq2r=r8(w4["Wq2"], D),
        Wk1r=r8(w4["Wk1"], 128), Wk2r=r8(w4["Wk2"], 128),
        Wvr=r8(w4["Wv"], 128), Woutr=r8(w4["Wout"], D),
        boutrow=np.asarray(w4["bout"], f32).reshape(1, D),
        lqk=np.ascontiguousarray(np.stack(
            [np.asarray(w4[k], f32) for k in ("lq1", "lk1", "lq2", "lk2")],
            axis=1)),
        ident=np.eye(128).astype(BF),
        ones1=np.ones((1, 128), f32))
    div = np.exp(np.arange(0, DH, 2, dtype=np.float64) *
                 (-np.log(10000.0) / DH))
    maps = []
    for c in range(NCORE):
        lo = c * 1024
        pos = np.arange(lo, lo + NT4 * 128, dtype=np.float64)
        sin = np.sin(pos[:, None] * div)
        cos = np.cos(pos[:, None] * div)
        t16 = lambda a: np.ascontiguousarray(
            np.tile(a, (1, H4)).astype(f32))
        m = dict(common)
        m.update(xg=np.ascontiguousarray(x3_full[lo:lo + NT4 * 128]),
                 xr3=np.ascontiguousarray(x3_full[c * S:(c + 1) * S]),
                 t4c=t16(cos), t4s=t16(sin),
                 t4cq=t16(cos * 0.125), t4sq=t16(sin * 0.125))
        maps.append({k: np.ascontiguousarray(v) if k != "ident" else v
                     for k, v in m.items()})
    return maps


def run_stage4(x3_full, w4):
    nc = build_stage4()
    maps = _stage4_inputs(x3_full, w4)
    res = run_bass_kernel_spmd(nc, maps, core_ids=list(range(NCORE)))
    return np.concatenate([np.asarray(r["x4"]) for r in res.results], axis=0)

S1_KEYS = [("xs", [S, D]), ("xh", [128, D]), ("hsc", [128, 1]),
           ("lnw", [1, D]), ("lnb", [1, D]), ("Win", [128, 8 * 2 * DI]),
           ("convw", [128, NM * 4]), ("convb", [128, NM]),
           ("Wxp", [128, NM * 96]), ("Wdt", [DTR, DI]), ("bdt", [128, NM]),
           ("Alog", [128, NM * NST]), ("Dvec", [128, NM]),
           ("Wout", [128, NM * D])]
LL_KEYS = [("wqkv_r", [128, 8 * 3 * INNER]), ("qs_row", [1, INNER]),
           ("ks_row", [1, INNER]), ("wo_r", [128, 4 * D]),
           ("lng", [1, D]), ("lnb2", [1, D]),
           ("w1a_r", [MC, 128, 8 * 128]), ("w1g_r", [MC, 128, 8 * 128]),
           ("b1a_c", [128, MC]), ("b1g_c", [128, MC]),
           ("w2_r", [128, MC * D]), ("b2row", [1, D])]
LL_SHARED = [("tq_cs", [EXTT, INNER]), ("tq_ss", [EXTT, INNER]),
             ("tk_cs", [EXTT, INNER]), ("tk_ss", [EXTT, INNER]),
             ("msk", [5, 128, 640]), ("hsc2", [1, 1]), ("hscH", [128, 1])]
S4_KEYS = [("Wq1r", [128, 8 * D]), ("Wq2r", [128, 8 * D]),
           ("Wk1r", [128, 8 * 128]), ("Wk2r", [128, 8 * 128]),
           ("Wvr", [128, 8 * 128]), ("Woutr", [128, 8 * D]),
           ("boutrow", [1, D]), ("lqk", [DTR, 4]),
           ("t4c", [NT4 * 128, 512]), ("t4s", [NT4 * 128, 512]),
           ("t4cq", [NT4 * 128, 512]), ("t4sq", [NT4 * 128, 512])]


def build_fused():
    nc = bass.Bass("TRN2", target_bir_lowering=False, debug=False,
                   num_devices=NCORE)
    din = lambda nm, sh, dt=F32: nc.dram_tensor(nm, sh, dt,
                                                kind="ExternalInput").ap()
    ident_ap = nc.dram_tensor("ident", [128, 128], BF16,
                              kind="ExternalInput").ap()
    ones_ap = din("ones1", [1, 128])
    io_s1 = {k: din("s1_" + k, sh) for k, sh in S1_KEYS}
    io_s1.update(ident=ident_ap, ones1=ones_ap)
    shared_ll = {k: din(k, sh) for k, sh in LL_SHARED}
    io_ll = {}
    for pfx in ("l0a_", "l0b_", "l1a_", "l1b_"):
        d = {k: din(pfx + k, sh) for k, sh in LL_KEYS}
        d.update(shared_ll)
        d.update(ident=ident_ap, ones1=ones_ap)
        io_ll[pfx] = d
    io_s4 = {k: din("s4_" + k, sh) for k, sh in S4_KEYS}
    io_s4.update(ident=ident_ap, ones1=ones_ap)
    out = nc.dram_tensor("out", [S, D], F32, kind="ExternalOutput").ap()

    x1_d = nc.dram_tensor("x1_d", [S, D], F32)
    t1_d = nc.dram_tensor("t1_d", [S, D], F32)
    x2_d = nc.dram_tensor("x2_d", [S, D], F32)
    x3_d = nc.dram_tensor("x3_d", [S, D], F32)
    t2_d = nc.dram_tensor("t2_d", [S, D], F32)
    hals = []
    for i in range(4):
        hin = nc.dram_tensor(f"hal{i}_in", [1280, D], BF16)
        hout = nc.dram_tensor(f"hal{i}_out", [NCORE * 1280, D], BF16,
                              addr_space="Shared")
        hals.append((hin, hout))
    x3g_in = nc.dram_tensor("x3g_in", [S, D], BF16)
    x3g_out = nc.dram_tensor("x3g_out", [NCORE * S, D], BF16,
                             addr_space="Shared")

    psA = nc.alloc_psum_tensor("psA", [128, 512], F32)
    psB = nc.alloc_psum_tensor("psB", [128, 512], F32)
    psC = nc.alloc_psum_tensor("psC", [128, 512], F32)
    psO = nc.alloc_psum_tensor("psO", [128, 512], F32)
    psT = nc.alloc_psum_tensor("psT", [128, 1024], BF16)
    ps_sc = nc.alloc_psum_tensor("ps_sc", [128, 640], F32)

    with nc.Block() as block:
        nc.cache_partition_id()
        p = P(nc)
        last_rd = {id(b): ("v", 0) for b in (psA, psB, psC, psO, psT, ps_sc)}
        base = dict(nc=nc, p=p, last_rd=last_rd, psA=psA, psB=psB, psC=psC,
                    psO=psO, ps_o=psO, psT=psT, ps_sc=ps_sc, regcache={})

        def join():
            for e in ("t", "v", "s", "g", "y"):
                for sem in ("pe", "v", "s", "ds", "dg", "c"):
                    p.wt(e, sem)

        def halo_ex(src_d, pair):
            hin, hout = pair
            p.wt("g", "ds"); p.wt("g", "v"); p.wt("g", "s")
            p.dma("g", lambda g: g.dma_start(out=hin[:, :],
                                             in_=src_d[S - 1280:S, :]))
            p.wt("g", "dg")
            p.on("g", lambda g: g.collective_compute(
                "AllGather", ALU.bypass, replica_groups=[list(range(NCORE))],
                ins=[hin[:, :]], outs=[hout[:, :]]), mark="c")
            p.wt("g", "c"); p.wt("y", "c")

        build_stage1(ctx=dict(base, tag="s1_", io=io_s1, x_out=x1_d))
        join()
        halo_ex(x1_d, hals[0])
        build_local_layer(False, ctx=dict(base, tag="l0a_", io=io_ll["l0a_"],
                                          x_in=x1_d, hal=hals[0][1],
                                          x_out=t1_d))
        join()
        halo_ex(t1_d, hals[1])
        build_local_layer(True, ctx=dict(base, tag="l0b_", io=io_ll["l0b_"],
                                         x_in=t1_d, hal=hals[1][1],
                                         xres=x1_d, x_out=x2_d))
        join()
        p.wt("g", "ds"); p.wt("g", "v"); p.wt("g", "s")
        p.dma("g", lambda g: g.dma_start(out=x3g_in[:, :], in_=x2_d[:, :]))
        p.wt("g", "dg")
        p.on("g", lambda g: g.collective_compute(
            "AllGather", ALU.bypass, replica_groups=[list(range(NCORE))],
            ins=[x3g_in[:, :]], outs=[x3g_out[:, :]]), mark="c")
        p.wt("g", "c"); p.wt("y", "c")
        build_stage4(ctx=dict(base, tag="s4_", io=io_s4, x_in=x2_d,
                              x3g=x3g_out, x_out=x3_d))
        join()
        halo_ex(x3_d, hals[2])
        build_local_layer(False, ctx=dict(base, tag="l1a_", io=io_ll["l1a_"],
                                          x_in=x3_d, hal=hals[2][1],
                                          x_out=t2_d))
        join()
        halo_ex(t2_d, hals[3])
        build_local_layer(True, ctx=dict(base, tag="l1b_", io=io_ll["l1b_"],
                                         x_in=t2_d, hal=hals[3][1],
                                         xres=x3_d, x_out=out))
        join()
        p.replay(block)
        for g in p.guards:
            g.__exit__(None, None, None)
    return nc


def _fused_inputs(inputs):
    x = np.ascontiguousarray(np.asarray(inputs["L0_attn_x"], np.float32)[0])
    w1m = {k[8:]: inputs[k] for k in inputs
           if k.startswith("L0_attn_") and k != "L0_attn_x"}
    l0f = {k[7:]: np.asarray(inputs[k]) for k in inputs
           if k.startswith("L0_ffn_")}
    w4 = {k[8:]: np.asarray(inputs[k]) for k in inputs
          if k.startswith("L1_attn_")}
    l1f = {k[7:]: np.asarray(inputs[k]) for k in inputs
           if k.startswith("L1_ffn_")}
    lsel = lambda lf, i: {k: v[i] for k, v in lf.items()}
    s1 = _stage1_inputs(x, w1m)
    zeros = np.zeros((T_FULL, D), np.float32)
    lls = {pfx: _layer_inputs(zeros, lw, False)
           for pfx, lw in (("l0a_", lsel(l0f, 0)), ("l0b_", lsel(l0f, 1)),
                           ("l1a_", lsel(l1f, 0)), ("l1b_", lsel(l1f, 1)))}
    s4 = _stage4_inputs(zeros, w4)
    shared_keys = {k for k, _ in LL_SHARED}
    maps = []
    for c in range(NCORE):
        m = {}
        for k, v in s1[c].items():
            if k in ("ident", "ones1"):
                m[k] = v
            else:
                m["s1_" + k] = v
        for pfx in ("l0a_", "l0b_", "l1a_", "l1b_"):
            for k, v in lls[pfx][c].items():
                if k in ("ident", "ones1", "xw", "xres"):
                    continue
                if k in shared_keys:
                    m[k] = v
                else:
                    m[pfx + k] = v
        m["hscH"] = np.full((128, 1), 0.0 if c == 0 else 1.0, np.float32)
        for k, v in s4[c].items():
            if k in ("ident", "ones1", "xg", "xr3"):
                continue
            m["s4_" + k] = v
        maps.append(m)
    return maps


_FUSED_NC = []


def kernel_fused(**inputs):
    if not _FUSED_NC:
        _FUSED_NC.append(build_fused())
    nc = _FUSED_NC[0]
    maps = _fused_inputs(inputs)
    res = run_bass_kernel_spmd(nc, maps, core_ids=list(range(NCORE)))
    outp = np.concatenate([np.asarray(r["out"]) for r in res.results], axis=0)
    return outp.reshape(1, T_FULL, D)


# version 12 (final)
# speedup vs baseline: 1.0952x; 1.0002x over previous
import contextlib
import numpy as np
import ml_dtypes
import concourse.bass as bass
import concourse.mybir as mybir
from concourse.bass_utils import run_bass_kernel_spmd

F32 = mybir.dt.float32
BF16 = mybir.dt.bfloat16
AF = mybir.ActivationFunctionType
ALU = mybir.AluOpType
AX = mybir.AxisListType.X
BF = ml_dtypes.bfloat16
ET = mybir.EngineType

NCORE = 8
T_FULL = 16384
S = 2048
D = 1024
DI = 2048
NST = 16
DTR = 64
TC = 512
NCH = S // TC
NM = DI // 128
EPS = 1e-5
DEBUG = False
DEBUG2 = False
FDBG = False
FSTAGES = None
SER_A1 = False
SER_DELTA = False
SER_A2 = False
SER_B = False


class P:

    ENGS = ("t", "v", "s", "g", "y")

    def __init__(self, nc):
        self.nc = nc
        self.th = {e: [] for e in P.ENGS}
        self.sem = {}
        self.ct = {}
        self.guards = []
        for n in ("dg", "ds", "pe", "v", "s", "c"):
            g = nc.semaphore(n)
            self.sem[n] = g.__enter__()
            self.guards.append(g)
            self.ct[n] = 0

    def on(self, e, fn, mark=None, k=1):
        if mark is not None:
            self.ct[mark] += k
        sem = self.sem.get(mark) if mark is not None else None

        def thunk(eng, fn=fn, sem=sem, k=k):
            ins = fn(eng)
            if sem is not None:
                ins.then_inc(sem, k)
        self.th[e].append(thunk)
        return self.ct[mark] if mark is not None else None

    def dma(self, e, fn):
        return self.on(e, fn, mark="ds" if e == "y" else "dg", k=16)

    def wt(self, e, name, val=None):
        v = self.ct[name] if val is None else val
        sem = self.sem[name]
        self.th[e].append(lambda eng, sem=sem, v=v: eng.wait_ge(sem, v))

    def replay(self, block):
        for e, deco in (("y", block.sync), ("g", block.gpsimd),
                        ("t", block.tensor), ("v", block.vector),
                        ("s", block.scalar)):
            def runner(eng, e=e):
                for thunk in self.th[e]:
                    thunk(eng)
            deco(runner)


def ln_stats(p, xt_ap, scratch, cols, width, eps=EPS):
    p.wt("s", "v")
    p.on("v", lambda v: v.reduce_sum(out=cols["sum"], in_=xt_ap, axis=AX),
         mark="v")
    p.on("s", lambda s: s.activation(out=scratch, in_=xt_ap, func=AF.Square,
                                     accum_out=cols["ssq"]), mark="s")
    p.wt("v", "s")
    p.wt("v", "v")
    p.on("v", lambda v: v.tensor_scalar_mul(out=cols["mean"], in0=cols["sum"],
                                            scalar1=1.0 / width))
    p.on("v", lambda v: v.tensor_scalar_mul(out=cols["nmean"],
                                            in0=cols["sum"],
                                            scalar1=-1.0 / width))
    p.on("v", lambda v: v.tensor_scalar(out=cols["ex2"], in0=cols["ssq"],
                                        scalar1=1.0 / width, scalar2=eps,
                                        op0=ALU.mult, op1=ALU.add), mark="v")
    p.wt("v", "v")
    p.on("v", lambda v: v.scalar_tensor_tensor(
        out=cols["nvar"], in0=cols["mean"], scalar=cols["mean"],
        in1=cols["ex2"], op0=ALU.mult, op1=ALU.subtract), mark="v")
    p.wt("s", "v")
    p.on("s", lambda s: s.activation(out=cols["lnv"], in_=cols["nvar"],
                                     func=AF.Ln, scale=-1.0))
    p.on("s", lambda s: s.activation(out=cols["rstd"], in_=cols["lnv"],
                                     func=AF.Exp, scale=-0.5), mark="s")
    p.wt("v", "s")
    p.on("v", lambda v: v.tensor_scalar(
        out=cols["nmr"], in0=cols["nmean"], scalar1=cols["rstd"],
        scalar2=None, op0=ALU.mult), mark="v")
    p.wt("v", "v")



def _gp_prev(nc, ctx):
    rc = ctx.get("regcache")
    if rc is None:
        rc = {}
        ctx["regcache"] = rc
    if "prev" not in rc:
        cid = nc.partition_id([ET.SP])
        rc["cid"] = cid
        rc["prev"] = (cid + 7) % 8
    return rc["prev"]


def _gp_cid(nc, ctx):
    _gp_prev(nc, ctx)
    return ctx["regcache"]["cid"]


def build_stage1(ctx=None):
    tag = "" if ctx is None else ctx["tag"]
    if ctx is None:
        nc = bass.Bass("TRN2", target_bir_lowering=False, debug=False,
                       num_devices=NCORE)
        din = lambda nm, sh, dt=F32: nc.dram_tensor(nm, sh, dt,
                                                    kind="ExternalInput").ap()
    else:
        nc = ctx["nc"]
        din = lambda nm, sh, dt=F32: ctx["io"][nm]
    xs = din("xs", [S, D])
    xh = din("xh", [128, D])
    hsc = din("hsc", [128, 1])
    lnw = din("lnw", [1, D]); lnb = din("lnb", [1, D])
    Win = din("Win", [128, 8 * 2 * DI])
    convw = din("convw", [128, NM * 4]); convb = din("convb", [128, NM])
    Wxp = din("Wxp", [128, NM * 96])
    Wdt = din("Wdt", [DTR, DI]); bdt = din("bdt", [128, NM])
    Alog = din("Alog", [128, NM * NST]); Dvec = din("Dvec", [128, NM])
    Wout = din("Wout", [128, NM * D])
    if ctx is None:
        ident_in = nc.dram_tensor("ident", [128, 128], BF16,
                                  kind="ExternalInput").ap()
        ones_in = din("ones1", [1, 128])
        x1 = nc.dram_tensor("x1", [S, D], F32, kind="ExternalOutput").ap()
        x1b = nc.dram_tensor("x1b", [S, D], F32,
                             kind="ExternalOutput").ap()
        xr_d = nc.dram_tensor("xr_d", [S, D], F32,
                              kind="ExternalOutput").ap()
        yf_d = nc.dram_tensor("yf_d", [NCH * 128, NM * TC], BF16,
                              kind="ExternalOutput").ap()
    else:
        ident_in = ctx["io"]["ident"]
        ones_in = ctx["io"]["ones1"]
        x1 = ctx["x_out"]
        x1b = nc.dram_tensor(tag + "x1b", [S, D], F32)
        xr_d = nc.dram_tensor(tag + "xr_d", [S, D], F32)
        yf_d = nc.dram_tensor(tag + "yf_d", [NCH * 128, NM * TC], BF16)
        ctx["xr_d"] = xr_d; ctx["yf_d"] = yf_d
    dbg = {}
    if DEBUG:
        for nm, sh, dt in [("dbg_normT", [128, 8 * TC], BF16),
                           ("dbg_u", [128, NM * TC], BF16),
                           ("dbg_dT", [128, NM * TC], BF16),
                           ("dbg_dt", [DTR, TC], BF16),
                           ("dbg_BC", [32, TC], F32),
                           ("dbg_hfin", [128, 256], F32),
                           ("dbg_decay", [128, 256], F32),
                           ("dbg_hstate", [128, 256], F32),
                           ("dbg_ball", [128, NST * TC], BF16),
                           ("dbg_Bbc", [128, NST * TC], BF16),
                           ("dbg_aall", [128, NST * TC], BF16),
                           ("dbg_hall", [128, NST * TC], BF16),
                           ("dbg_yfin", [128, NM * TC], BF16),
                           ("dbg_hs4", [NCH, 128, NM * NST], F32),
                           ("dbg_yf4", [NCH, 128, NM * TC], BF16),
                           ("dbg_xr", [128, 4 * D], F32)]:
            dbg[nm] = nc.dram_tensor(nm, sh, dt, kind="ExternalOutput").ap()

    uT_sp = nc.dram_tensor(tag + "uT_sp", [128, NM, S], BF16)
    szT_sp = nc.dram_tensor(tag + "szT_sp", [128, NM, S], BF16)
    dT_sp = nc.dram_tensor(tag + "dT_sp", [128, NM, S], BF16)
    b_sp = nc.dram_tensor(tag + "b_sp", [NCH, NM, 128, NST * TC], BF16)
    BC_d = nc.dram_tensor(tag + "BC_d", [32, S], F32)
    cc_in = nc.dram_tensor(tag + "cc_in", [256, NM * NST], F32)
    cc_out = nc.dram_tensor(tag + "cc_out", [256 * NCORE, NM * NST], F32,
                            addr_space="Shared")
    g_bounce = nc.dram_tensor(tag + "g_bounce", [128, NCORE * 256], F32)

    _stage_guards = []

    def sb(nm, sh, dt=F32):
        g = nc.sbuf_tensor(tag + nm, sh, dt)
        h = g.__enter__()
        _stage_guards.append(g)
        return h
    ident = sb("identb", [128, 128], BF16)
    ones1 = sb("ones1b", [1, 128])
    cw_sb = sb("cw_sb", [128, NM * 4]); cb_sb = sb("cb_sb", [128, NM])
    bdt_sb = sb("bdt_sb", [128, NM]); D_sb = sb("D_sb", [128, NM])
    hsc_sb = sb("hsc_sb", [128, 1])
    negA = sb("negA", [128, NM * NST]); A_sb = sb("A_sb", [128, NM * NST])
    AdT = sb("AdT", [128, NM * NST])
    dcum_start = sb("dcum_start", [128, NCH * NM])
    dc_carry = sb("dc_carry", [128, NM])
    hfin = sb("hfin", [128, NM * NST]); decay = sb("decay", [128, NM * NST])
    hacc = sb("hacc", [128, NM * NST]); h_state = sb("h_state", [128, NM * NST])
    lnsc = sb("lnsc", [128, D]); lncol = sb("lncol", [128, 12])
    padv = sb("padv", [128, 384]); pads = sb("pads", [128, 96])
    ones_tc = sb("ones_tc", [128, TC])
    dcum_f = sb("dcum_f", [128, 2 * TC])
    bcrow = sb("bcrow", [1, 2 * TC])

    if ctx is None:
        ps0 = nc.alloc_psum_tensor("ps0", [128, 512], F32)
        ps1 = nc.alloc_psum_tensor("ps1", [128, 512], F32)
        psT = nc.alloc_psum_tensor("psT", [128, 1024], BF16)
        ps2 = nc.alloc_psum_tensor("ps2", [128, 512], F32)
        ps3 = nc.alloc_psum_tensor("ps3", [128, 512], F32)
    else:
        ps0, ps1, ps2, ps3, psT = (ctx["psA"], ctx["psB"], ctx["psC"],
                                   ctx["psO"], ctx["psT"])
    psmm = [ps0, ps1]

    with (nc.Block() if ctx is None else contextlib.nullcontext()) as block:
        p = P(nc) if ctx is None else ctx["p"]

        def vpad():
            p.on("v", lambda v: v.memset(padv[:], 0.0), mark="v")

        def spad():
            p.on("s", lambda s_: s_.activation(out=pads[:],
                                               in_=padv[:, 0:pads.shape[1]],
                                               func=AF.Square), mark="s")
        cols = dict(sum=lncol[:, 0:1], ssq=lncol[:, 1:2], mean=lncol[:, 2:3],
                    nmean=lncol[:, 3:4], ex2=lncol[:, 4:5], nvar=lncol[:, 5:6],
                    lnv=lncol[:, 6:7], rstd=lncol[:, 7:8], nmr=lncol[:, 8:9])
        last_rd = ({id(b): ("v", 0) for b in (ps0, ps1, psT, ps2, ps3)}
                   if ctx is None else ctx["last_rd"])

        def psum_guard(bank):
            p.wt("t", last_rd[id(bank)][0], last_rd[id(bank)][1])

        def set_rd(bank, eng):
            last_rd[id(bank)] = (eng, p.ct[eng])

        def ln_apply_tile(xt_ap, out_bf):
            p.wt("v", "pe")
            p.on("v", lambda v: v.tensor_scalar(
                out=tnorm[:], in0=xt_ap, scalar1=cols["rstd"],
                scalar2=cols["nmr"], op0=ALU.mult, op1=ALU.add), mark="v")
            p.on("v", lambda v: v.tensor_tensor(out=lnsc[:], in0=tnorm[:],
                                                in1=lnw_b[:], op=ALU.mult))
            p.on("v", lambda v, o=out_bf: v.tensor_tensor(
                out=o, in0=lnsc[:], in1=lnb_b[:], op=ALU.add), mark="v")

        def transposes(src_bf, nj=8):
            p.wt("t", "v")
            psum_guard(psT)
            for j in range(nj):
                p.on("t", lambda t, j=j: t.transpose(
                    out=psT[:, j * 128:(j + 1) * 128],
                    in_=src_bf[:, j * 128:(j + 1) * 128], identity=ident[:]),
                    mark="pe" if j == nj - 1 else None)
            p.wt("s", "pe")

        with contextlib.ExitStack() as a1st:
            a1sb = lambda nm, sh, dt=F32: a1st.enter_context(
                nc.sbuf_tensor(tag + nm, sh, dt))
            Win_bf = a1sb("Win_bf", [128, 8 * 2 * DI], BF16)
            lnw_b = a1sb("lnw_b", [128, D])
            lnb_b = a1sb("lnb_b", [128, D])
            normT = a1sb("normT", [128, 8 * TC], BF16)
            arena = a1sb("arena", [128, NM * 515], BF16)
            uT_bf = a1sb("uT_bf", [128, NM * TC], BF16)
            dT_bf = a1sb("dT_bf", [128, NM * TC], BF16)
            Wxp_bf = a1sb("Wxp_bf", [128, NM * 96], BF16)
            Wdt_bf = a1sb("Wdt_bf", [DTR, DI], BF16)
            dt_bf = a1sb("dt_bf", [DTR, S], BF16)
            BC_f = a1st.enter_context(nc.sbuf_tensor(tag + "BC_f", [32, S], F32))
            xt = a1st.enter_context(nc.sbuf_tensor(tag + "xt", [128, 2 * D], F32))
            tnorm = a1st.enter_context(nc.sbuf_tensor(tag + "tnorm", [128, D], F32))
            norm_bf = a1sb("norm_bf", [128, D], BF16)
            hnT = a1sb("hnT", [128, 8 * 3], BF16)
            halo_sav = a1st.enter_context(
                nc.sbuf_tensor(tag + "halo_sav", [128, NM * 3], F32))
            cbuf0 = a1st.enter_context(
                nc.sbuf_tensor(tag + "cbuf0", [128, 2 * TC], F32))
            cbuf1 = a1st.enter_context(
                nc.sbuf_tensor(tag + "cbuf1", [128, 2 * TC], F32))
            spl0 = a1st.enter_context(
                nc.sbuf_tensor(tag + "spl0", [128, 2 * TC], F32))
            spl1 = a1st.enter_context(
                nc.sbuf_tensor(tag + "spl1", [128, 2 * TC], F32))
            spl2 = a1st.enter_context(nc.sbuf_tensor(tag + "spl2", [128, TC], F32))
            sz_small = a1sb("sz_small", [128, 2 * TC], BF16)

            p.dma("g", lambda g: g.dma_start(out=Win_bf[:], in_=Win[:, :]))
            p.dma("g", lambda g: g.dma_start(out=Wxp_bf[:], in_=Wxp[:, :]))
            p.dma("g", lambda g: g.dma_start(out=Wdt_bf[:], in_=Wdt[:, :]))
            p.dma("g", lambda g: g.dma_start(out=ident[:], in_=ident_in[:, :]))
            for dst, src in [(cw_sb, convw), (cb_sb, convb), (bdt_sb, bdt),
                             (D_sb, Dvec), (hsc_sb, hsc), (ones1, ones_in),
                             (AdT, Alog)]:
                p.dma("y", lambda y, d=dst, s_=src: y.dma_start(out=d[:],
                                                                in_=s_[:, :]))
            p.dma("y", lambda y: y.dma_start(out=lnw_b[0:1, :], in_=lnw[:, :]))
            p.dma("y", lambda y: y.dma_start(out=lnb_b[0:1, :], in_=lnb[:, :]))
            p.wt("v", "ds")
            p.on("v", lambda v: v.memset(ones_tc[:], 1.0))
            p.on("v", lambda v: v.memset(h_state[:], 0.0))
            p.on("v", lambda v: v.memset(hfin[:], 0.0))
            p.on("v", lambda v: v.memset(dc_carry[:], 0.0), mark="v")
            p.wt("s", "ds")
            p.on("s", lambda s: s.activation(out=negA[:], in_=AdT[:],
                                             func=AF.Exp), mark="s")
            p.wt("v", "s")
            p.on("v", lambda v: v.tensor_scalar_mul(out=A_sb[:], in0=negA[:],
                                                    scalar1=-1.0), mark="v")
            p.wt("t", "ds")
            for dstb in (lnw_b, lnb_b):
                for half in range(2):
                    sl = slice(half * 512, (half + 1) * 512)
                    psum_guard(ps2)
                    p.on("t", lambda t, d=dstb, sl=sl: t.matmul(
                        out=ps2[:], lhsT=ones1[:, :], rhs=d[0:1, sl],
                        start=True, stop=True), mark="pe")
                    p.wt("v", "pe")
                    p.on("v", lambda v, d=dstb, sl=sl: v.tensor_copy(
                        out=d[:, sl], in_=ps2[:]), mark="v")
                    set_rd(ps2, "v")

            p.dma("y", lambda y: y.dma_start(out=xt[:, 0:D], in_=xh[:, :]))
            p.wt("v", "ds"); p.wt("s", "ds")
            ln_stats(p, xt[:, 0:D], lnsc[:], cols, D)
            ln_apply_tile(xt[:, 0:D], norm_bf[:])
            transposes(norm_bf)
            for j in range(8):
                p.on("s", lambda s, j=j: s.copy(
                    out=hnT[:, j * 3:(j + 1) * 3],
                    in_=psT[:, j * 128 + 125:j * 128 + 128]), mark="s")
            set_rd(psT, "s")
            p.wt("t", "dg"); p.wt("t", "s")
            for m in range(NM):
                psum_guard(ps2)
                for k in range(8):
                    p.on("t", lambda t, m=m, k=k: t.matmul(
                        out=ps2[:, 0:3],
                        lhsT=Win_bf[:, k * 2 * DI + m * 128:
                                    k * 2 * DI + (m + 1) * 128],
                        rhs=hnT[:, k * 3:(k + 1) * 3],
                        start=(k == 0), stop=(k == 7)),
                        mark="pe" if k == 7 else None)
                p.wt("v", "pe")
                p.on("v", lambda v, m=m: v.tensor_scalar(
                    out=halo_sav[:, m * 3:(m + 1) * 3], in0=ps2[:, 0:3],
                    scalar1=hsc_sb[:, 0:1], scalar2=None, op0=ALU.mult),
                    mark="v")
                set_rd(ps2, "v")

            s_after_silu = {}
            for ck in range(NCH):
                p.wt("v", "ds"); p.wt("s", "ds")
                p.wt("v", "pe"); p.wt("s", "pe")
                for m in range(NM):
                    p.on("v", lambda v, m=m: v.tensor_copy(
                        out=arena[:, m * 515:m * 515 + 3],
                        in_=halo_sav[:, m * 3:(m + 1) * 3]),
                        mark="v" if m == NM - 1 else None)
                for kt in range(4):
                    tile = 4 * ck + kt
                    sl_x = slice((kt % 2) * D, (kt % 2) * D + D)
                    p.wt("y", "v"); p.wt("y", "s")
                    p.dma("y", lambda y, t_=tile, sl=sl_x: y.dma_start(
                        out=xt[:, sl], in_=xs[t_ * 128:(t_ + 1) * 128, :]))
                    p.wt("v", "ds"); p.wt("s", "ds")
                    ln_stats(p, xt[:, sl_x], lnsc[:], cols, D)
                    ln_apply_tile(xt[:, sl_x], norm_bf[:])
                    transposes(norm_bf)
                    for j in range(8):
                        p.on("s", lambda s, j=j, kt=kt: s.copy(
                            out=normT[:, j * TC + kt * 128:
                                      j * TC + kt * 128 + 128],
                            in_=psT[:, j * 128:(j + 1) * 128]), mark="s")
                    set_rd(psT, "s")
                p.wt("t", "s")
                for m in range(32):
                    bank = psmm[m % 2]
                    psum_guard(bank)
                    if SER_A1:
                        p.wt("t", "s")
                    for k in range(8):
                        p.on("t", lambda t, m=m, k=k, b=bank: t.matmul(
                            out=b[:],
                            lhsT=Win_bf[:, k * 2 * DI + m * 128:
                                        k * 2 * DI + (m + 1) * 128],
                            rhs=normT[:, k * TC:(k + 1) * TC],
                            start=(k == 0), stop=(k == 7)),
                            mark="pe" if k == 7 else None)
                    if m < NM:
                        gm1 = ck * NM + m
                        sm1 = m % 2
                        c0 = cbuf0[:, sm1 * TC:(sm1 + 1) * TC]
                        c1 = cbuf1[:, sm1 * TC:(sm1 + 1) * TC]
                        p.wt("v", "pe")
                        if gm1 >= 2:
                            p.wt("v", "s", s_after_silu[gm1 - 2])
                        p.on("v", lambda v, m=m, b=bank: v.tensor_copy(
                            out=arena[:, m * 515 + 3:m * 515 + 515],
                            in_=b[:]), mark="v")
                        set_rd(bank, "v")
                        p.on("v", lambda v, m=m, c0=c0: v.tensor_scalar(
                            out=c0, in0=arena[:, m * 515:m * 515 + TC],
                            scalar1=cw_sb[:, m * 4:m * 4 + 1], scalar2=None,
                            op0=ALU.mult))
                        for j in range(1, 4):
                            si = c0 if j % 2 == 1 else c1
                            so = c1 if j % 2 == 1 else c0
                            p.on("v", lambda v, m=m, j=j, si=si, so=so:
                                 v.scalar_tensor_tensor(
                                     out=so,
                                     in0=arena[:, m * 515 + j:
                                               m * 515 + j + TC],
                                     scalar=cw_sb[:, m * 4 + j:m * 4 + j + 1],
                                     in1=si, op0=ALU.mult, op1=ALU.add),
                                 mark="v" if j == 3 else None)
                        p.on("v", lambda v, m=m: v.tensor_copy(
                            out=halo_sav[:, m * 3:(m + 1) * 3],
                            in_=arena[:, m * 515 + 512:m * 515 + 515]),
                            mark="v")
                        p.wt("s", "v")
                        p.on("s", lambda s, m=m, c1=c1: s.activation(
                            out=uT_bf[:, m * TC:(m + 1) * TC], in_=c1,
                            func=AF.Silu, bias=cb_sb[:, m:m + 1]), mark="s")
                        s_after_silu[gm1] = p.ct["s"]
                        spad(); p.wt("y", "s")
                        p.dma("y", lambda y, m=m, ck=ck: y.dma_start(
                            out=uT_sp[:, m, ck * TC:(ck + 1) * TC],
                            in_=uT_bf[:, m * TC:(m + 1) * TC]))
                    else:
                        mz = m - NM
                        slz = slice((mz % 2) * TC, (mz % 2) * TC + TC)
                        p.wt("s", "pe")
                        p.wt("s", "ds")
                        p.on("s", lambda s, b=bank, sl=slz: s.activation(
                            out=sz_small[:, sl], in_=b[:], func=AF.Silu),
                            mark="s")
                        set_rd(bank, "s")
                        spad(); p.wt("y", "s")
                        p.dma("y", lambda y, mz=mz, ck=ck, sl=slz:
                              y.dma_start(
                                  out=szT_sp[:, mz, ck * TC:(ck + 1) * TC],
                                  in_=sz_small[:, sl]))
                psum_guard(ps2)
                p.wt("t", "s")
                for m in range(NM):
                    p.on("t", lambda t, m=m: t.matmul(
                        out=ps2[0:96, :],
                        lhsT=Wxp_bf[:, m * 96:(m + 1) * 96],
                        rhs=uT_bf[:, m * TC:(m + 1) * TC],
                        start=(m == 0), stop=(m == NM - 1)),
                        mark="pe" if m == NM - 1 else None)
                p.wt("v", "pe")
                p.on("v", lambda v, ck=ck: v.tensor_copy(
                    out=dt_bf[:, ck * TC:(ck + 1) * TC], in_=ps2[0:64, :]))
                p.on("v", lambda v, ck=ck: v.tensor_copy(
                    out=BC_f[:, ck * TC:(ck + 1) * TC], in_=ps2[64:96, :]),
                    mark="v")
                set_rd(ps2, "v")
                vpad(); p.wt("y", "v")
                p.dma("y", lambda y, ck=ck: y.dma_start(
                    out=BC_d[:, ck * TC:(ck + 1) * TC],
                    in_=BC_f[:, ck * TC:(ck + 1) * TC]))
                p.wt("t", "v")
                v_after_bias = {}
                sc_after_m = {}
                for m in range(NM):
                    sm = m % 2
                    bank = ps3 if sm == 0 else ps2
                    sp0 = spl0[:, sm * TC:(sm + 1) * TC]
                    sp1 = spl1[:, sm * TC:(sm + 1) * TC]
                    psum_guard(bank)
                    if SER_DELTA:
                        p.wt("t", "v"); p.wt("t", "s")
                    p.on("t", lambda t, m=m, ck=ck, b=bank: t.matmul(
                        out=b[:], lhsT=Wdt_bf[:, m * 128:(m + 1) * 128],
                        rhs=dt_bf[:, ck * TC:(ck + 1) * TC],
                        start=True, stop=True), mark="pe")
                    p.wt("v", "pe")
                    if SER_DELTA:
                        p.wt("v", "s")
                    elif m >= 2:
                        p.wt("v", "s", sc_after_m[m - 2])
                    p.on("v", lambda v, m=m, b=bank, o=sp0: v.tensor_scalar(
                        out=o, in0=b[:], scalar1=bdt_sb[:, m:m + 1],
                        scalar2=None, op0=ALU.add), mark="v")
                    v_after_bias[m] = p.ct["v"]
                    set_rd(bank, "v")
                    p.wt("s", "v", v_after_bias[m])
                    p.on("s", lambda s, i=sp0, o=sp1: s.activation(
                        out=o, in_=i, func=AF.Abs))
                    p.on("s", lambda s, i=sp1: s.activation(
                        out=spl2[:], in_=i, func=AF.Exp, scale=-1.0))
                    p.on("s", lambda s, o=sp1: s.activation(
                        out=o, in_=spl2[:], func=AF.Ln, bias=1.0),
                         mark="s")
                    sc_after_m[m] = p.ct["s"]
                    p.wt("v", "s")
                    p.on("v", lambda v, i=sp0: v.tensor_scalar_max(
                        out=cbuf0[:, 0:TC], in0=i, scalar1=0.0))
                    p.on("v", lambda v, m=m, i=sp1: v.tensor_tensor(
                        out=dT_bf[:, m * TC:(m + 1) * TC],
                        in0=cbuf0[:, 0:TC],
                        in1=i, op=ALU.add), mark="v")
                    p.on("v", lambda v, m=m, ck=ck: v.tensor_copy(
                        out=dcum_start[:, ck * NM + m:ck * NM + m + 1],
                        in_=dc_carry[:, m:m + 1]))
                    p.on("v", lambda v, m=m: v.reduce_sum(
                        out=dcum_f[:, 0:1], in_=dT_bf[:, m * TC:(m + 1) * TC],
                        axis=AX), mark="v")
                    p.wt("v", "v")
                    p.on("v", lambda v, m=m: v.tensor_tensor(
                        out=dcum_f[:, 1:2], in0=dc_carry[:, m:m + 1],
                        in1=dcum_f[:, 0:1], op=ALU.add), mark="v")
                    p.wt("v", "v")
                    p.on("v", lambda v, m=m: v.tensor_copy(
                        out=dc_carry[:, m:m + 1], in_=dcum_f[:, 1:2]),
                        mark="v")
                    p.wt("v", "v")
                    vpad(); p.wt("y", "v")
                    p.dma("y", lambda y, m=m, ck=ck: y.dma_start(
                        out=dT_sp[:, m, ck * TC:(ck + 1) * TC],
                        in_=dT_bf[:, m * TC:(m + 1) * TC]))
                if DEBUG and ck == 0:
                    p.wt("y", "v"); p.wt("y", "s")
                    p.dma("y", lambda y: y.dma_start(out=dbg["dbg_normT"][:, :],
                                                     in_=normT[:]))
                    p.dma("y", lambda y: y.dma_start(out=dbg["dbg_u"][:, :],
                                                     in_=uT_bf[:]))
                    p.dma("y", lambda y: y.dma_start(out=dbg["dbg_dT"][:, :],
                                                     in_=dT_bf[:]))
                    p.dma("y", lambda y: y.dma_start(out=dbg["dbg_dt"][:, :],
                                                     in_=dt_bf[:, 0:TC]))
                    p.dma("y", lambda y: y.dma_start(out=dbg["dbg_BC"][:, :],
                                                     in_=BC_f[:, 0:TC]))

        p.wt("v", "v")
        for m in range(NM):
            p.on("v", lambda v, m=m: v.tensor_scalar(
                out=AdT[:, m * NST:(m + 1) * NST],
                in0=A_sb[:, m * NST:(m + 1) * NST],
                scalar1=dc_carry[:, m:m + 1], scalar2=None, op0=ALU.mult),
                mark="v")
        p.wt("s", "v")
        for m in range(NM):
            p.on("s", lambda s, m=m: s.activation(
                out=decay[:, m * NST:(m + 1) * NST],
                in_=A_sb[:, m * NST:(m + 1) * NST], func=AF.Exp,
                scale=dc_carry[:, m:m + 1]), mark="s")

        with contextlib.ExitStack() as a2st:
            a2sb = lambda nm, sh, dt=BF16: a2st.enter_context(
                nc.sbuf_tensor(tag + nm, sh, dt))
            Bbc = a2sb("Bbc", [128, NST * TC])
            b_all = a2sb("b_all", [128, 2 * NST * TC])
            du_bf = a2sb("du_bf", [128, 2 * TC])
            w_bf = a2sb("w_bf", [128, 2 * NST * TC])
            sttout = a2sb("sttout", [128, TC])
            uT_rd = a2sb("uT_rd", [128, 2 * TC])
            dT_rd = a2sb("dT_rd", [128, 2 * TC])
            v_after_du = {}
            sc_after_exp = {}
            v_after_stt = {}
            ds_after_load = {}
            ds_after_spill = {}
            for ck in range(NCH):
                for n in range(NST):
                    slb = slice((n % 2) * TC, (n % 2) * TC + TC)
                    p.wt("y", "pe")
                    p.dma("y", lambda y, n=n, ck=ck, sl=slb: y.dma_start(
                        out=bcrow[0:1, sl],
                        in_=BC_d[n:n + 1, ck * TC:(ck + 1) * TC]))
                    bank = ps2 if n % 2 == 0 else ps3
                    psum_guard(bank)
                    p.wt("t", "ds")
                    p.on("t", lambda t, b=bank, sl=slb: t.matmul(
                        out=b[:], lhsT=ones1[:, :], rhs=bcrow[0:1, sl],
                        start=True, stop=True), mark="pe")
                    p.wt("v", "pe")
                    p.on("v", lambda v, n=n, b=bank: v.tensor_copy(
                        out=Bbc[:, n * TC:(n + 1) * TC], in_=b[:]), mark="v")
                    set_rd(bank, "v")
                for m in range(NM):
                    gm = ck * NM + m
                    slm = slice((m % 2) * TC, (m % 2) * TC + TC)
                    wsl = (gm % 2) * NST * TC
                    bsl = (gm % 2) * NST * TC
                    if SER_A2:
                        p.wt("y", "v")
                    elif gm >= 2:
                        p.wt("y", "v", v_after_du[gm - 2])
                    p.dma("y", lambda y, m=m, ck=ck, sl=slm: y.dma_start(
                        out=uT_rd[:, sl],
                        in_=uT_sp[:, m, ck * TC:(ck + 1) * TC]))
                    p.dma("y", lambda y, m=m, ck=ck, sl=slm: y.dma_start(
                        out=dT_rd[:, sl],
                        in_=dT_sp[:, m, ck * TC:(ck + 1) * TC]))
                    ds_after_load[gm] = p.ct["ds"]
                    if SER_A2:
                        p.wt("v", "ds"); p.wt("v", "s")
                    else:
                        p.wt("v", "ds", ds_after_load[gm])
                        if gm >= 2:
                            p.wt("v", "s", sc_after_exp[gm - 2])
                    p.on("v", lambda v, m=m, ck=ck, sl=slm:
                         v.tensor_tensor_scan(
                             out=dcum_f[:, sl], data0=ones_tc[:],
                             data1=dT_rd[:, sl],
                             initial=dcum_start[:, ck * NM + m:
                                                ck * NM + m + 1],
                             op0=ALU.mult, op1=ALU.add))
                    p.on("v", lambda v, sl=slm: v.tensor_tensor(
                        out=du_bf[:, sl], in0=dT_rd[:, sl],
                        in1=uT_rd[:, sl], op=ALU.mult), mark="v")
                    v_after_du[gm] = p.ct["v"]
                    if SER_A2:
                        p.wt("s", "v")
                    else:
                        p.wt("s", "v", v_after_du[gm])
                        if gm >= 2:
                            p.wt("s", "v", v_after_stt[gm - 2])
                    for n in range(NST):
                        mn = m * NST + n
                        p.on("s", lambda s, mn=mn, sl=slm, n=n, w0=wsl:
                             s.activation(
                                 out=w_bf[:, w0 + n * TC:w0 + (n + 1) * TC],
                                 in_=dcum_f[:, sl],
                                 func=AF.Exp, scale=negA[:, mn:mn + 1],
                                 bias=AdT[:, mn:mn + 1]),
                             mark="s" if n == NST - 1 else None)
                    sc_after_exp[gm] = p.ct["s"]
                    if SER_A2:
                        p.wt("v", "s"); p.wt("v", "ds")
                    else:
                        p.wt("v", "s", sc_after_exp[gm])
                        if gm >= 2:
                            p.wt("v", "ds", ds_after_spill[gm - 2])
                    for n in range(NST):
                        mn = m * NST + n
                        p.on("v", lambda v, n=n, sl=slm, b0=bsl:
                             v.tensor_tensor(
                                 out=b_all[:, b0 + n * TC:b0 + (n + 1) * TC],
                                 in0=du_bf[:, sl],
                                 in1=Bbc[:, n * TC:(n + 1) * TC],
                                 op=ALU.mult))
                        p.on("v", lambda v, mn=mn, n=n, w0=wsl, b0=bsl:
                             v.scalar_tensor_tensor(
                                 out=sttout[:], in0=w_bf[:, w0 + n * TC:
                                                         w0 + (n + 1) * TC],
                                 scalar=1.0,
                                 in1=b_all[:, b0 + n * TC:b0 + (n + 1) * TC],
                                 op0=ALU.mult, op1=ALU.mult,
                                 accum_out=hacc[:, mn:mn + 1]),
                             mark="v" if n == NST - 1 else None)
                    v_after_stt[gm] = p.ct["v"]
                    vpad(); p.wt("y", "v")
                    p.dma("y", lambda y, m=m, ck=ck, b0=bsl: y.dma_start(
                        out=b_sp[ck, m, :, :],
                        in_=b_all[:, b0:b0 + NST * TC]))
                    ds_after_spill[gm] = p.ct["ds"]
                    if DEBUG and ck == 0 and m == 0:
                        p.dma("y", lambda y, b0=bsl: y.dma_start(
                            out=dbg["dbg_ball"][:, :],
                            in_=b_all[:, b0:b0 + NST * TC]))
                        p.dma("y", lambda y: y.dma_start(
                            out=dbg["dbg_Bbc"][:, :], in_=Bbc[:]))
                p.wt("v", "v")
                p.on("v", lambda v: v.tensor_tensor(
                    out=hfin[:], in0=hfin[:], in1=hacc[:], op=ALU.add),
                    mark="v")

        p.on("v", lambda v: v.tensor_copy(out=hacc[:], in_=hfin[:]), mark="v")
        p.wt("v", "v")
        p.on("v", lambda v: v.tensor_copy(out=AdT[:], in_=hacc[:]), mark="v")
        p.wt("y", "v"); p.wt("y", "s")
        p.dma("y", lambda y: y.dma_start(out=cc_in[0:128, :], in_=decay[:]))
        p.dma("y", lambda y: y.dma_start(out=cc_in[128:256, :], in_=hfin[:]))
        p.wt("g", "ds")
        p.on("g", lambda g: g.collective_compute(
            "AllGather", ALU.bypass, replica_groups=[list(range(NCORE))],
            ins=[cc_in[:, :]], outs=[cc_out[:, :]]), mark="c")
        p.wt("g", "c")
        with contextlib.ExitStack() as cbst:
            dechf = cbst.enter_context(
                nc.sbuf_tensor(tag + "dechf", [128, NCORE * 512], F32))
            gall = cbst.enter_context(
                nc.sbuf_tensor(tag + "gall", [128, NCORE * 256], F32))
            gtmp = cbst.enter_context(
                nc.sbuf_tensor(tag + "gtmp", [128, 256], F32))
            for r in range(NCORE):
                p.dma("g", lambda g, r=r: g.dma_start(
                    out=dechf[:, r * 512:r * 512 + 256],
                    in_=cc_out[256 * r:256 * r + 128, :]))
                p.dma("g", lambda g, r=r: g.dma_start(
                    out=dechf[:, r * 512 + 256:r * 512 + 512],
                    in_=cc_out[256 * r + 128:256 * r + 256, :]))
            p.wt("v", "dg")
            p.on("v", lambda v: v.memset(gall[:, 0:256], 0.0), mark="v")
            for r in range(1, NCORE):
                p.on("v", lambda v, r=r: v.tensor_tensor(
                    out=gtmp[:],
                    in0=dechf[:, (r - 1) * 512:(r - 1) * 512 + 256],
                    in1=gall[:, (r - 1) * 256:r * 256], op=ALU.mult),
                    mark="v")
                p.wt("v", "v")
                p.on("v", lambda v, r=r: v.tensor_tensor(
                    out=gall[:, r * 256:(r + 1) * 256], in0=gtmp[:],
                    in1=dechf[:, (r - 1) * 512 + 256:(r - 1) * 512 + 512],
                    op=ALU.add), mark="v")
                p.wt("v", "v")
            vpad(); p.wt("y", "v")
            p.dma("y", lambda y: y.dma_start(out=g_bounce[:, :], in_=gall[:]))
            p.wt("y", "ds")

            def pick_hstate(y):
                cid = (_gp_cid(nc, ctx) if ctx is not None
                       else nc.partition_id([ET.SP]))
                return y.dma_start(out=h_state[:, :],
                                   in_=g_bounce[:, bass.ts(cid, 256)])
            p.dma("y", pick_hstate)

        if DEBUG:
            p.wt("y", "dg")
            p.dma("y", lambda y: y.dma_start(out=dbg["dbg_decay"][:, :],
                                             in_=decay[:]))
            p.dma("y", lambda y: y.dma_start(out=dbg["dbg_hfin"][:, :],
                                             in_=hfin[:]))
            p.dma("y", lambda y: y.dma_start(out=dbg["dbg_hstate"][:, :],
                                             in_=h_state[:]))
        with contextlib.ExitStack() as bst:
            bsb = lambda nm, sh, dt=BF16: bst.enter_context(
                nc.sbuf_tensor(tag + nm, sh, dt))
            CbcB = bsb("CbcB", [128, NST * TC])
            b_allB = bsb("b_allB", [128, 2 * NST * TC])
            tmp_allB = bsb("tmp_allB", [128, NST * TC // 2])
            a_allB = bst.enter_context(
                nc.sbuf_tensor(tag + "a_allB", [128, NST * TC], F32))
            h_allB = bst.enter_context(
                nc.sbuf_tensor(tag + "h_allB", [128, NST * TC], F32))
            v_after_yfin = {}
            v_after_scans = {}
            sc_after_aexp = {}
            ds_after_loadB = {}
            ds_after_out = {}
            uT_rdB = bsb("uT_rdB", [128, 2 * TC])
            dT_rdB = bsb("dT_rdB", [128, 2 * TC])
            sz_rdB = bsb("sz_rdB", [128, 2 * TC])
            y_ch = bst.enter_context(nc.sbuf_tensor(tag + "y_chB", [128, TC], F32))
            y_ch2 = bst.enter_context(
                nc.sbuf_tensor(tag + "y_ch2B", [128, TC], F32))
            yfin = bsb("yfinB", [128, NM * TC])
            t1f = bst.enter_context(nc.sbuf_tensor(tag + "t1fB", [128, TC], F32))
            Wo_bf = bsb("Wo_bfB", [128, 2 * 512])
            xt4 = bst.enter_context(nc.sbuf_tensor(tag + "xt4B", [128, 4 * D], F32))
            xr4 = bst.enter_context(nc.sbuf_tensor(tag + "xr4B", [128, 4 * D], F32))
            p.wt("v", "ds")
            for ck in range(NCH):
                for grp, dst in ((1, CbcB),):
                    for n in range(NST):
                        slb = slice((n % 2) * TC, (n % 2) * TC + TC)
                        row = grp * NST + n
                        p.wt("y", "pe")
                        p.dma("y", lambda y, row=row, ck=ck, sl=slb:
                              y.dma_start(
                                  out=bcrow[0:1, sl],
                                  in_=BC_d[row:row + 1,
                                           ck * TC:(ck + 1) * TC]))
                        bank = ps2 if n % 2 == 0 else ps3
                        psum_guard(bank)
                        p.wt("t", "ds")
                        p.on("t", lambda t, b=bank, sl=slb: t.matmul(
                            out=b[:], lhsT=ones1[:, :], rhs=bcrow[0:1, sl],
                            start=True, stop=True), mark="pe")
                        p.wt("v", "pe")
                        p.on("v", lambda v, n=n, b=bank, d=dst:
                             v.tensor_copy(
                                 out=d[:, n * TC:(n + 1) * TC], in_=b[:]),
                             mark="v")
                        set_rd(bank, "v")
                p.wt("v", "pe")
                v_chunk0 = p.ct["v"]
                for m in range(NM):
                    gm = ck * NM + m
                    slm = slice((m % 2) * TC, (m % 2) * TC + TC)
                    b0 = (gm % 2) * NST * TC
                    if SER_B or gm < 2:
                        p.wt("y", "v"); p.wt("y", "s")
                    else:
                        p.wt("y", "v", v_after_yfin[gm - 2])
                        p.wt("y", "s", sc_after_aexp[gm - 2])
                    p.dma("y", lambda y, m=m, ck=ck, sl=slm: y.dma_start(
                        out=uT_rdB[:, sl],
                        in_=uT_sp[:, m, ck * TC:(ck + 1) * TC]))
                    p.dma("y", lambda y, m=m, ck=ck, sl=slm: y.dma_start(
                        out=dT_rdB[:, sl],
                        in_=dT_sp[:, m, ck * TC:(ck + 1) * TC]))
                    p.dma("y", lambda y, m=m, ck=ck, sl=slm: y.dma_start(
                        out=sz_rdB[:, sl],
                        in_=szT_sp[:, m, ck * TC:(ck + 1) * TC]))
                    p.dma("y", lambda y, m=m, ck=ck, b0=b0: y.dma_start(
                        out=b_allB[:, b0:b0 + NST * TC],
                        in_=b_sp[ck, m, :, :]))
                    ds_after_loadB[gm] = p.ct["ds"]
                    if SER_B:
                        p.wt("s", "ds"); p.wt("s", "v")
                    else:
                        p.wt("s", "ds", ds_after_loadB[gm])
                        if gm >= 1:
                            p.wt("s", "v", v_after_scans[gm - 1])
                    for n in range(NST):
                        mn = m * NST + n
                        p.on("s", lambda s, n=n, mn=mn, sl=slm: s.activation(
                            out=a_allB[:, n * TC:(n + 1) * TC],
                            in_=dT_rdB[:, sl], func=AF.Exp,
                            scale=A_sb[:, mn:mn + 1]),
                            mark="s" if n == NST - 1 else None)
                    sc_after_aexp[gm] = p.ct["s"]
                    if SER_B:
                        p.wt("v", "s"); p.wt("v", "ds")
                    else:
                        p.wt("v", "s", sc_after_aexp[gm])
                        p.wt("v", "ds", ds_after_loadB[gm])
                    if m == 0:
                        p.wt("v", "v", v_chunk0)
                    for n in range(NST):
                        mn = m * NST + n
                        p.on("v", lambda v, n=n, mn=mn, b0=b0:
                             v.tensor_tensor_scan(
                                 out=h_allB[:, n * TC:(n + 1) * TC],
                                 data0=a_allB[:, n * TC:(n + 1) * TC],
                                 data1=b_allB[:, b0 + n * TC:
                                              b0 + (n + 1) * TC],
                                 initial=h_state[:, mn:mn + 1],
                                 op0=ALU.mult, op1=ALU.add),
                            mark="v" if n == NST - 1 else None)
                    v_after_scans[gm] = p.ct["v"]
                    p.wt("v", "v")
                    HW = NST * TC // 2
                    for hf in range(2):
                        p.on("v", lambda v, hf=hf: v.tensor_tensor(
                            out=tmp_allB[:],
                            in0=h_allB[:, hf * HW:(hf + 1) * HW],
                            in1=CbcB[:, hf * HW:(hf + 1) * HW],
                            op=ALU.mult))
                        tv = tmp_allB[:, :].rearrange("p (n t) -> p t n",
                                                      n=NST // 2)
                        p.on("v", lambda v, hf=hf, tv=tv: v.reduce_sum(
                            out=(y_ch[:, :] if hf == 0 else y_ch2[:, :]),
                            in_=tv, axis=AX), mark="v")
                        p.wt("v", "v")
                    hv = h_allB[:, :].rearrange("p (n t) -> p n t", n=NST)
                    p.on("v", lambda v, m=m, hv=hv: v.tensor_copy(
                        out=h_state[:, m * NST:(m + 1) * NST],
                        in_=hv[:, :, TC - 1]))
                    p.on("v", lambda v, m=m, sl=slm: v.scalar_tensor_tensor(
                        out=t1f[:], in0=uT_rdB[:, sl],
                        scalar=D_sb[:, m:m + 1],
                        in1=y_ch[:, :],
                        op0=ALU.mult, op1=ALU.add))
                    p.on("v", lambda v: v.tensor_tensor(
                        out=y_ch[:, :], in0=t1f[:], in1=y_ch2[:, :],
                        op=ALU.add))
                    p.on("v", lambda v, m=m, sl=slm: v.tensor_tensor(
                        out=yfin[:, m * TC:(m + 1) * TC],
                        in0=y_ch[:, :],
                        in1=sz_rdB[:, sl], op=ALU.mult), mark="v")
                    v_after_yfin[gm] = p.ct["v"]
                    if DEBUG and ck == 0 and m == 0:
                        p.wt("y", "v")
                        p.dma("g", lambda g: g.dma_start(
                            out=dbg["dbg_aall"][:, :], in_=a_allB[:]))
                        p.dma("g", lambda g: g.dma_start(
                            out=dbg["dbg_hall"][:, :], in_=h_allB[:]))
                if DEBUG and ck == 0:
                    p.wt("y", "v")
                    p.dma("y", lambda y: y.dma_start(
                        out=dbg["dbg_yfin"][:, :], in_=yfin[:]))
                if DEBUG:
                    p.wt("y", "v")
                    p.dma("y", lambda y, ck=ck: y.dma_start(
                        out=dbg["dbg_hs4"][ck, :, :], in_=h_state[:]))
                    p.dma("y", lambda y, ck=ck: y.dma_start(
                        out=dbg["dbg_yf4"][ck, :, :], in_=yfin[:]))
                vpad(); vpad()
                p.wt("y", "v")
                p.dma("y", lambda y, ck=ck: y.dma_start(
                    out=yf_d[ck * 128:(ck + 1) * 128, :], in_=yfin[:]))
                vpad(); vpad(); vpad(); vpad()
                if ck > 0:
                    p.wt("g", "ds", ds_after_out[ck - 1])
                p.wt("g", "v")
                for tb in range(4):
                    tile = ck * 4 + tb
                    p.dma("g", lambda g, t_=tile, tb=tb: g.dma_start(
                        out=xt4[:, tb * D:(tb + 1) * D],
                        in_=xs[t_ * 128:(t_ + 1) * 128, :]))
                pes_after_m = {}
                pe_chunk_start = p.ct["pe"]
                for half in range(2):
                    for m in range(NM):
                        slw = slice((m % 2) * 512, (m % 2) * 512 + 512)
                        if m >= 2:
                            p.wt("g", "pe", pes_after_m[(half, m - 2)])
                        elif half == 1:
                            p.wt("g", "pe", pes_after_m[(0, NM - 2 + m)])
                        else:
                            p.wt("g", "pe", pe_chunk_start)
                        p.dma("g", lambda g, m=m, half=half, sl=slw:
                              g.dma_start(
                                  out=Wo_bf[:, sl],
                                  in_=Wout[:, m * D + half * 512:
                                           m * D + half * 512 + 512]))
                        p.wt("t", "dg")
                        if m == 0:
                            for tb in range(4):
                                bank = [ps0, ps1, ps2, ps3][tb]
                                psum_guard(bank)
                            p.wt("t", "v")
                        for tb in range(4):
                            bank = [ps0, ps1, ps2, ps3][tb]
                            p.on("t", lambda t, m=m, tb=tb, b=bank, sl=slw:
                                 t.matmul(
                                     out=b[:],
                                     lhsT=yfin[:, m * TC + tb * 128:
                                               m * TC + (tb + 1) * 128],
                                     rhs=Wo_bf[:, sl],
                                     start=(m == 0), stop=(m == NM - 1)),
                                 mark="pe" if tb == 3 else None)
                        pes_after_m[(half, m)] = p.ct["pe"]
                    p.wt("v", "pe"); p.wt("v", "ds"); p.wt("v", "s")
                    p.wt("v", "dg")
                    for tb in range(4):
                        bank = [ps0, ps1, ps2, ps3][tb]
                        p.on("v", lambda v, tb=tb, b=bank, half=half:
                             v.scalar_tensor_tensor(
                                 out=xr4[:, tb * D + half * 512:
                                         tb * D + half * 512 + 512],
                                 in0=xt4[:, tb * D + half * 512:
                                         tb * D + half * 512 + 512],
                                 scalar=2.0, in1=b[:],
                                 op0=ALU.mult, op1=ALU.add), mark="v")
                        set_rd(bank, "v")
                if DEBUG and ck == 0:
                    p.wt("y", "v")
                    p.dma("y", lambda y: y.dma_start(
                        out=dbg["dbg_xr"][:, :], in_=xr4[:]))
                for tb in range(4):
                    xap = xr4[:, tb * D:(tb + 1) * D]
                    ln_stats(p, xap, lnsc[:], cols, D)
                    p.wt("v", "ds")
                    p.on("v", lambda v, xap=xap, tb=tb: v.tensor_scalar(
                        out=xt4[:, tb * D:(tb + 1) * D], in0=xap,
                        scalar1=cols["rstd"], scalar2=cols["nmr"],
                        op0=ALU.mult, op1=ALU.add), mark="v")
                vpad(); vpad()
                p.wt("y", "v")
                p.dma("y", lambda y, ck=ck: y.dma_start(
                    out=xr_d[ck * 512:(ck + 1) * 512, :].rearrange(
                        "(t p) c -> p t c", p=128),
                    in_=xr4[:, :].rearrange("p (t c) -> p t c", t=4)))
                p.dma("y", lambda y, ck=ck: y.dma_start(
                    out=x1[ck * 512:(ck + 1) * 512, :].rearrange(
                        "(t p) c -> p t c", p=128),
                    in_=xt4[:, :].rearrange("p (t c) -> p t c", t=4)))
                p.dma("y", lambda y, ck=ck: y.dma_start(
                    out=x1b[ck * 512:(ck + 1) * 512, :].rearrange(
                        "(t p) c -> p t c", p=128),
                    in_=xt4[:, :].rearrange("p (t c) -> p t c", t=4)))
                ds_after_out[ck] = p.ct["ds"]

        p.on("v", lambda v: v.memset(lncol[:, 9:10], 0.0), mark="v")
        p.on("v", lambda v: v.memset(lncol[:, 9:10], 0.0), mark="v")
        p.on("s", lambda s_: s_.activation(out=lncol[:, 10:11],
                                           in_=lncol[:, 9:10],
                                           func=AF.Square), mark="s")
        p.on("s", lambda s_: s_.activation(out=lncol[:, 11:12],
                                           in_=lncol[:, 9:10],
                                           func=AF.Square), mark="s")
        if ctx is None:
            p.wt("y", "v"); p.wt("y", "s")
            p.wt("y", "ds")
            p.wt("y", "dg")
            p.replay(block)
            for g in p.guards:
                g.__exit__(None, None, None)
    for g in reversed(_stage_guards):
        g.__exit__(None, None, None)
    return nc


def _stage1_inputs(x_full, w):
    ident = np.eye(128).astype(BF)
    ones1 = np.ones((1, 128), np.float32)
    conv_w = np.ascontiguousarray(
        w["conv_w"].reshape(DI, 4).reshape(NM, 128, 4).transpose(1, 0, 2)
        .reshape(128, NM * 4))
    r128 = lambda a: np.ascontiguousarray(np.asarray(a).reshape(NM, 128).T)
    Wxp_r = np.ascontiguousarray(
        np.asarray(w["W_xproj"]).reshape(NM, 128, 96).transpose(1, 0, 2)
        .reshape(128, NM * 96))
    Alog_r = np.ascontiguousarray(
        np.asarray(w["A_log"]).reshape(NM, 128, NST).transpose(1, 0, 2)
        .reshape(128, NM * NST))
    Wout_r = np.ascontiguousarray(
        np.asarray(w["W_out"]).reshape(NM, 128, D).transpose(1, 0, 2)
        .reshape(128, NM * D))
    common = dict(
        lnw=np.asarray(w["ln_w"]).reshape(1, D),
        lnb=np.asarray(w["ln_b"]).reshape(1, D),
        Win=np.ascontiguousarray(
            np.asarray(w["W_in"]).reshape(8, 128, 2 * DI).transpose(1, 0, 2)
            .reshape(128, 8 * 2 * DI)), convw=conv_w,
        convb=r128(w["conv_b"]), Wxp=Wxp_r, Wdt=np.asarray(w["W_dt"]),
        bdt=r128(w["b_dt"]), Alog=Alog_r, Dvec=r128(w["D"]), Wout=Wout_r,
        ident=ident, ones1=ones1)
    maps = []
    for c in range(NCORE):
        xsl = x_full[c * S:(c + 1) * S]
        if c == 0:
            xhalo = np.zeros((128, D), np.float32)
            hs = np.zeros((128, 1), np.float32)
        else:
            xhalo = x_full[c * S - 128:c * S]
            hs = np.ones((128, 1), np.float32)
        m = dict(common)
        m.update(xs=xsl, xh=xhalo, hsc=hs)
        maps.append({k: np.ascontiguousarray(np.asarray(v, np.float32))
                     if k != "ident" else v for k, v in m.items()})
    return maps


def run_stage1(x_full, w, full=False):
    nc = build_stage1()
    maps = _stage1_inputs(x_full, w)
    res = run_bass_kernel_spmd(nc, maps, core_ids=list(range(NCORE)))
    if full:
        return res.results
    return np.concatenate([np.asarray(r["x1"]) for r in res.results], axis=0)


USE_FUSED = True


def kernel(**inputs):
    if USE_FUSED:
        return kernel_fused(**inputs)
    x = np.ascontiguousarray(np.asarray(inputs["L0_attn_x"],
                                        np.float32)[0])
    w1m = {k[len("L0_attn_"):]: inputs[k] for k in inputs
           if k.startswith("L0_attn_") and k != "L0_attn_x"}
    l0f = {k[len("L0_ffn_"):]: np.asarray(inputs[k]) for k in inputs
           if k.startswith("L0_ffn_")}
    w4 = {k[len("L1_attn_"):]: np.asarray(inputs[k]) for k in inputs
          if k.startswith("L1_attn_")}
    l1f = {k[len("L1_ffn_"):]: np.asarray(inputs[k]) for k in inputs
           if k.startswith("L1_ffn_")}
    lsel = lambda lf, i: {k: v[i] for k, v in lf.items()}

    x1 = run_stage1(x, w1m)
    t1 = run_local_layer(x1, lsel(l0f, 0), post_ln=False)
    x2 = run_local_layer(t1, lsel(l0f, 1), post_ln=True, xres_full=x1)
    x3 = run_stage4(x2, w4)
    t2 = run_local_layer(x3, lsel(l1f, 0), post_ln=False)
    x4 = run_local_layer(t2, lsel(l1f, 1), post_ln=True, xres_full=x3)
    return x4.reshape(1, T_FULL, D)



EXT = 26
EXTT = EXT * 128
QT0 = 9
KT0 = 2
NQT = EXT - QT0
NKT = EXT - KT0
HEADS = 8
DH = 64
INNER = 512
MC = 22
HID2 = MC * 128


def build_local_layer(post_ln, ctx=None):
    tag = "" if ctx is None else ctx["tag"]
    if ctx is None:
        nc = bass.Bass("TRN2", target_bir_lowering=False, debug=False,
                       num_devices=NCORE)
        din = lambda nm, sh, dt=F32: nc.dram_tensor(nm, sh, dt,
                                                    kind="ExternalInput").ap()
        xw = din("xw", [EXTT, D])
        xres = din("xres", [S, D]) if post_ln else None
    else:
        nc = ctx["nc"]
        din = lambda nm, sh, dt=F32: ctx["io"][nm]
        xw = ctx["x_in"]
        hal = ctx["hal"]
        xres = ctx.get("xres")
    hsc2 = din("hsc2", [1, 1])
    wqkv_r = din("wqkv_r", [128, 8 * 3 * INNER])
    qs_row = din("qs_row", [1, INNER]); ks_row = din("ks_row", [1, INNER])
    wo_r = din("wo_r", [128, 4 * D])
    lng = din("lng", [1, D]); lnb2 = din("lnb2", [1, D])
    w1a_r = din("w1a_r", [MC, 128, 8 * 128])
    w1g_r = din("w1g_r", [MC, 128, 8 * 128])
    b1a_c = din("b1a_c", [128, MC]); b1g_c = din("b1g_c", [128, MC])
    w2_r = din("w2_r", [128, MC * D])
    b2row = din("b2row", [1, D])
    tq_cs = din("tq_cs", [EXTT, INNER]); tq_ss = din("tq_ss", [EXTT, INNER])
    tk_cs = din("tk_cs", [EXTT, INNER]); tk_ss = din("tk_ss", [EXTT, INNER])
    msk = din("msk", [5, 128, 640])
    if ctx is None:
        ident_in = nc.dram_tensor("ident", [128, 128], BF16,
                                  kind="ExternalInput").ap()
        x2 = nc.dram_tensor("x2", [S, D], F32, kind="ExternalOutput").ap()
    else:
        ident_in = ctx["io"]["ident"]
        x2 = ctx["x_out"]
    ones_in = din("ones1", [1, 128])
    xmid_sp = nc.dram_tensor(tag + "xmid_sp", [NQT * 128, D], F32)
    d2 = {}
    if DEBUG2:
        for nm, sh, dt in [("d2_qT", [128, 4 * NQT * 128], BF16),
                           ("d2_kT", [128, 4 * NKT * 128], BF16),
                           ("d2_vbuf", [128, NKT * INNER], BF16),
                           ("d2_P", [128, 640], F32),
                           ("d2_Pm", [128, 640], BF16),
                           ("d2_otok", [128, 512], F32),
                           ("d2_xmid", [NQT * 128, D], F32),
                           ("d2_lnT", [128, 8 * S], BF16),
                           ("d2_xT", [128, 8 * EXTT], BF16),
                           ("d2_wq", [128, 8 * 3 * INNER], BF16),
                           ("d2_qbuf", [128, 512], F32),
                           ("d2_v2", [128, 512], BF16),
                           ("d2_xT2", [128, 8 * EXTT], BF16),
                           ("d2_hh", [128, S], BF16)]:
            d2[nm] = nc.dram_tensor(nm, sh, dt, kind="ExternalOutput").ap()

    _stage_guards = []

    def sb(nm, sh, dt=F32):
        g = nc.sbuf_tensor(tag + nm, sh, dt)
        h = g.__enter__()
        _stage_guards.append(g)
        return h
    ident = sb("identb", [128, 128], BF16)
    ones1 = sb("ones1b", [1, 128])
    ones_bf = sb("ones_bfb", [1, 128], BF16)
    b2_bf = sb("b2_bfb", [1, D], BF16)
    lng_b = sb("lng_b", [128, D]); lnb2_b = sb("lnb2_b", [128, D])
    b1a_sb = sb("b1a_sb", [128, MC]); b1g_sb = sb("b1g_sb", [128, MC])
    hsc2_sb = sb("hsc2_sb", [1, 1])
    lnsc = sb("lnsc2", [128, D]); lncol = sb("lncol2", [128, 12])
    padv = sb("padv", [128, 96]); pads = sb("pads", [128, 32])

    if ctx is None:
        psA = nc.alloc_psum_tensor("psA", [128, 512], F32)
        psB = nc.alloc_psum_tensor("psB", [128, 512], F32)
        psC = nc.alloc_psum_tensor("psC", [128, 512], F32)
        psT = nc.alloc_psum_tensor("psTl", [128, 1024], BF16)
        ps_sc = nc.alloc_psum_tensor("ps_sc", [128, 640], F32)
        ps_o = nc.alloc_psum_tensor("ps_o", [128, 128], F32)
    else:
        psA, psB, psC, psT = (ctx["psA"], ctx["psB"], ctx["psC"], ctx["psT"])
        ps_sc, ps_o = ctx["ps_sc"], ctx["ps_o"]

    with (nc.Block() if ctx is None else contextlib.nullcontext()) as block:
        p = P(nc) if ctx is None else ctx["p"]

        def vpad():
            p.on("v", lambda v: v.memset(padv[:], 0.0), mark="v")

        def spad():
            p.on("s", lambda s_: s_.activation(out=pads[:],
                                               in_=padv[:, 0:pads.shape[1]],
                                               func=AF.Square), mark="s")
        cols = dict(sum=lncol[:, 0:1], ssq=lncol[:, 1:2], mean=lncol[:, 2:3],
                    nmean=lncol[:, 3:4], ex2=lncol[:, 4:5], nvar=lncol[:, 5:6],
                    lnv=lncol[:, 6:7], rstd=lncol[:, 7:8], nmr=lncol[:, 8:9])
        last_rd = ({id(b): ("v", 0) for b in (psA, psB, psC, psT, ps_sc,
                                              ps_o)}
                   if ctx is None else ctx["last_rd"])

        def psum_guard(bank):
            p.wt("t", last_rd[id(bank)][0], last_rd[id(bank)][1])

        def set_rd(bank, eng):
            last_rd[id(bank)] = (eng, p.ct[eng])

        def bcast_row(dst, width):
            for off in range(0, width, 512):
                wdt = min(512, width - off)
                psum_guard(psA)
                p.on("t", lambda t, dst=dst, off=off, wdt=wdt: t.matmul(
                    out=psA[:, 0:wdt], lhsT=ones1[:, :],
                    rhs=dst[0:1, off:off + wdt], start=True, stop=True),
                    mark="pe")
                p.wt("v", "pe")
                p.on("v", lambda v, dst=dst, off=off, wdt=wdt: v.tensor_copy(
                    out=dst[:, off:off + wdt], in_=psA[:, 0:wdt]), mark="v")
                set_rd(psA, "v")

        def hv(buf, half):
            return buf[:, :].rearrange("p (h s d) -> p h s d",
                                       h=HEADS, s=2)[:, :, half]

        p.dma("g", lambda g: g.dma_start(out=ident[:], in_=ident_in[:, :]))
        p.dma("g", lambda g: g.dma_start(out=ones_bf[:], in_=ones_in[:, :]))
        p.dma("g", lambda g: g.dma_start(out=b2_bf[:], in_=b2row[:, :]))
        for dst, src, row in [(ones1, ones_in, True), (lng_b, lng, True),
                              (lnb2_b, lnb2, True), (b1a_sb, b1a_c, False),
                              (b1g_sb, b1g_c, False), (hsc2_sb, hsc2, True)]:
            p.dma("y", lambda y, d=dst, s_=src, row=row: y.dma_start(
                out=d[0:1, :] if row else d[:], in_=s_[:, :]))
        p.wt("t", "ds"); p.wt("t", "dg")
        for dst, wdt in [(lng_b, D), (lnb2_b, D)]:
            bcast_row(dst, wdt)

        with contextlib.ExitStack() as stQ:
            sQ = lambda nm, sh, dt=BF16: stQ.enter_context(
                nc.sbuf_tensor(tag + nm, sh, dt))
            qs_b = stQ.enter_context(
                nc.sbuf_tensor(tag + "qs_b", [128, INNER], F32))
            ks_b = stQ.enter_context(
                nc.sbuf_tensor(tag + "ks_b", [128, INNER], F32))
            msk_sb = stQ.enter_context(
                nc.sbuf_tensor(tag + "msk_sb", [128, 5 * 640], F32))
            qT = sQ("qT", [128, 4 * NQT * 128])
            p.dma("y", lambda y: y.dma_start(out=qs_b[0:1, :],
                                             in_=qs_row[:, :]))
            p.dma("y", lambda y: y.dma_start(out=ks_b[0:1, :],
                                             in_=ks_row[:, :]))
            for i in range(5):
                p.dma("y", lambda y, i=i: y.dma_start(
                    out=msk_sb[:, i * 640:(i + 1) * 640], in_=msk[i, :, :]))
            p.wt("t", "ds")
            for dst, wdt in [(qs_b, INNER), (ks_b, INNER)]:
                bcast_row(dst, wdt)
            kT = sQ("kT", [128, 4 * NKT * 128])
            vbuf = sQ("vbuf", [128, NKT * INNER])

            with contextlib.ExitStack() as st1:
                s1 = lambda nm, sh, dt=BF16: st1.enter_context(
                    nc.sbuf_tensor(tag + nm, sh, dt))
                xT = s1("xT", [128, 8 * EXTT])
                wqkv_bf = s1("wqkv_bf", [128, 8 * 3 * INNER])
                xw_t = st1.enter_context(
                    nc.sbuf_tensor(tag + "xw_t", [128, 2 * D], F32))
                xw_tb = st1.enter_context(
                    nc.sbuf_tensor(tag + "xw_tb", [128, D], BF16))
                if ctx is not None:
                    hscH = st1.enter_context(
                        nc.sbuf_tensor(tag + "hscH", [128, 1], F32))
                    p.dma("y", lambda y: y.dma_start(
                        out=hscH[:], in_=ctx["io"]["hscH"]))
                xbf = s1("xbf", [128, D])
                qbuf = st1.enter_context(
                    nc.sbuf_tensor(tag + "qbuf", [128, 512], F32))
                sqb = st1.enter_context(
                    nc.sbuf_tensor(tag + "sqb", [128, 512], F32))
                nq = st1.enter_context(nc.sbuf_tensor(tag + "nq", [128, 32], F32))
                t1b = st1.enter_context(
                    nc.sbuf_tensor(tag + "t1b", [128, 512], F32))
                t2b = st1.enter_context(
                    nc.sbuf_tensor(tag + "t2b", [128, 512], F32))
                qnb = st1.enter_context(
                    nc.sbuf_tensor(tag + "qnb", [128, 512], F32))
                qrbf = s1("qrbf", [128, 512])
                tbl = st1.enter_context(
                    nc.sbuf_tensor(tag + "tbl", [128, 2 * 512], F32))
                p.dma("g", lambda g: g.dma_start(out=wqkv_bf[:],
                                                 in_=wqkv_r[:, :]))

                for T in range(EXT):
                    slx = slice((T % 2) * D, (T % 2) * D + D)
                    p.wt("y", "v")
                    if ctx is not None and T < 10:
                        def ldhal(y, T=T):
                            prev = _gp_prev(nc, ctx)
                            blk = hal[bass.ts(prev, 1280), :]
                            return y.dma_start(
                                out=xw_tb[:],
                                in_=blk[T * 128:(T + 1) * 128, :])
                        p.dma("y", ldhal)
                        p.wt("v", "ds"); p.wt("v", "pe")
                        p.on("v", lambda v: v.tensor_scalar(
                            out=xbf[:], in0=xw_tb[:], scalar1=hscH[:, 0:1],
                            scalar2=None, op0=ALU.mult), mark="v")
                    else:
                        src = (xw[T * 128:(T + 1) * 128, :] if ctx is None
                               else xw[(T - 10) * 128:(T - 9) * 128, :])
                        p.dma("y", lambda y, src=src, sl=slx: y.dma_start(
                            out=xw_t[:, sl], in_=src))
                        p.wt("v", "ds"); p.wt("v", "pe")
                        p.on("v", lambda v, sl=slx: v.tensor_copy(
                            out=xbf[:], in_=xw_t[:, sl]), mark="v")
                    p.wt("t", "v")
                    psum_guard(psT)
                    for j in range(8):
                        p.on("t", lambda t, j=j: t.transpose(
                            out=psT[:, j * 128:(j + 1) * 128],
                            in_=xbf[:, j * 128:(j + 1) * 128],
                            identity=ident[:]),
                            mark="pe" if j == 7 else None)
                    p.wt("s", "pe")
                    for j in range(8):
                        p.on("s", lambda s, j=j, T=T: s.copy(
                            out=xT[:, j * EXTT + T * 128:
                                   j * EXTT + (T + 1) * 128],
                            in_=psT[:, j * 128:(j + 1) * 128]), mark="s")
                    set_rd(psT, "s")

                def qk_post(T, src_psum, which):
                    qsb = qs_b if which == "q" else ks_b
                    tcs = tq_cs if which == "q" else tk_cs
                    tss = tq_ss if which == "q" else tk_ss
                    dstT = qT if which == "q" else kT
                    t0 = QT0 if which == "q" else KT0
                    stride = NQT * 128 if which == "q" else NKT * 128
                    p.wt("v", "pe"); p.wt("v", "s")
                    p.on("v", lambda v: v.tensor_copy(out=qbuf[:],
                                                      in_=src_psum[:]),
                         mark="v")
                    set_rd(src_psum, "v")
                    p.wt("s", "v")
                    p.on("s", lambda s: s.activation(out=sqb[:], in_=qbuf[:],
                                                     func=AF.Square),
                         mark="s")
                    p.wt("v", "s")
                    sqv = sqb[:, :].rearrange("p (h d) -> p h d", h=HEADS)
                    p.on("v", lambda v, sqv=sqv: v.reduce_sum(
                        out=nq[:, 0:8], in_=sqv, axis=AX), mark="v")
                    p.wt("v", "v")
                    p.on("v", lambda v: v.tensor_scalar_max(
                        out=nq[:, 8:16], in0=nq[:, 0:8], scalar1=1e-24),
                        mark="v")
                    p.wt("s", "v")
                    p.on("s", lambda s: s.activation(
                        out=nq[:, 16:24], in_=nq[:, 8:16], func=AF.Ln))
                    p.on("s", lambda s: s.activation(
                        out=nq[:, 24:32], in_=nq[:, 16:24], func=AF.Exp,
                        scale=-0.5), mark="s")
                    p.wt("v", "s")
                    for h in range(HEADS):
                        p.on("v", lambda v, h=h, qsb=qsb:
                             v.scalar_tensor_tensor(
                                 out=qnb[:, h * DH:(h + 1) * DH],
                                 in0=qbuf[:, h * DH:(h + 1) * DH],
                                 scalar=nq[:, 24 + h:25 + h],
                                 in1=qsb[:, h * DH:(h + 1) * DH],
                                 op0=ALU.mult, op1=ALU.mult),
                             mark="v" if h == HEADS - 1 else None)
                    p.wt("y", "v")
                    p.dma("y", lambda y, T=T, tcs=tcs: y.dma_start(
                        out=tbl[:, 0:512], in_=tcs[T * 128:(T + 1) * 128, :]))
                    p.dma("y", lambda y, T=T, tss=tss: y.dma_start(
                        out=tbl[:, 512:1024],
                        in_=tss[T * 128:(T + 1) * 128, :]))
                    p.wt("v", "ds")
                    p.on("v", lambda v: v.tensor_tensor(
                        out=t1b[:], in0=qnb[:], in1=tbl[:, 0:512],
                        op=ALU.mult))
                    ssv = lambda half: tbl[:, 512:1024].rearrange(
                        "p (h s d) -> p h s d", h=HEADS, s=2)[:, :, half]
                    p.on("v", lambda v: v.tensor_tensor(
                        out=hv(t2b, 0), in0=hv(qnb, 1), in1=ssv(0),
                        op=ALU.mult))
                    p.on("v", lambda v: v.tensor_tensor(
                        out=hv(t2b, 1), in0=hv(qnb, 0), in1=ssv(1),
                        op=ALU.mult), mark="v")
                    p.on("v", lambda v: v.tensor_tensor(
                        out=hv(qrbf, 0), in0=hv(t1b, 0), in1=hv(t2b, 0),
                        op=ALU.subtract))
                    p.on("v", lambda v: v.tensor_tensor(
                        out=hv(qrbf, 1), in0=hv(t1b, 1), in1=hv(t2b, 1),
                        op=ALU.add), mark="v")
                    p.wt("t", "v")
                    psum_guard(psT)
                    for j in range(4):
                        p.on("t", lambda t, j=j: t.transpose(
                            out=psT[:, j * 128:(j + 1) * 128],
                            in_=qrbf[:, j * 128:(j + 1) * 128],
                            identity=ident[:]),
                            mark="pe" if j == 3 else None)
                    p.wt("s", "pe")
                    for j in range(4):
                        p.on("s", lambda s, j=j, T=T, dstT=dstT, t0=t0,
                             stride=stride: s.copy(
                                 out=dstT[:, j * stride + (T - t0) * 128:
                                          j * stride + (T - t0 + 1) * 128],
                                 in_=psT[:, j * 128:(j + 1) * 128]),
                             mark="s")
                    set_rd(psT, "s")

                p.wt("t", "dg")
                for T in range(KT0, EXT):
                    for nb, bank in ((0, psA), (1, psB), (2, psC)):
                        psum_guard(bank)
                        p.wt("t", "s")
                        for k in range(8):
                            coff = T * 128 - (1 if k >= 4 else 0)
                            p.on("t", lambda t, k=k, nb=nb, bank=bank,
                                 coff=coff: t.matmul(
                                     out=bank[:],
                                     lhsT=xT[:, k * EXTT + coff:
                                             k * EXTT + coff + 128],
                                     rhs=wqkv_bf[:,
                                                 k * 3 * INNER + nb * 512:
                                                 k * 3 * INNER + nb * 512
                                                 + 512],
                                     start=(k == 0), stop=(k == 7)),
                                 mark="pe" if k == 7 else None)
                    p.wt("v", "pe")
                    p.on("v", lambda v, T=T: v.tensor_copy(
                        out=vbuf[:, (T - KT0) * INNER:
                                 (T - KT0 + 1) * INNER],
                        in_=psC[:]), mark="v")
                    set_rd(psC, "v")
                    if DEBUG2 and T == KT0:
                        p.wt("y", "v"); p.wt("y", "s")
                        p.dma("y", lambda y: y.dma_start(
                            out=d2["d2_v2"][:, :], in_=vbuf[:, 0:512]))
                        p.dma("y", lambda y: y.dma_start(
                            out=d2["d2_xT2"][:, :], in_=xT[:]))
                    if DEBUG2 and T == 10:
                        p.wt("y", "v"); p.wt("y", "s")
                        p.dma("y", lambda y: y.dma_start(
                            out=d2["d2_wq"][:, :], in_=wqkv_bf[:]))
                        p.on("v", lambda v: v.tensor_copy(
                            out=qbuf[:], in_=psA[:]), mark="v")
                        p.wt("y", "v")
                        p.dma("y", lambda y: y.dma_start(
                            out=d2["d2_qbuf"][:, :], in_=qbuf[:]))
                    qk_post(T, psB, "k")
                    if T >= QT0:
                        qk_post(T, psA, "q")
                    else:
                        p.wt("v", "pe"); p.wt("v", "s")
                        p.on("v", lambda v: v.tensor_copy(out=qbuf[:],
                                                          in_=psA[:]),
                             mark="v")
                        set_rd(psA, "v")

            if DEBUG2:
                p.wt("y", "s"); p.wt("y", "v")
                p.dma("y", lambda y: y.dma_start(out=d2["d2_xT"][:, :],
                                                 in_=xT[:]))
                p.dma("y", lambda y: y.dma_start(out=d2["d2_qT"][:, :],
                                                 in_=qT[:]))
                p.dma("y", lambda y: y.dma_start(out=d2["d2_kT"][:, :],
                                                 in_=kT[:]))
                p.dma("y", lambda y: y.dma_start(out=d2["d2_vbuf"][:, :],
                                                 in_=vbuf[:]))
            with contextlib.ExitStack() as st2:
                s2 = lambda nm, sh, dt=BF16: st2.enter_context(
                    nc.sbuf_tensor(tag + nm, sh, dt))
                P_f32 = st2.enter_context(
                    nc.sbuf_tensor(tag + "P_f32", [128, 640], F32))
                Pm_bf = s2("Pm_bf", [128, 640])
                PT_sb = s2("PT_sb", [128, 640])
                o_tok = st2.enter_context(
                    nc.sbuf_tensor(tag + "o_tok", [128, 512], F32))
                obf = s2("obf", [128, 512])
                oT = s2("oT", [128, 4 * NQT * 128])
                wo_bf = s2("wo_bf", [128, 4 * D])
                acol = st2.enter_context(
                    nc.sbuf_tensor(tag + "acol2", [128, 4], F32))
                xm_t = st2.enter_context(
                    nc.sbuf_tensor(tag + "xm_t", [128, 2 * D], F32))
                xw_t2 = st2.enter_context(
                    nc.sbuf_tensor(tag + "xw_t2", [128, 2 * D], F32))
                h9bf = st2.enter_context(
                    nc.sbuf_tensor(tag + "h9bf", [128, D], BF16))
                p.wt("g", "pe"); p.wt("g", "v"); p.wt("g", "s")
                p.dma("g", lambda g: g.dma_start(out=wo_bf[:], in_=wo_r[:, :]))
                for T in range(QT0, EXT):
                    for h in range(HEADS):
                        pt = h // 2
                        pb = (h % 2) * 64
                        msel = 0
                        tl = T - 10
                        if 0 <= tl <= 3:
                            msel = 1 + tl
                        psum_guard(ps_sc)
                        p.wt("t", "dg")
                        for so, sw in ((0, 512), (512, 128)):
                            p.on("t", lambda t, T=T, pt=pt, pb=pb, so=so,
                                 sw=sw: t.matmul(
                                     out=ps_sc[:, so:so + sw],
                                     lhsT=qT[pb:pb + 64,
                                             pt * NQT * 128 + (T - QT0) * 128:
                                             pt * NQT * 128 +
                                             (T - QT0 + 1) * 128],
                                     rhs=kT[pb:pb + 64,
                                            pt * NKT * 128 + (T - 6) * 128
                                            + so:
                                            pt * NKT * 128 + (T - 6) * 128
                                            + so + sw],
                                     start=True, stop=True),
                                 mark="pe" if so else None)
                        p.wt("v", "pe"); p.wt("v", "s")
                        p.on("v", lambda v: v.reduce_max(
                            out=acol[:, 0:1], in_=ps_sc[:, :], axis=AX,
                            negate=True), mark="v")
                        p.wt("s", "v")
                        p.on("s", lambda s: s.activation(
                            out=P_f32[:], in_=ps_sc[:, :], func=AF.Exp,
                            bias=acol[:, 0:1]), mark="s")
                        set_rd(ps_sc, "s")
                        p.wt("v", "s")
                        p.on("v", lambda v, msel=msel: v.scalar_tensor_tensor(
                            out=Pm_bf[:], in0=P_f32[:], scalar=1.0,
                            in1=msk_sb[:, msel * 640:(msel + 1) * 640],
                            op0=ALU.mult, op1=ALU.mult,
                            accum_out=acol[:, 1:2]), mark="v")
                        p.wt("v", "v")
                        p.on("v", lambda v: v.reciprocal(
                            out=acol[:, 2:3], in_=acol[:, 1:2]), mark="v")
                        if DEBUG2 and T == 10 and h == 0:
                            p.wt("y", "v"); p.wt("y", "s")
                            p.dma("y", lambda y: y.dma_start(
                                out=d2["d2_P"][:, :], in_=P_f32[:]))
                            p.dma("y", lambda y: y.dma_start(
                                out=d2["d2_Pm"][:, :], in_=Pm_bf[:]))
                        p.wt("t", "v")
                        psum_guard(psT)
                        for j in range(5):
                            p.on("t", lambda t, j=j: t.transpose(
                                out=psT[:, j * 128:(j + 1) * 128],
                                in_=Pm_bf[:, j * 128:(j + 1) * 128],
                                identity=ident[:]),
                                mark="pe" if j == 4 else None)
                        p.wt("s", "pe")
                        p.on("s", lambda s: s.copy(out=PT_sb[:],
                                                   in_=psT[:, 0:640]),
                             mark="s")
                        set_rd(psT, "s")
                        psum_guard(ps_o)
                        p.wt("t", "s")
                        for j in range(5):
                            p.on("t", lambda t, j=j, T=T, h=h: t.matmul(
                                out=ps_o[:, 0:64],
                                lhsT=PT_sb[:, j * 128:(j + 1) * 128],
                                rhs=vbuf[:, (T - 6 + j) * INNER + h * DH:
                                         (T - 6 + j) * INNER + h * DH + 64],
                                start=(j == 0), stop=(j == 4)),
                                mark="pe" if j == 4 else None)
                        p.wt("v", "pe")
                        p.on("v", lambda v, h=h: v.tensor_scalar(
                            out=o_tok[:, h * DH:(h + 1) * DH],
                            in0=ps_o[:, 0:64], scalar1=acol[:, 2:3],
                            scalar2=None, op0=ALU.mult), mark="v")
                        set_rd(ps_o, "v")
                    if DEBUG2 and T == 10:
                        p.wt("y", "v")
                        p.dma("y", lambda y: y.dma_start(
                            out=d2["d2_otok"][:, :], in_=o_tok[:]))
                    p.on("v", lambda v: v.tensor_copy(out=obf[:],
                                                      in_=o_tok[:]),
                         mark="v")
                    p.wt("t", "v")
                    psum_guard(psT)
                    for j in range(4):
                        p.on("t", lambda t, j=j: t.transpose(
                            out=psT[:, j * 128:(j + 1) * 128],
                            in_=obf[:, j * 128:(j + 1) * 128],
                            identity=ident[:]), mark="pe" if j == 3 else None)
                    p.wt("s", "pe")
                    for j in range(4):
                        p.on("s", lambda s, j=j, T=T: s.copy(
                            out=oT[:, j * NQT * 128 + (T - QT0) * 128:
                                   j * NQT * 128 + (T - QT0 + 1) * 128],
                            in_=psT[:, j * 128:(j + 1) * 128]), mark="s")
                    set_rd(psT, "s")
                for T in range(QT0, EXT):
                    slx = slice((T % 2) * D, (T % 2) * D + D)
                    p.wt("y", "v")
                    if ctx is not None and T == QT0:
                        def ldhal9(y):
                            prev = _gp_prev(nc, ctx)
                            blk = hal[bass.ts(prev, 1280), :]
                            return y.dma_start(
                                out=h9bf[:],
                                in_=blk[QT0 * 128:(QT0 + 1) * 128, :])
                        p.dma("y", ldhal9)
                        p.wt("v", "ds")
                        p.on("v", lambda v, sl=slx: v.tensor_copy(
                            out=xw_t2[:, sl], in_=h9bf[:]), mark="v")
                    else:
                        src = (xw[T * 128:(T + 1) * 128, :] if ctx is None
                               else xw[(T - 10) * 128:(T - 9) * 128, :])
                        p.dma("y", lambda y, src=src, sl=slx: y.dma_start(
                            out=xw_t2[:, sl], in_=src))
                    for nb, bank in ((0, psA), (1, psB)):
                        psum_guard(bank)
                        p.wt("t", "s"); p.wt("t", "dg")
                        for k in range(4):
                            p.on("t", lambda t, k=k, T=T, nb=nb, bank=bank:
                                 t.matmul(
                                     out=bank[:],
                                     lhsT=oT[:, k * NQT * 128 +
                                             (T - QT0) * 128:
                                             k * NQT * 128 +
                                             (T - QT0 + 1) * 128],
                                     rhs=wo_bf[:, k * D + nb * 512:
                                               k * D + nb * 512 + 512],
                                     start=(k == 0), stop=(k == 3)),
                                 mark="pe" if k == 3 else None)
                    p.wt("v", "pe"); p.wt("v", "ds")
                    slm = slice((T % 2) * D, (T % 2) * D + D)
                    for nb, bank in ((0, psA), (1, psB)):
                        p.on("v", lambda v, nb=nb, bank=bank, sl=slx:
                             v.tensor_tensor(
                                 out=xm_t[:, sl.start + nb * 512:
                                          sl.start + nb * 512 + 512],
                                 in0=bank[:],
                                 in1=xw_t2[:, sl.start + nb * 512:
                                           sl.start + nb * 512 + 512],
                                 op=ALU.add), mark="v")
                        set_rd(bank, "v")
                    vpad(); p.wt("y", "v")
                    p.dma("y", lambda y, T=T, sl=slx: y.dma_start(
                        out=xmid_sp[(T - QT0) * 128:(T - QT0 + 1) * 128, :],
                        in_=xm_t[:, sl]))
                    if DEBUG2:
                        p.dma("y", lambda y, T=T, sl=slx: y.dma_start(
                            out=d2["d2_xmid"][(T - QT0) * 128:
                                              (T - QT0 + 1) * 128, :],
                            in_=xm_t[:, sl]))

        with contextlib.ExitStack() as st3:
            s3 = lambda nm, sh, dt=BF16: st3.enter_context(
                nc.sbuf_tensor(tag + nm, sh, dt))
            lnT = s3("lnT", [128, 8 * S])
            hh = s3("hh", [128, MC * S])
            w2_bf = s3("w2_bf", [128, MC * D])
            w1s = s3("w1s", [128, 2 * 8 * 128])
            xs2 = st3.enter_context(nc.sbuf_tensor(tag + "xs2", [128, D], F32))
            tnormA = st3.enter_context(
                nc.sbuf_tensor(tag + "tnormA", [128, D], F32))
            tnormB = lnsc
            ln_bf = s3("ln_bf", [128, D])
            gbuf = st3.enter_context(nc.sbuf_tensor(tag + "gbuf", [128, 512], F32))
            x2t = st3.enter_context(nc.sbuf_tensor(tag + "x2t", [128, D], F32))
            x2u = tnormA
            xm_t3 = st3.enter_context(
                nc.sbuf_tensor(tag + "xm_t3", [128, D], F32))
            xr_t = st3.enter_context(
                nc.sbuf_tensor(tag + "xr_t", [128, D], F32))
            tfix = gbuf
            p.wt("g", "pe"); p.wt("g", "v"); p.wt("g", "s")
            p.dma("g", lambda g: g.dma_start(out=w2_bf[:], in_=w2_r[:, :]))
            for tt in range(16):
                r0 = 128 + tt * 128
                p.wt("y", "v"); p.wt("y", "s")
                p.dma("y", lambda y, r0=r0: y.dma_start(
                    out=xs2[:, 0:512], in_=xmid_sp[r0:r0 + 128, 0:512]))
                p.dma("y", lambda y, r0=r0: y.dma_start(
                    out=xs2[:, 512:1024],
                    in_=xmid_sp[r0 - 1:r0 + 127, 512:1024]))
                p.wt("v", "ds"); p.wt("s", "ds")
                if tt == 0:
                    p.on("v", lambda v: v.tensor_scalar(
                        out=tfix[0:1, 0:512], in0=xs2[0:1, 512:1024],
                        scalar1=hsc2_sb[0:1, 0:1], scalar2=None,
                        op0=ALU.mult), mark="v")
                    p.wt("v", "v")
                    p.on("v", lambda v: v.tensor_copy(
                        out=xs2[0:1, 512:1024], in_=tfix[0:1, 0:512]),
                        mark="v")
                    p.wt("v", "v"); p.wt("s", "v")
                ln_stats(p, xs2[:], lnsc[:], cols, D)
                p.on("v", lambda v: v.tensor_scalar(
                    out=tnormA[:], in0=xs2[:], scalar1=cols["rstd"],
                    scalar2=cols["nmr"], op0=ALU.mult, op1=ALU.add))
                p.on("v", lambda v: v.tensor_tensor(
                    out=tnormB[:], in0=tnormA[:], in1=lng_b[:], op=ALU.mult))
                p.on("v", lambda v: v.tensor_tensor(
                    out=ln_bf[:], in0=tnormB[:], in1=lnb2_b[:], op=ALU.add),
                    mark="v")
                p.wt("t", "v")
                psum_guard(psT)
                for j in range(8):
                    p.on("t", lambda t, j=j: t.transpose(
                        out=psT[:, j * 128:(j + 1) * 128],
                        in_=ln_bf[:, j * 128:(j + 1) * 128],
                        identity=ident[:]), mark="pe" if j == 7 else None)
                p.wt("s", "pe")
                for j in range(8):
                    p.on("s", lambda s, j=j, tt=tt: s.copy(
                        out=lnT[:, j * S + tt * 128:j * S + (tt + 1) * 128],
                        in_=psT[:, j * 128:(j + 1) * 128]), mark="s")
                set_rd(psT, "s")
            if DEBUG2:
                p.wt("y", "s")
                p.dma("y", lambda y: y.dma_start(out=d2["d2_lnT"][:, :],
                                                 in_=lnT[:]))
            pe_hist = {}
            pe_b2start = p.ct["pe"]
            for mc in range(MC):
                if mc >= 1:
                    p.wt("g", "pe", pe_hist.get(mc - 1, pe_b2start))
                else:
                    p.wt("g", "pe", pe_b2start)
                p.dma("g", lambda g, mc=mc: g.dma_start(
                    out=w1s[:, 0:1024], in_=w1a_r[mc, :, :]))
                p.dma("g", lambda g, mc=mc: g.dma_start(
                    out=w1s[:, 1024:2048], in_=w1g_r[mc, :, :]))
                p.wt("t", "dg"); p.wt("t", "s")
                for tc in range(4):
                    for gi, bank in ((0, psA), (1, psB)):
                        psum_guard(bank)
                        for k in range(8):
                            p.on("t", lambda t, k=k, tc=tc, gi=gi, bank=bank:
                                 t.matmul(
                                     out=bank[:],
                                     lhsT=w1s[:, gi * 1024 + k * 128:
                                              gi * 1024 + (k + 1) * 128],
                                     rhs=lnT[:, k * S + tc * 512:
                                             k * S + (tc + 1) * 512],
                                     start=(k == 0), stop=(k == 7)),
                                 mark="pe" if k == 7 else None)
                    p.wt("s", "pe")
                    p.on("s", lambda s, mc=mc: s.activation(
                        out=gbuf[:], in_=psB[:], func=AF.Gelu,
                        bias=b1g_sb[:, mc:mc + 1]), mark="s")
                    set_rd(psB, "s")
                    p.wt("v", "s")
                    p.on("v", lambda v, mc=mc, tc=tc: v.scalar_tensor_tensor(
                        out=hh[:, mc * S + tc * 512:mc * S + (tc + 1) * 512],
                        in0=psA[:], scalar=b1a_sb[:, mc:mc + 1], in1=gbuf[:],
                        op0=ALU.add, op1=ALU.mult), mark="v")
                    set_rd(psA, "v")
                pe_hist[mc] = p.ct["pe"]
            if DEBUG2:
                p.wt("y", "v")
                p.dma("y", lambda y: y.dma_start(out=d2["d2_hh"][:, :],
                                                 in_=hh[:, 0:S]))
            p.wt("t", "dg")
            for tt in range(16):
                r0 = 128 + tt * 128
                p.wt("y", "v")
                p.dma("y", lambda y, r0=r0: y.dma_start(
                    out=xm_t3[:], in_=xmid_sp[r0:r0 + 128, :]))
                if post_ln:
                    p.dma("y", lambda y, tt=tt: y.dma_start(
                        out=xr_t[:], in_=xres[tt * 128:(tt + 1) * 128, :]))
                for nb, bank in ((0, psA), (1, psB)):
                    psum_guard(bank)
                    p.wt("t", "v")
                    p.on("t", lambda t, nb=nb, bank=bank: t.matmul(
                        out=bank[:], lhsT=ones_bf[:, :],
                        rhs=b2_bf[0:1, nb * 512:nb * 512 + 512],
                        start=True, stop=False))
                    for mc in range(MC):
                        p.on("t", lambda t, mc=mc, tt=tt, nb=nb, bank=bank:
                             t.matmul(
                                 out=bank[:],
                                 lhsT=hh[:, mc * S + tt * 128:
                                         mc * S + (tt + 1) * 128],
                                 rhs=w2_bf[:, mc * D + nb * 512:
                                           mc * D + nb * 512 + 512],
                                 start=False, stop=(mc == MC - 1)),
                             mark="pe" if mc == MC - 1 else None)
                p.wt("v", "pe"); p.wt("v", "ds"); p.wt("v", "s")
                for nb, bank in ((0, psA), (1, psB)):
                    p.on("v", lambda v, nb=nb, bank=bank:
                         v.tensor_tensor(
                             out=x2t[:, nb * 512:nb * 512 + 512],
                             in0=bank[:],
                             in1=xm_t3[:, nb * 512:nb * 512 + 512],
                             op=ALU.add), mark="v")
                    set_rd(bank, "v")
                if post_ln:
                    p.on("v", lambda v: v.tensor_tensor(
                        out=x2u[:], in0=x2t[:], in1=xr_t[:], op=ALU.add),
                        mark="v")
                    ln_stats(p, x2u[:], lnsc[:], cols, D)
                    p.on("v", lambda v: v.tensor_scalar(
                        out=x2t[:], in0=x2u[:], scalar1=cols["rstd"],
                        scalar2=cols["nmr"], op0=ALU.mult, op1=ALU.add),
                        mark="v")
                vpad(); p.wt("y", "v")
                p.dma("y", lambda y, tt=tt: y.dma_start(
                    out=x2[tt * 128:(tt + 1) * 128, :], in_=x2t[:]))

        p.on("v", lambda v: v.memset(lncol[:, 9:10], 0.0), mark="v")
        p.on("v", lambda v: v.memset(lncol[:, 9:10], 0.0), mark="v")
        p.on("s", lambda s_: s_.activation(out=lncol[:, 10:11],
                                           in_=lncol[:, 9:10],
                                           func=AF.Square), mark="s")
        p.on("s", lambda s_: s_.activation(out=lncol[:, 11:12],
                                           in_=lncol[:, 9:10],
                                           func=AF.Square), mark="s")
        if ctx is None:
            p.wt("y", "v"); p.wt("y", "s")
            p.wt("y", "ds")
            p.wt("y", "dg")
            p.replay(block)
            for g in p.guards:
                g.__exit__(None, None, None)
    for g in reversed(_stage_guards):
        g.__exit__(None, None, None)
    return nc


def _layer_inputs(x_full, lw, post_ln, xres_full=None):
    f32 = np.float32
    wqkv_r = np.ascontiguousarray(
        np.asarray(lw["wqkv"], f32).reshape(8, 128, 3 * INNER)
        .transpose(1, 0, 2).reshape(128, 8 * 3 * INNER))
    wo_rr = np.ascontiguousarray(
        np.asarray(lw["wo"], f32).reshape(4, 128, D).transpose(1, 0, 2)
        .reshape(128, 4 * D))
    HID = 2730
    w1 = np.asarray(lw["w1"], f32)
    w1a = np.zeros((D, HID2), f32); w1a[:, :HID] = w1[:, :HID]
    w1g = np.zeros((D, HID2), f32); w1g[:, :HID] = w1[:, HID:]
    b1 = np.asarray(lw["b1"], f32)
    b1a = np.zeros(HID2, f32); b1a[:HID] = b1[:HID]
    b1g = np.zeros(HID2, f32); b1g[:HID] = b1[HID:]
    w2p = np.zeros((HID2, D), f32); w2p[:HID] = np.asarray(lw["w2"], f32)
    rearr1 = lambda w: np.ascontiguousarray(
        w.reshape(8, 128, MC, 128).transpose(2, 1, 0, 3)
        .reshape(MC, 128, 8 * 128))
    w2_rr = np.ascontiguousarray(
        w2p.reshape(MC, 128, D).transpose(1, 0, 2).reshape(128, MC * D))
    bcol = lambda b: np.ascontiguousarray(b.reshape(MC, 128).T)
    tile8 = lambda a: np.tile(a, (1, 8))

    common = dict(
        wqkv_r=wqkv_r,
        qs_row=tile8(np.asarray(lw["q_scale"], f32).reshape(1, DH)),
        ks_row=tile8(np.asarray(lw["k_scale"], f32).reshape(1, DH)),
        wo_r=wo_rr, lng=np.asarray(lw["ln_g"], f32).reshape(1, D),
        lnb2=np.asarray(lw["ln_b"], f32).reshape(1, D),
        w1a_r=rearr1(w1a), w1g_r=rearr1(w1g),
        b1a_c=bcol(b1a), b1g_c=bcol(b1g),
        w2_r=w2_rr, b2row=np.asarray(lw["b2"], f32).reshape(1, D),
        ident=np.eye(128).astype(BF),
        ones1=np.ones((1, 128), f32))
    half_idx = np.arange(0, DH, 2, dtype=np.float64)
    inv_freq = 1.0 / 10000.0 ** (half_idx / DH)
    base = (half_idx + 0.4 * DH) / (1.4 * DH)
    ii = np.arange(128)[:, None]
    jj = np.arange(640)[None, :]
    std = ((jj >= ii) & (jj <= ii + 512)).astype(f32)
    maps = []
    for c in range(NCORE):
        lo = c * S - 1280
        xwin = np.zeros((EXTT, D), f32)
        s0 = max(0, -lo)
        xwin[s0:] = x_full[lo + s0:lo + EXTT]
        t_abs = np.arange(lo, lo + EXTT, dtype=np.float64)
        t_c = np.clip(t_abs, 0, None)
        freqs = t_c[:, None] * inv_freq[None, :]
        cosc = np.concatenate([np.cos(freqs)] * 2, 1)
        sinc = np.concatenate([np.sin(freqs)] * 2, 1)
        power = (t_c - T_FULL // 2) / (512 // 2)
        scale = np.concatenate([base[None, :] ** power[:, None]] * 2, 1)
        QS = 8.0
        m = dict(common)
        m.update(
            xw=xwin,
            tq_cs=tile8((cosc * scale * QS).astype(f32)),
            tq_ss=tile8((sinc * scale * QS).astype(f32)),
            tk_cs=tile8((cosc / scale).astype(f32)),
            tk_ss=tile8((sinc / scale).astype(f32)),
            hsc2=np.array([[0.0 if c == 0 else 1.0]], f32))
        mk = np.stack([std] * 5)
        if c == 0:
            for ti in range(4):
                mk[1 + ti] = std * (jj >= 512 - 128 * ti)
        m["msk"] = np.ascontiguousarray(mk.astype(f32))
        if post_ln:
            m["xres"] = np.ascontiguousarray(xres_full[c * S:(c + 1) * S])
        maps.append({k: np.ascontiguousarray(v) if k != "ident" else v
                     for k, v in m.items()})
    return maps


_LAYER_NC = {}


def run_local_layer(x_full, lw, post_ln, xres_full=None, full=False):
    if post_ln not in _LAYER_NC:
        _LAYER_NC[post_ln] = build_local_layer(post_ln)
    nc = _LAYER_NC[post_ln]
    maps = _layer_inputs(x_full, lw, post_ln, xres_full)
    res = run_bass_kernel_spmd(nc, maps, core_ids=list(range(NCORE)))
    if full:
        return res.results
    return np.concatenate([np.asarray(r["x2"]) for r in res.results], axis=0)

G4 = 2
H4 = 16
NT4 = 10


def build_stage4(ctx=None):
    tag = "" if ctx is None else ctx["tag"]
    if ctx is None:
        nc = bass.Bass("TRN2", target_bir_lowering=False, debug=False,
                       num_devices=NCORE)
        din = lambda nm, sh, dt=F32: nc.dram_tensor(nm, sh, dt,
                                                    kind="ExternalInput").ap()
        xg = din("xg", [NT4 * 128, D])
        xr3 = din("xr3", [S, D])
        xg_bf = None
    else:
        nc = ctx["nc"]
        din = lambda nm, sh, dt=F32: ctx["io"][nm]
        xr3 = ctx["x_in"]
        xg = None
        xg_bf = nc.dram_tensor(tag + "xg_bf", [NT4 * 128, D], BF16)
    Wq1r = din("Wq1r", [128, 8 * D]); Wq2r = din("Wq2r", [128, 8 * D])
    Wk1r = din("Wk1r", [128, 8 * 128]); Wk2r = din("Wk2r", [128, 8 * 128])
    Wvr = din("Wvr", [128, 8 * 128])
    Woutr = din("Woutr", [128, 8 * D])
    boutrow = din("boutrow", [1, D])
    lqk = din("lqk", [DTR, 4])
    t4c = din("t4c", [NT4 * 128, 512]); t4s = din("t4s", [NT4 * 128, 512])
    t4cq = din("t4cq", [NT4 * 128, 512]); t4sq = din("t4sq", [NT4 * 128, 512])
    if ctx is None:
        ident_in = nc.dram_tensor("ident", [128, 128], BF16,
                                  kind="ExternalInput").ap()
        x4 = nc.dram_tensor("x4", [S, D], F32, kind="ExternalOutput").ap()
    else:
        ident_in = ctx["io"]["ident"]
        x4 = ctx["x_out"]
    ones_in = din("ones1", [1, 128])

    _stage_guards = []

    def sb(nm, sh, dt=F32):
        g = nc.sbuf_tensor(tag + nm, sh, dt)
        h = g.__enter__()
        _stage_guards.append(g)
        return h
    ident = sb("identb", [128, 128], BF16)
    ones1 = sb("ones1b", [1, 128])
    ones_bf = sb("ones_bfb", [1, 128], BF16)
    bout_bf = sb("bout_bfb", [1, D], BF16)
    lqk_sb = sb("lqk_sb", [DTR, 4])
    lamw = sb("lamw", [1, 8])
    nlam_col = sb("nlam_col", [128, 1])
    xgT = sb("xgT", [128, 8 * NT4 * 128], BF16)
    q1T = sb("q1T", [128, 8 * NT4 * 128], BF16)
    q2T = sb("q2T", [128, 8 * NT4 * 128], BF16)
    k1T = sb("k1T", [128, NT4 * 128], BF16)
    k2T = sb("k2T", [128, NT4 * 128], BF16)
    vb4 = sb("vb4", [128, NT4 * 128], BF16)
    Wq1b = sb("Wq1b", [128, 8 * D], BF16)
    Wq2b = sb("Wq2b", [128, 8 * D], BF16)
    Wk1b = sb("Wk1b", [128, 8 * 128], BF16)
    Wk2b = sb("Wk2b", [128, 8 * 128], BF16)
    Wvb = sb("Wvb", [128, 8 * 128], BF16)
    Woutb = sb("Woutb", [128, 8 * D], BF16)
    xt4 = sb("xt4s", [128, 2 * D])
    qbuf = sb("qbuf4", [128, D])
    qr4 = sb("qr4", [128, D])
    qrb = sb("qrb4", [128, D], BF16)
    tA = sb("tA4", [128, 512]); tB = sb("tB4", [128, 512])
    tbl_c = sb("tbl_c4", [128, 512]); tbl_s = sb("tbl_s4", [128, 512])
    P4 = sb("P4", [128, 256], BF16)
    PT4 = sb("PT4", [128, 256], BF16)
    o2s = sb("o2s", [128, 64]); o1s = sb("o1s", [128, 64])
    o_acc = sb("o_acc", [128, D])
    o_bf = sb("o_bf4", [128, D], BF16)
    oT4 = sb("oT4", [128, 8 * 128], BF16)
    acol = sb("acol4", [128, 8])
    xr_t = sb("xr_t4", [128, D])
    xo_t = sb("xo_t4", [128, D])
    lnsc = sb("lnsc4", [128, D]); lncol = sb("lncol4", [128, 12])
    padv = sb("padv", [128, 96]); pads = sb("pads", [128, 32])

    if ctx is None:
        psA = nc.alloc_psum_tensor("psA4", [128, 512], F32)
        psB = nc.alloc_psum_tensor("psB4", [128, 512], F32)
        psT = nc.alloc_psum_tensor("psT4", [128, 1024], BF16)
        psS = nc.alloc_psum_tensor("psS4", [128, 256], F32)
        psO = nc.alloc_psum_tensor("psO4", [128, 128], F32)
    else:
        psA, psB, psT = ctx["psA"], ctx["psB"], ctx["psT"]
        psS, psO = ctx["ps_sc"], ctx["ps_o"]

    with (nc.Block() if ctx is None else contextlib.nullcontext()) as block:
        p = P(nc) if ctx is None else ctx["p"]

        def vpad():
            p.on("v", lambda v: v.memset(padv[:], 0.0), mark="v")

        def spad():
            p.on("s", lambda s_: s_.activation(out=pads[:],
                                               in_=padv[:, 0:pads.shape[1]],
                                               func=AF.Square), mark="s")
        cols = dict(sum=lncol[:, 0:1], ssq=lncol[:, 1:2], mean=lncol[:, 2:3],
                    nmean=lncol[:, 3:4], ex2=lncol[:, 4:5], nvar=lncol[:, 5:6],
                    lnv=lncol[:, 6:7], rstd=lncol[:, 7:8], nmr=lncol[:, 8:9])
        last_rd = ({id(b): ("v", 0) for b in (psA, psB, psT, psS, psO)}
                   if ctx is None else ctx["last_rd"])

        def psum_guard(bank):
            p.wt("t", last_rd[id(bank)][0], last_rd[id(bank)][1])

        def set_rd(bank, eng):
            last_rd[id(bank)] = (eng, p.ct[eng])

        p.dma("g", lambda g: g.dma_start(out=ident[:], in_=ident_in[:, :]))
        p.dma("g", lambda g: g.dma_start(out=ones_bf[:], in_=ones_in[:, :]))
        p.dma("g", lambda g: g.dma_start(out=bout_bf[:], in_=boutrow[:, :]))
        for dst, src in [(Wq1b, Wq1r), (Wq2b, Wq2r), (Wk1b, Wk1r),
                         (Wk2b, Wk2r), (Wvb, Wvr), (Woutb, Woutr)]:
            p.dma("g", lambda g, d=dst, s_=src: g.dma_start(out=d[:],
                                                            in_=s_[:, :]))
        p.dma("y", lambda y: y.dma_start(out=ones1[0:1, :], in_=ones_in[:, :]))
        p.dma("y", lambda y: y.dma_start(out=lqk_sb[:], in_=lqk[:, :]))
        p.wt("t", "ds"); p.wt("t", "dg")
        psum_guard(psB)
        p.on("t", lambda t: t.matmul(out=psB[0:1, 0:1],
                                     lhsT=lqk_sb[:, 0:1], rhs=lqk_sb[:, 1:2],
                                     start=True, stop=True))
        p.on("t", lambda t: t.matmul(out=psB[0:1, 1:2],
                                     lhsT=lqk_sb[:, 2:3], rhs=lqk_sb[:, 3:4],
                                     start=True, stop=True), mark="pe")
        p.wt("s", "pe")
        p.on("s", lambda s: s.activation(out=lamw[0:1, 0:2],
                                         in_=psB[0:1, 0:2], func=AF.Exp),
             mark="s")
        set_rd(psB, "s")
        p.wt("v", "s")
        p.on("v", lambda v: v.tensor_tensor(out=lamw[0:1, 2:3],
                                            in0=lamw[0:1, 0:1],
                                            in1=lamw[0:1, 1:2],
                                            op=ALU.subtract), mark="v")
        p.wt("v", "v")
        p.on("v", lambda v: v.tensor_scalar_add(out=lamw[0:1, 3:4],
                                                in0=lamw[0:1, 2:3],
                                                scalar1=0.8), mark="v")
        p.wt("v", "v")
        p.on("v", lambda v: v.tensor_scalar_max(out=lamw[0:1, 4:5],
                                                in0=lamw[0:1, 3:4],
                                                scalar1=0.1), mark="v")
        p.wt("v", "v")
        p.on("v", lambda v: v.tensor_scalar_min(out=lamw[0:1, 5:6],
                                                in0=lamw[0:1, 4:5],
                                                scalar1=0.9), mark="v")
        p.wt("v", "v")
        p.on("v", lambda v: v.tensor_scalar_mul(out=lamw[0:1, 6:7],
                                                in0=lamw[0:1, 5:6],
                                                scalar1=-1.0), mark="v")
        p.wt("t", "v")
        psum_guard(psB)
        p.on("t", lambda t: t.matmul(out=psB[:, 0:1], lhsT=ones1[:, :],
                                     rhs=lamw[0:1, 6:7], start=True,
                                     stop=True), mark="pe")
        p.wt("v", "pe")
        p.on("v", lambda v: v.tensor_copy(out=nlam_col[:], in_=psB[:, 0:1]),
             mark="v")
        set_rd(psB, "v")

        if ctx is not None:
            def ldwin(y, part):
                cid = _gp_cid(nc, ctx)
                if part == 0:
                    return y.dma_start(out=xg_bf[0:1024, :],
                                       in_=ctx["x3g"][bass.ts(cid, 1024), :])
                return y.dma_start(
                    out=xg_bf[1024:1280, :],
                    in_=ctx["x3g"][bass.ts(4 * cid + 4, 256), :])
            p.dma("y", lambda y: ldwin(y, 0))
            p.dma("y", lambda y: ldwin(y, 1))
            p.wt("g", "ds")
        for T in range(NT4):
            slx = slice((T % 2) * D, (T % 2) * D + D)
            p.wt("y", "v")
            if ctx is None:
                p.dma("y", lambda y, T=T, sl=slx: y.dma_start(
                    out=xt4[:, sl], in_=xg[T * 128:(T + 1) * 128, :]))
            else:
                p.wt("g", "v")
                p.dma("g", lambda g, T=T, sl=slx: g.dma_start(
                    out=xt4[:, sl], in_=xg_bf[T * 128:(T + 1) * 128, :]))
                p.wt("v", "dg")
            p.wt("v", "ds"); p.wt("v", "pe")
            p.on("v", lambda v, sl=slx: v.tensor_copy(out=qr4[:],
                                                      in_=xt4[:, sl]))
            p.on("v", lambda v: v.tensor_copy(out=qrb[:], in_=qr4[:]),
                 mark="v")
            p.wt("t", "v")
            psum_guard(psT)
            for j in range(8):
                p.on("t", lambda t, j=j: t.transpose(
                    out=psT[:, j * 128:(j + 1) * 128],
                    in_=qrb[:, j * 128:(j + 1) * 128], identity=ident[:]),
                    mark="pe" if j == 7 else None)
            p.wt("s", "pe")
            for j in range(8):
                p.on("s", lambda s, j=j, T=T: s.copy(
                    out=xgT[:, j * NT4 * 128 + T * 128:
                            j * NT4 * 128 + (T + 1) * 128],
                    in_=psT[:, j * 128:(j + 1) * 128]), mark="s")
            set_rd(psT, "s")

        def rope4(T, dstT, tq, nheads):
            wide = nheads * DH
            p.wt("y", "v")
            p.dma("y", lambda y, T=T, tq=tq: y.dma_start(
                out=tbl_c[:], in_=(t4cq if tq else t4c)[T * 128:
                                                        (T + 1) * 128, :]))
            p.dma("y", lambda y, T=T, tq=tq: y.dma_start(
                out=tbl_s[:], in_=(t4sq if tq else t4s)[T * 128:
                                                        (T + 1) * 128, :]))
            p.wt("v", "ds")
            qe = qbuf[:, 0:wide].rearrange("p (h d two) -> p h d two",
                                           two=2, d=32)[:, :, :, 0]
            qo = qbuf[:, 0:wide].rearrange("p (h d two) -> p h d two",
                                           two=2, d=32)[:, :, :, 1]
            cr = tbl_c[:, 0:nheads * 32].rearrange("p (h d) -> p h d", d=32)
            sr = tbl_s[:, 0:nheads * 32].rearrange("p (h d) -> p h d", d=32)
            tAv = tA[:, 0:nheads * 32].rearrange("p (h d) -> p h d", d=32)
            tBv = tB[:, 0:nheads * 32].rearrange("p (h d) -> p h d", d=32)
            qre = qr4[:, 0:wide].rearrange("p (h s d) -> p h s d",
                                           s=2, d=32)[:, :, 0]
            qro = qr4[:, 0:wide].rearrange("p (h s d) -> p h s d",
                                           s=2, d=32)[:, :, 1]
            p.on("v", lambda v: v.tensor_tensor(out=tAv, in0=qe, in1=cr,
                                                op=ALU.mult))
            p.on("v", lambda v: v.tensor_tensor(out=tBv, in0=qo, in1=sr,
                                                op=ALU.mult), mark="v")
            p.on("v", lambda v: v.tensor_tensor(out=qre, in0=tAv, in1=tBv,
                                                op=ALU.subtract), mark="v")
            p.on("v", lambda v: v.tensor_tensor(out=tAv, in0=qe, in1=sr,
                                                op=ALU.mult))
            p.on("v", lambda v: v.tensor_tensor(out=tBv, in0=qo, in1=cr,
                                                op=ALU.mult), mark="v")
            p.on("v", lambda v: v.tensor_tensor(out=qro, in0=tAv, in1=tBv,
                                                op=ALU.add))
            p.on("v", lambda v: v.tensor_copy(out=qrb[:, 0:wide],
                                              in_=qr4[:, 0:wide]), mark="v")
            p.wt("t", "v")
            psum_guard(psT)
            nj = wide // 128
            for j in range(nj):
                p.on("t", lambda t, j=j: t.transpose(
                    out=psT[:, j * 128:(j + 1) * 128],
                    in_=qrb[:, j * 128:(j + 1) * 128], identity=ident[:]),
                    mark="pe" if j == nj - 1 else None)
            p.wt("s", "pe")
            for j in range(nj):
                p.on("s", lambda s, j=j, T=T, dstT=dstT: s.copy(
                    out=dstT[:, j * NT4 * 128 + T * 128:
                            j * NT4 * 128 + (T + 1) * 128],
                    in_=psT[:, j * 128:(j + 1) * 128]), mark="s")
            set_rd(psT, "s")

        for T in range(NT4):
            for qi, (Wb, dstT) in enumerate(((Wq1b, q1T), (Wq2b, q2T))):
                for nb, bank in ((0, psA), (1, psB)):
                    psum_guard(bank)
                    p.wt("t", "s")
                    for k in range(8):
                        p.on("t", lambda t, k=k, T=T, nb=nb, bank=bank,
                             Wb=Wb: t.matmul(
                                 out=bank[:],
                                 lhsT=xgT[:, k * NT4 * 128 + T * 128:
                                          k * NT4 * 128 + (T + 1) * 128],
                                 rhs=Wb[:, k * D + nb * 512:
                                        k * D + nb * 512 + 512],
                                 start=(k == 0), stop=(k == 7)),
                             mark="pe" if k == 7 else None)
                p.wt("v", "pe"); p.wt("v", "s")
                p.on("v", lambda v: v.tensor_copy(out=qbuf[:, 0:512],
                                                  in_=psA[:]))
                p.on("v", lambda v: v.tensor_copy(out=qbuf[:, 512:1024],
                                                  in_=psB[:]), mark="v")
                set_rd(psA, "v"); set_rd(psB, "v")
                rope4(T, dstT, 1, H4)
            for src, dstT, tq in ((Wk1b, k1T, 0), (Wk2b, k2T, 0),
                                  (Wvb, vb4, -1)):
                psum_guard(psO)
                p.wt("t", "s"); p.wt("t", "v")
                for k in range(8):
                    p.on("t", lambda t, k=k, T=T, src=src: t.matmul(
                        out=psO[:, 0:128],
                        lhsT=xgT[:, k * NT4 * 128 + T * 128:
                                 k * NT4 * 128 + (T + 1) * 128],
                        rhs=src[:, k * 128:(k + 1) * 128],
                        start=(k == 0), stop=(k == 7)),
                        mark="pe" if k == 7 else None)
                p.wt("v", "pe"); p.wt("v", "s")
                if tq == -1:
                    p.on("v", lambda v, T=T: v.tensor_copy(
                        out=vb4[:, T * 128:(T + 1) * 128],
                        in_=psO[:, 0:128]), mark="v")
                    set_rd(psO, "v")
                else:
                    p.on("v", lambda v: v.tensor_copy(
                        out=qbuf[:, 0:128], in_=psO[:, 0:128]), mark="v")
                    set_rd(psO, "v")
                    rope4(T, dstT, tq, G4)

        for wl in range(8):
            for qt in range(2):
                Ti = wl + qt
                Tq = 2 * wl + qt
                for h in range(H4):
                    g = h % G4
                    for br, (qTx, kTx) in enumerate(((q1T, k1T),
                                                     (q2T, k2T))):
                        pt = h // 2
                        pb = (h % 2) * 64
                        psum_guard(psS)
                        p.wt("t", "s"); p.wt("t", "v")
                        p.on("t", lambda t, Ti=Ti, wl=wl, pt=pt, pb=pb,
                             qTx=qTx, kTx=kTx, g=g: t.matmul(
                                 out=psS[:, 0:256],
                                 lhsT=qTx[pb:pb + 64,
                                          pt * NT4 * 128 + Ti * 128:
                                          pt * NT4 * 128 + (Ti + 1) * 128],
                                 rhs=kTx[64 * g:64 * g + 64,
                                         wl * 128:wl * 128 + 256],
                                 start=True, stop=True), mark="pe")
                        p.wt("v", "pe"); p.wt("v", "s")
                        p.on("v", lambda v, br=br: v.reduce_max(
                            out=acol[:, br:br + 1], in_=psS[:, 0:256],
                            axis=AX, negate=True), mark="v")
                        p.wt("s", "v")
                        p.on("s", lambda s, br=br: s.activation(
                            out=P4[:], in_=psS[:, 0:256], func=AF.Exp,
                            bias=acol[:, br:br + 1],
                            accum_out=acol[:, 2 + br:3 + br]), mark="s")
                        set_rd(psS, "s")
                        p.wt("v", "s")
                        p.on("v", lambda v, br=br: v.reciprocal(
                            out=acol[:, 4 + br:5 + br],
                            in_=acol[:, 2 + br:3 + br]), mark="v")
                        p.wt("t", "s")
                        psum_guard(psT)
                        for j in range(2):
                            p.on("t", lambda t, j=j: t.transpose(
                                out=psT[:, j * 128:(j + 1) * 128],
                                in_=P4[:, j * 128:(j + 1) * 128],
                                identity=ident[:]),
                                mark="pe" if j == 1 else None)
                        p.wt("s", "pe")
                        p.on("s", lambda s: s.copy(out=PT4[:],
                                                   in_=psT[:, 0:256]),
                             mark="s")
                        set_rd(psT, "s")
                        psum_guard(psO)
                        p.wt("t", "s")
                        for j in range(2):
                            p.on("t", lambda t, j=j, wl=wl, g=g: t.matmul(
                                out=psO[:, 0:64],
                                lhsT=PT4[:, j * 128:(j + 1) * 128],
                                rhs=vb4[:, (wl + j) * 128 + 64 * g:
                                        (wl + j) * 128 + 64 * g + 64],
                                start=(j == 0), stop=(j == 1)),
                                mark="pe" if j == 1 else None)
                        p.wt("v", "pe"); p.wt("v", "v")
                        dst = o1s if br == 0 else o2s
                        p.on("v", lambda v, br=br, dst=dst: v.tensor_scalar(
                            out=dst[:], in0=psO[:, 0:64],
                            scalar1=acol[:, 4 + br:5 + br], scalar2=None,
                            op0=ALU.mult), mark="v")
                        set_rd(psO, "v")
                    p.wt("v", "v")
                    p.on("v", lambda v, h=h: v.scalar_tensor_tensor(
                        out=o_acc[:, h * DH:(h + 1) * DH], in0=o2s[:],
                        scalar=nlam_col[:, 0:1], in1=o1s[:],
                        op0=ALU.mult, op1=ALU.add), mark="v")
                p.on("v", lambda v: v.tensor_copy(out=o_bf[:], in_=o_acc[:]),
                     mark="v")
                p.wt("t", "v")
                psum_guard(psT)
                for j in range(8):
                    p.on("t", lambda t, j=j: t.transpose(
                        out=psT[:, j * 128:(j + 1) * 128],
                        in_=o_bf[:, j * 128:(j + 1) * 128],
                        identity=ident[:]), mark="pe" if j == 7 else None)
                p.wt("s", "pe")
                for j in range(8):
                    p.on("s", lambda s, j=j: s.copy(
                        out=oT4[:, j * 128:(j + 1) * 128],
                        in_=psT[:, j * 128:(j + 1) * 128]), mark="s")
                set_rd(psT, "s")
                p.wt("y", "v"); p.wt("y", "s")
                p.wt("y", "ds")
                p.dma("y", lambda y, Tq=Tq: y.dma_start(
                    out=xr_t[:], in_=xr3[Tq * 128:(Tq + 1) * 128, :]))
                for nb, bank in ((0, psA), (1, psB)):
                    psum_guard(bank)
                    p.wt("t", "s")
                    p.on("t", lambda t, nb=nb, bank=bank: t.matmul(
                        out=bank[:], lhsT=ones_bf[:, :],
                        rhs=bout_bf[0:1, nb * 512:nb * 512 + 512],
                        start=True, stop=False))
                    for j in range(8):
                        p.on("t", lambda t, j=j, nb=nb, bank=bank: t.matmul(
                            out=bank[:],
                            lhsT=oT4[:, j * 128:(j + 1) * 128],
                            rhs=Woutb[:, j * D + nb * 512:
                                      j * D + nb * 512 + 512],
                            start=False, stop=(j == 7)),
                            mark="pe" if j == 7 else None)
                p.wt("v", "pe"); p.wt("v", "ds"); p.wt("v", "s")
                for nb, bank in ((0, psA), (1, psB)):
                    p.on("v", lambda v, nb=nb, bank=bank: v.tensor_tensor(
                        out=xo_t[:, nb * 512:nb * 512 + 512], in0=bank[:],
                        in1=xr_t[:, nb * 512:nb * 512 + 512], op=ALU.add),
                        mark="v")
                    set_rd(bank, "v")
                ln_stats(p, xo_t[:], lnsc[:], cols, D)
                p.on("v", lambda v: v.tensor_scalar(
                    out=xr_t[:], in0=xo_t[:], scalar1=cols["rstd"],
                    scalar2=cols["nmr"], op0=ALU.mult, op1=ALU.add),
                    mark="v")
                vpad(); p.wt("y", "v")
                p.dma("y", lambda y, Tq=Tq: y.dma_start(
                    out=x4[Tq * 128:(Tq + 1) * 128, :], in_=xr_t[:]))

        p.on("v", lambda v: v.memset(lncol[:, 9:10], 0.0), mark="v")
        p.on("v", lambda v: v.memset(lncol[:, 9:10], 0.0), mark="v")
        p.on("s", lambda s_: s_.activation(out=lncol[:, 10:11],
                                           in_=lncol[:, 9:10],
                                           func=AF.Square), mark="s")
        p.on("s", lambda s_: s_.activation(out=lncol[:, 11:12],
                                           in_=lncol[:, 9:10],
                                           func=AF.Square), mark="s")
        if ctx is None:
            p.wt("y", "v"); p.wt("y", "s")
            p.wt("y", "ds")
            p.wt("y", "dg")
            p.replay(block)
            for g in p.guards:
                g.__exit__(None, None, None)
    for g in reversed(_stage_guards):
        g.__exit__(None, None, None)
    return nc


def _stage4_inputs(x3_full, w4):
    f32 = np.float32
    r8 = lambda w, n: np.ascontiguousarray(
        np.asarray(w, f32).reshape(8, 128, n).transpose(1, 0, 2)
        .reshape(128, 8 * n))
    common = dict(
        Wq1r=r8(w4["Wq1"], D), Wq2r=r8(w4["Wq2"], D),
        Wk1r=r8(w4["Wk1"], 128), Wk2r=r8(w4["Wk2"], 128),
        Wvr=r8(w4["Wv"], 128), Woutr=r8(w4["Wout"], D),
        boutrow=np.asarray(w4["bout"], f32).reshape(1, D),
        lqk=np.ascontiguousarray(np.stack(
            [np.asarray(w4[k], f32) for k in ("lq1", "lk1", "lq2", "lk2")],
            axis=1)),
        ident=np.eye(128).astype(BF),
        ones1=np.ones((1, 128), f32))
    div = np.exp(np.arange(0, DH, 2, dtype=np.float64) *
                 (-np.log(10000.0) / DH))
    maps = []
    for c in range(NCORE):
        lo = c * 1024
        pos = np.arange(lo, lo + NT4 * 128, dtype=np.float64)
        sin = np.sin(pos[:, None] * div)
        cos = np.cos(pos[:, None] * div)
        t16 = lambda a: np.ascontiguousarray(
            np.tile(a, (1, H4)).astype(f32))
        m = dict(common)
        m.update(xg=np.ascontiguousarray(x3_full[lo:lo + NT4 * 128]),
                 xr3=np.ascontiguousarray(x3_full[c * S:(c + 1) * S]),
                 t4c=t16(cos), t4s=t16(sin),
                 t4cq=t16(cos * 0.125), t4sq=t16(sin * 0.125))
        maps.append({k: np.ascontiguousarray(v) if k != "ident" else v
                     for k, v in m.items()})
    return maps


def run_stage4(x3_full, w4):
    nc = build_stage4()
    maps = _stage4_inputs(x3_full, w4)
    res = run_bass_kernel_spmd(nc, maps, core_ids=list(range(NCORE)))
    return np.concatenate([np.asarray(r["x4"]) for r in res.results], axis=0)

S1_KEYS = [("xs", [S, D]), ("xh", [128, D]), ("hsc", [128, 1]),
           ("lnw", [1, D]), ("lnb", [1, D]), ("Win", [128, 8 * 2 * DI]),
           ("convw", [128, NM * 4]), ("convb", [128, NM]),
           ("Wxp", [128, NM * 96]), ("Wdt", [DTR, DI]), ("bdt", [128, NM]),
           ("Alog", [128, NM * NST]), ("Dvec", [128, NM]),
           ("Wout", [128, NM * D])]
LL_KEYS = [("wqkv_r", [128, 8 * 3 * INNER]), ("qs_row", [1, INNER]),
           ("ks_row", [1, INNER]), ("wo_r", [128, 4 * D]),
           ("lng", [1, D]), ("lnb2", [1, D]),
           ("w1a_r", [MC, 128, 8 * 128]), ("w1g_r", [MC, 128, 8 * 128]),
           ("b1a_c", [128, MC]), ("b1g_c", [128, MC]),
           ("w2_r", [128, MC * D]), ("b2row", [1, D])]
LL_SHARED = [("tq_cs", [EXTT, INNER]), ("tq_ss", [EXTT, INNER]),
             ("tk_cs", [EXTT, INNER]), ("tk_ss", [EXTT, INNER]),
             ("msk", [5, 128, 640]), ("hsc2", [1, 1]), ("hscH", [128, 1])]
S4_KEYS = [("Wq1r", [128, 8 * D]), ("Wq2r", [128, 8 * D]),
           ("Wk1r", [128, 8 * 128]), ("Wk2r", [128, 8 * 128]),
           ("Wvr", [128, 8 * 128]), ("Woutr", [128, 8 * D]),
           ("boutrow", [1, D]), ("lqk", [DTR, 4]),
           ("t4c", [NT4 * 128, 512]), ("t4s", [NT4 * 128, 512]),
           ("t4cq", [NT4 * 128, 512]), ("t4sq", [NT4 * 128, 512])]


def build_fused():
    nc = bass.Bass("TRN2", target_bir_lowering=False, debug=False,
                   num_devices=NCORE)
    din = lambda nm, sh, dt=F32: nc.dram_tensor(nm, sh, dt,
                                                kind="ExternalInput").ap()
    ident_ap = nc.dram_tensor("ident", [128, 128], BF16,
                              kind="ExternalInput").ap()
    ones_ap = din("ones1", [1, 128])
    io_s1 = {k: din("s1_" + k, sh) for k, sh in S1_KEYS}
    io_s1.update(ident=ident_ap, ones1=ones_ap)
    shared_ll = {k: din(k, sh) for k, sh in LL_SHARED}
    io_ll = {}
    for pfx in ("l0a_", "l0b_", "l1a_", "l1b_"):
        d = {k: din(pfx + k, sh) for k, sh in LL_KEYS}
        d.update(shared_ll)
        d.update(ident=ident_ap, ones1=ones_ap)
        io_ll[pfx] = d
    io_s4 = {k: din("s4_" + k, sh) for k, sh in S4_KEYS}
    io_s4.update(ident=ident_ap, ones1=ones_ap)
    out = nc.dram_tensor("out", [S, D], F32, kind="ExternalOutput").ap()

    x1_d = nc.dram_tensor("x1_d", [S, D], F32)
    t1_d = nc.dram_tensor("t1_d", [S, D], F32)
    x2_d = nc.dram_tensor("x2_d", [S, D], F32)
    x3_d = nc.dram_tensor("x3_d", [S, D], F32)
    t2_d = nc.dram_tensor("t2_d", [S, D], F32)
    hals = []
    for i in range(4):
        hin = nc.dram_tensor(f"hal{i}_in", [1280, D], BF16)
        hout = nc.dram_tensor(f"hal{i}_out", [NCORE * 1280, D], BF16,
                              addr_space="Shared")
        hals.append((hin, hout))
    x3g_in = nc.dram_tensor("x3g_in", [S, D], BF16)
    x3g_out = nc.dram_tensor("x3g_out", [NCORE * S, D], BF16,
                             addr_space="Shared")

    psA = nc.alloc_psum_tensor("psA", [128, 512], F32)
    psB = nc.alloc_psum_tensor("psB", [128, 512], F32)
    psC = nc.alloc_psum_tensor("psC", [128, 512], F32)
    psO = nc.alloc_psum_tensor("psO", [128, 512], F32)
    psT = nc.alloc_psum_tensor("psT", [128, 1024], BF16)
    ps_sc = nc.alloc_psum_tensor("ps_sc", [128, 640], F32)

    with nc.Block() as block:
        p = P(nc)
        last_rd = {id(b): ("v", 0) for b in (psA, psB, psC, psO, psT, ps_sc)}
        base = dict(nc=nc, p=p, last_rd=last_rd, psA=psA, psB=psB, psC=psC,
                    psO=psO, ps_o=psO, psT=psT, ps_sc=ps_sc, regcache={})

        def join():
            for e in ("t", "v", "s", "g", "y"):
                for sem in ("pe", "v", "s", "ds", "dg", "c"):
                    p.wt(e, sem)

        def halo_ex(src_d, pair):
            hin, hout = pair
            p.wt("g", "ds"); p.wt("g", "v"); p.wt("g", "s")
            p.dma("g", lambda g: g.dma_start(out=hin[:, :],
                                             in_=src_d[S - 1280:S, :]))
            p.wt("g", "dg")
            p.on("g", lambda g: g.collective_compute(
                "AllGather", ALU.bypass, replica_groups=[list(range(NCORE))],
                ins=[hin[:, :]], outs=[hout[:, :]]), mark="c")
            p.wt("g", "c"); p.wt("y", "c")

        ctx_s1 = dict(base, tag="s1_", io=io_s1, x_out=x1_d)
        build_stage1(ctx=ctx_s1)
        join()
        if FSTAGES == 1:
            tap = nc.dram_tensor("tap_x1", [S, D], F32,
                                 kind="ExternalOutput").ap()
            p.dma("y", lambda y, o=tap: y.dma_start(out=o[:, :],
                                                    in_=x1_d[:, :]))
            tap2 = nc.dram_tensor("tap_xr", [S, D], F32,
                                  kind="ExternalOutput").ap()
            p.dma("y", lambda y, o=tap2: y.dma_start(
                out=o[:, :], in_=ctx_s1["xr_d"][:, :]))
            p.wt("y", "ds")
            p.replay(block)
            for g in p.guards:
                g.__exit__(None, None, None)
            return nc
        halo_ex(x1_d, hals[0])
        build_local_layer(False, ctx=dict(base, tag="l0a_", io=io_ll["l0a_"],
                                          x_in=x1_d, hal=hals[0][1],
                                          x_out=t1_d))
        join()
        halo_ex(t1_d, hals[1])
        build_local_layer(True, ctx=dict(base, tag="l0b_", io=io_ll["l0b_"],
                                         x_in=t1_d, hal=hals[1][1],
                                         xres=x1_d, x_out=x2_d))
        join()
        p.wt("g", "ds"); p.wt("g", "v"); p.wt("g", "s")
        p.dma("g", lambda g: g.dma_start(out=x3g_in[:, :], in_=x2_d[:, :]))
        p.wt("g", "dg")
        p.on("g", lambda g: g.collective_compute(
            "AllGather", ALU.bypass, replica_groups=[list(range(NCORE))],
            ins=[x3g_in[:, :]], outs=[x3g_out[:, :]]), mark="c")
        p.wt("g", "c"); p.wt("y", "c")
        build_stage4(ctx=dict(base, tag="s4_", io=io_s4, x_in=x2_d,
                              x3g=x3g_out, x_out=x3_d))
        join()
        halo_ex(x3_d, hals[2])
        build_local_layer(False, ctx=dict(base, tag="l1a_", io=io_ll["l1a_"],
                                          x_in=x3_d, hal=hals[2][1],
                                          x_out=t2_d))
        join()
        halo_ex(t2_d, hals[3])
        build_local_layer(True, ctx=dict(base, tag="l1b_", io=io_ll["l1b_"],
                                         x_in=t2_d, hal=hals[3][1],
                                         xres=x3_d, x_out=out))
        join()
        if FDBG:
            for nm, src in (("x1", x1_d), ("t1", t1_d), ("x2", x2_d),
                            ("x3", x3_d), ("t2", t2_d)):
                tap = nc.dram_tensor("tap_" + nm, [S, D], F32,
                                     kind="ExternalOutput").ap()
                p.dma("y", lambda y, o=tap, s_=src: y.dma_start(
                    out=o[:, :], in_=s_[:, :]))
            p.wt("y", "ds")
        p.replay(block)
        for g in p.guards:
            g.__exit__(None, None, None)
    return nc


def _fused_inputs(inputs):
    x = np.ascontiguousarray(np.asarray(inputs["L0_attn_x"], np.float32)[0])
    w1m = {k[8:]: inputs[k] for k in inputs
           if k.startswith("L0_attn_") and k != "L0_attn_x"}
    l0f = {k[7:]: np.asarray(inputs[k]) for k in inputs
           if k.startswith("L0_ffn_")}
    w4 = {k[8:]: np.asarray(inputs[k]) for k in inputs
          if k.startswith("L1_attn_")}
    l1f = {k[7:]: np.asarray(inputs[k]) for k in inputs
           if k.startswith("L1_ffn_")}
    lsel = lambda lf, i: {k: v[i] for k, v in lf.items()}
    s1 = _stage1_inputs(x, w1m)
    zeros = np.zeros((T_FULL, D), np.float32)
    lls = {pfx: _layer_inputs(zeros, lw, False)
           for pfx, lw in (("l0a_", lsel(l0f, 0)), ("l0b_", lsel(l0f, 1)),
                           ("l1a_", lsel(l1f, 0)), ("l1b_", lsel(l1f, 1)))}
    s4 = _stage4_inputs(zeros, w4)
    shared_keys = {k for k, _ in LL_SHARED}
    maps = []
    for c in range(NCORE):
        m = {}
        for k, v in s1[c].items():
            if k in ("ident", "ones1"):
                m[k] = v
            else:
                m["s1_" + k] = v
        for pfx in ("l0a_", "l0b_", "l1a_", "l1b_"):
            for k, v in lls[pfx][c].items():
                if k in ("ident", "ones1", "xw", "xres"):
                    continue
                if k in shared_keys:
                    m[k] = v
                else:
                    m[pfx + k] = v
        m["hscH"] = np.full((128, 1), 0.0 if c == 0 else 1.0, np.float32)
        for k, v in s4[c].items():
            if k in ("ident", "ones1", "xg", "xr3"):
                continue
            m["s4_" + k] = v
        maps.append(m)
    return maps


_FUSED_NC = []


def kernel_fused(**inputs):
    if not _FUSED_NC:
        _FUSED_NC.append(build_fused())
    nc = _FUSED_NC[0]
    maps = _fused_inputs(inputs)
    res = run_bass_kernel_spmd(nc, maps, core_ids=list(range(NCORE)))
    outp = np.concatenate([np.asarray(r["out"]) for r in res.results], axis=0)
    return outp.reshape(1, T_FULL, D)
